# version 13
# baseline (speedup 1.0000x reference)
import numpy as np
import ml_dtypes
from contextlib import ExitStack
import concourse.bass as bass
import concourse.mybir as mybir
from concourse.bass_utils import run_bass_kernel_spmd

F32 = mybir.dt.float32
BF16 = mybir.dt.bfloat16
AF = mybir.ActivationFunctionType
ALU = mybir.AluOpType
AX = mybir.AxisListType
BF16NP = ml_dtypes.bfloat16

NCORES = 8
B, S, H = 16, 2048, 1024
BPC = B // NCORES
TC = BPC * S
NHCH = H // 128
TSTRIP = 512
NSTRIP = TC // TSTRIP

E0, I0 = 8, 2816
NI0 = I0 // 128
WIN = 128
NH1, DH1 = 16, 64
E1 = 8
RH = 512

ENGINES = ("tensor", "vector", "scalar", "gpsimd", "sync")


class Sem:
    __slots__ = ("name", "h", "n")

    def __init__(self, name, h):
        self.name = name
        self.h = h
        self.n = 0

    def mark(self):
        return self.n


class Eng:

    def __init__(self, prog, name):
        self.P = prog
        self.name = name
        self.ops = []
        self.dsem = None

    def _rec(self, fn, incs=None):
        incs = incs or []
        for sem, k in incs:
            sem.n += k

        def run(e, st):
            inst = fn(e)
            for sem, k in incs:
                inst.then_inc(sem.h, k)
            st["last"][self.name] = inst
        self.ops.append(run)
        return self

    def wait(self, sem, val=None):
        v = sem.n if val is None else val
        self.ops.append(lambda e, st: e.wait_ge(sem.h, v))
        return self

    def flush(self, inc=None):
        if self.dsem is None:
            self.dsem = self.P.sem(f"_d_{self.name}")
        self.dsem.n += 1
        v = self.dsem.n
        sem = self.dsem
        name = self.name

        def run(e, st):
            li = st["last"].get(name)
            assert li is not None, f"flush with no prior inst on {name}"
            li.then_inc(sem.h, 1)
            e.wait_ge(sem.h, v)
        self.ops.append(run)
        return self

    def op(self, method, *args, inc=None, **kwargs):
        import traceback
        stk = traceback.extract_stack()
        site = next((f for f in reversed(stk[:-1])
                     if 'kernel.py' in f.filename and f.lineno > 300), stk[-2])
        def fn(e):
            try:
                return getattr(e, method)(*args, **kwargs)
            except Exception as ex:
                import traceback as tb
                tb.print_exc()
                raise RuntimeError(
                    f"op {method} on {self.name} failed (recorded at "
                    f"{site.filename}:{site.lineno} {site.line}): {ex}") from ex
        return self._rec(fn, incs=inc)

    def mark_last(self, sem, k=1):
        sem.n += k
        name = self.name

        def run(e, st):
            st["last"][name].then_inc(sem.h, k)
        self.ops.append(run)
        return self

    def post_inc(self, sem, k=1):
        sem.n += k

        def run(e, st):
            e.sem_inc(sem.h, k)
        self.ops.append(run)
        return self

    def dma(self, out, in_, inc=None, **kw):
        return self.op("dma_start", out=out, in_=in_, inc=inc, **kw)

    def mm(self, out, lhsT, rhs, start=True, stop=True, inc=None, **kw):
        return self.op("matmul", out, lhsT, rhs, start=start, stop=stop, inc=inc, **kw)


class Prog:
    def __init__(self, nc, es):
        self.nc = nc
        self.es = es
        self._scopes = [es]
        self.e = {n: Eng(self, n) for n in ENGINES}
        self._sems = {}
        self._free_sems = []
        self._scope_sems = [[]]
        self._uid = 0

    def __getattr__(self, name):
        if name in ENGINES:
            return self.e[name]
        raise AttributeError(name)

    def sem(self, name):
        if name not in self._sems:
            glob = name.startswith("_")
            if not glob and self._free_sems and globals().get("SEM_RECYCLE", True):
                so = self._free_sems.pop()
                so.name = name
            else:
                self._uid += 1
                h = self.es.enter_context(self.nc.semaphore(f"{name}_{self._uid}"))
                so = Sem(name, h)
            self._sems[name] = so
            if not glob:
                self._scope_sems[-1].append(name)
        return self._sems[name]

    def sbuf(self, name, shape, dtype):
        self._uid += 1
        return self._scopes[-1].enter_context(
            self.nc.sbuf_tensor(f"{name}_{self._uid}", shape, dtype))

    def psum(self, name, shape, dtype=F32):
        self._uid += 1
        return self._scopes[-1].enter_context(
            self.nc.psum_tensor(f"{name}_{self._uid}", shape, dtype))

    from contextlib import contextmanager as _ctxm

    @_ctxm
    def scope(self):
        es = ExitStack()
        self._scopes.append(es)
        self._scope_sems.append([])
        try:
            with es:
                yield
        finally:
            self._scopes.pop()
            for name in self._scope_sems.pop():
                so = self._sems.pop(name, None)
                if so is not None:
                    self._free_sems.append(so)

    def barrier(self):
        bsem = self.sem("_barrier")
        bsem.n += len(ENGINES)
        v = bsem.n
        for name in ENGINES:
            def run(e, st, h=bsem.h, v=v):
                e.sem_inc(h, 1)
                e.wait_ge(h, v)
            self.e[name].ops.append(run)

    def replay(self):
        st = {"last": {}}
        with self.nc.Block() as block:
            for name in ENGINES:
                def make(ops):
                    def runner(eng):
                        for f in ops:
                            f(eng, st)
                    return runner
                getattr(block, name)(make(self.e[name].ops))



def _kmajor(w):
    return np.ascontiguousarray(np.asarray(w).T).astype(np.float32, copy=False)


def prep_weights(inp):
    w = {}
    w["l0a_wqk"] = np.ascontiguousarray(np.asarray(inp["L0_attn_w_qkv"])[:, 0:2 * H])
    w["l0a_wo"] = np.asarray(inp["L0_attn_w_out"])
    w["l0a_bo"] = np.ascontiguousarray(np.asarray(inp["L0_attn_b_out"]).reshape(NHCH, 128).T)
    w["l0f_wr"] = _kmajor(inp["L0_ffn_w_router"])
    w["l0f_wg"] = np.ascontiguousarray(np.asarray(inp["L0_ffn_w_gate"]).transpose(0, 2, 1))
    w["l0f_wu"] = np.ascontiguousarray(np.asarray(inp["L0_ffn_w_up"]).transpose(0, 2, 1))
    w["l0f_wd"] = np.ascontiguousarray(np.asarray(inp["L0_ffn_w_down"]).transpose(0, 2, 1))
    qkv = np.asarray(inp["L1_attn_qkv_w"])
    w["l1a_wqkv"] = np.ascontiguousarray(np.concatenate(
        [qkv[0:H].T, qkv[H:2 * H].T, qkv[2 * H:3 * H].T], axis=1)).astype(np.float32)
    w["l1a_bqkv"] = np.ascontiguousarray(np.asarray(inp["L1_attn_qkv_b"]).reshape(3 * NHCH, 128).T)
    w["l1a_wo"] = _kmajor(inp["L1_attn_out_w"])
    w["l1a_bo"] = np.ascontiguousarray(np.asarray(inp["L1_attn_out_b"]).reshape(NHCH, 128).T)
    w["l1f_rW1"] = _kmajor(inp["L1_ffn_rW1"])
    w["l1f_rb1"] = np.ascontiguousarray(np.asarray(inp["L1_ffn_rb1"]).reshape(RH // 128, 128).T)
    w["l1f_lng"] = np.ascontiguousarray(np.asarray(inp["L1_ffn_ln_g"]).reshape(RH // 128, 128).T)
    w["l1f_lnb"] = np.ascontiguousarray(np.asarray(inp["L1_ffn_ln_b"]).reshape(RH // 128, 128).T)
    w["l1f_rW2"] = _kmajor(inp["L1_ffn_rW2"])
    w["l1f_rb2"] = np.asarray(inp["L1_ffn_rb2"]).reshape(1, 8)
    for nm, wkey, bkey in (("q", "qW", "qb"), ("k", "kW", "kb"),
                           ("v", "vW", "vb"), ("o", "oW", "ob")):
        w[f"l1f_w{nm}"] = np.ascontiguousarray(
            np.asarray(inp[f"L1_ffn_{wkey}"]).transpose(0, 2, 1))
        w[f"l1f_b{nm}"] = np.ascontiguousarray(np.asarray(inp[f"L1_ffn_{bkey}"]).reshape(E1, NHCH, 128).transpose(0, 2, 1))
    w["ident"] = np.eye(128, dtype=BF16NP)
    w["identf"] = np.eye(128, dtype=np.float32)
    w["ones1"] = np.ones((1, 128), dtype=BF16NP)
    i = np.arange(WIN)[:, None]
    j = np.arange(2 * WIN)[None, :]
    w["l0a_mask"] = np.where(j > i + WIN, -3.0e38, 0.0).astype(np.float32)
    bi = np.arange(128) // 8
    si = np.arange(128) % 8
    valid = (si[:, None] == si[None, :]) & (bi[None, :] <= bi[:, None])
    w["l1a_mask"] = np.where(valid, 0.0, -10000.0).astype(np.float32)
    return w


def prep_x_per_core(x):
    x = np.asarray(x)
    out = []
    for c in range(NCORES):
        xc = np.ascontiguousarray(
            x[c * BPC:(c + 1) * BPC].transpose(2, 0, 1).reshape(H, TC))
        out.append(xc.astype(np.float32, copy=False))
    return out



def emit_load_consts(P, io, bufs):
    g = P.gpsimd
    ld = P.sem("cld")
    bufs["ident"] = P.sbuf("c_ident", [128, 128], BF16)
    bufs["ones1"] = P.sbuf("c_ones1", [1, 128], BF16)
    g.dma(bufs["ident"][:], io["ident"][:], inc=[(ld, 16)])
    g.dma(bufs["ones1"][:], io["ones1"][:], inc=[(ld, 16)])
    for name in ENGINES:
        P.e[name].wait(ld)






def make_consts(P, bufs):
    v = P.e["vector"]
    bufs["ones1f"] = P.sbuf("c_ones1f", [1, 128], F32)
    bufs["onescf"] = P.sbuf("c_onescf", [128, 1], F32)
    bufs["ones1b"] = P.sbuf("c_ones1b", [1, 128], BF16)
    v.op("memset", bufs["ones1f"][:], 1.0)
    v.op("memset", bufs["onescf"][:], 1.0)
    v.op("memset", bufs["ones1b"][:], 1.0)
    v.flush()


def make_pool(P, bufs):
    bufs["pool"] = [P.psum(f"pool{i}", [128, TSTRIP]) for i in range(6)]
    bufs["poolr"] = [P.psum(f"poolr{i}", [1, TSTRIP]) for i in range(2)]


def emit_ln(P, io, bufs, src_x, out_ln, acc=None, dst_x=None, acc_scale=1.0,
            out_lnf=None):
    t, v, s, g, sy = (P.e[n] for n in ENGINES)
    lds = [P.sem("ln_ld0"), P.sem("ln_ld1")]
    stws = [P.sem("ln_stw0"), P.sem("ln_stw1")]
    vx = P.sem("ln_vx")
    sq_s = P.sem("ln_sq")
    tsq = P.sem("ln_tsq")
    tst = P.sem("ln_tst")
    vrow = P.sem("ln_vrow")
    vst = P.sem("ln_vst")
    sst = P.sem("ln_sst")
    tbc = P.sem("ln_tbc")
    vout = P.sem("ln_vout")

    CW = NHCH * TSTRIP
    xt = bufs.setdefault("ln_xt", P.sbuf("ln_xt", [128, 2 * CW], F32))
    at = bufs.setdefault("ln_at", P.sbuf("ln_at", [128, 2 * CW], F32))
    sqb = bufs.setdefault("ln_sqb", P.sbuf("ln_sqb", [128, 2 * TSTRIP], F32))
    tmpb = bufs.setdefault("ln_tmpb", P.sbuf("ln_tmpb", [128, TSTRIP], F32))
    tmp2b = bufs.setdefault("ln_tmp2b", P.sbuf("ln_tmp2b", [128, 2 * NHCH * TSTRIP], F32)) \
        if out_lnf is not None else None
    strow = bufs.setdefault("ln_strow", P.sbuf("ln_strow", [1, 4 * TSTRIP], F32))
    ot = bufs.setdefault("ln_ot", P.sbuf("ln_ot", [128, 2 * CW], BF16))
    psum_r, pssq_r = bufs["poolr"][0], bufs["poolr"][1]
    psA, psB = bufs["pool"][0], bufs["pool"][1]
    onesf = bufs["ones1f"]
    onescf = bufs["onescf"]

    vout_marks = {}
    stw_marks = {}
    for it in range(NSTRIP):
        par = it % 2
        xs = xt[:, par * CW:(par + 1) * CW]
        as_ = at[:, par * CW:(par + 1) * CW]
        os_ = ot[:, par * CW:(par + 1) * CW]
        tsl = slice(it * TSTRIP, (it + 1) * TSTRIP)
        r_mu = strow[0:1, 0 * TSTRIP:1 * TSTRIP]
        r_v1 = strow[0:1, 1 * TSTRIP:2 * TSTRIP]
        r_v2 = strow[0:1, 2 * TSTRIP:3 * TSTRIP]

        if it >= 2:
            sy.wait(vout, vout_marks[it - 2])
            sy.wait(stws[par])
        for h in range(NHCH):
            sy.dma(xs[:, h * TSTRIP:(h + 1) * TSTRIP],
                   src_x[h * 128:(h + 1) * 128, tsl], inc=[(lds[par], 16)])
        if acc is not None:
            for h in range(NHCH):
                sy.dma(as_[:, h * TSTRIP:(h + 1) * TSTRIP],
                       acc[h * 128:(h + 1) * 128, tsl], inc=[(lds[par], 16)])
            v.wait(lds[par])
            for h in range(NHCH):
                sl = slice(h * TSTRIP, (h + 1) * TSTRIP)
                if acc_scale == 1.0:
                    v.op("tensor_tensor", xs[:, sl], xs[:, sl], as_[:, sl], ALU.add)
                else:
                    v.op("scalar_tensor_tensor", xs[:, sl], as_[:, sl], acc_scale,
                         xs[:, sl], ALU.mult, ALU.add)
            v.flush()
            v.post_inc(vx)
            if dst_x is not None:
                sy.wait(vx)
                for h in range(NHCH):
                    sl = slice(h * TSTRIP, (h + 1) * TSTRIP)
                    sy.dma(dst_x[h * 128:(h + 1) * 128, tsl], xs[:, sl],
                           inc=[(stws[par], 16)])
            t.wait(vx)
            s.wait(vx)
        else:
            t.wait(lds[par])
            s.wait(lds[par])
        if it >= 1:
            t.wait(vrow)
        for h in range(NHCH):
            sl = slice(h * TSTRIP, (h + 1) * TSTRIP)
            t.mm(psum_r[:], onescf[:, 0:1], xs[:, sl],
                 start=(h == 0), stop=(h == NHCH - 1))
        tsq_base = tsq.mark()
        for h in range(NHCH):
            sl = slice(h * TSTRIP, (h + 1) * TSTRIP)
            sqs = sqb[:, (h % 2) * TSTRIP:(h % 2 + 1) * TSTRIP]
            if h >= 2:
                s.wait(tsq, tsq_base + h - 1)
            s.op("activation", sqs, xs[:, sl], AF.Square, inc=[(sq_s, 1)])
            t.wait(sq_s)
            t.mm(pssq_r[:], onescf[:, 0:1], sqs,
                 start=(h == 0), stop=(h == NHCH - 1), inc=[(tsq, 1)])
        v.wait(tsq)
        v.op("tensor_scalar", r_mu, psum_r[:], 1.0 / H, None, ALU.mult)
        v.op("tensor_scalar", r_v1, pssq_r[:], 1.0 / H, None, ALU.mult)
        v.flush()
        v.post_inc(vrow)
        v.op("tensor_tensor", r_v2, r_mu, r_mu, ALU.mult)
        v.flush()
        v.op("tensor_tensor", r_v1, r_v1, r_v2, ALU.subtract)
        v.flush()
        v.op("tensor_scalar", r_v1, r_v1, 1e-5, None, ALU.add, inc=[(vst, 1)])
        s.wait(vst)
        s.op("activation", r_v2, r_v1, AF.Sqrt, inc=[(sst, 1)])
        v.wait(sst)
        v.op("reciprocal", r_v1, r_v2)
        v.flush()
        v.op("tensor_tensor", r_v2, r_mu, r_v1, ALU.mult)
        v.flush()
        v.op("tensor_scalar", r_v2, r_v2, -1.0, None, ALU.mult)
        v.flush()
        v.post_inc(vst)
        t.wait(vst)
        if it >= 2:
            t.wait(vout, vout_marks[it - 2])
        t.mm(psA[:], onesf[0:1, :], r_v1, start=True, stop=True)
        t.mm(psB[:], onesf[0:1, :], r_v2, start=True, stop=True)
        t.mark_last(tbc)
        v.wait(tbc)
        for h in range(NHCH):
            sl = slice(h * TSTRIP, (h + 1) * TSTRIP)
            v.op("tensor_tensor", tmpb[:], xs[:, sl], psA[:], ALU.mult)
            v.flush()
            if out_lnf is None:
                v.op("tensor_tensor", os_[:, sl], tmpb[:], psB[:], ALU.add)
                v.flush()
            else:
                f_ = tmp2b[:, par * NHCH * TSTRIP:(par + 1) * NHCH * TSTRIP]
                v.op("tensor_tensor", f_[:, sl], tmpb[:], psB[:], ALU.add)
                v.flush()
                v.op("tensor_copy", os_[:, sl], f_[:, sl])
                v.flush()
        v.post_inc(vout)
        vout_marks[it] = vout.mark()
        sy.wait(vout)
        for h in range(NHCH):
            sl = slice(h * TSTRIP, (h + 1) * TSTRIP)
            sy.dma(out_ln[h * 128:(h + 1) * 128, tsl], os_[:, sl], inc=[(stws[par], 16)])
            if out_lnf is not None:
                f_ = tmp2b[:, par * NHCH * TSTRIP:(par + 1) * NHCH * TSTRIP]
                sy.dma(out_lnf[h * 128:(h + 1) * 128, tsl], f_[:, sl],
                       inc=[(stws[par], 16)])
    sy.wait(stws[0])
    sy.wait(stws[1])


def emit_copy_x(P, io, bufs, src, dst):
    sy = P.e["sync"]
    lds = [P.sem("cp_ld0"), P.sem("cp_ld1")]
    stws = [P.sem("cp_stw0"), P.sem("cp_stw1")]
    ct = bufs.setdefault("cp_t", P.sbuf("cp_t", [128, 2 * NHCH * TSTRIP], F32))
    CW = NHCH * TSTRIP
    for it in range(NSTRIP):
        par = it % 2
        tsl = slice(it * TSTRIP, (it + 1) * TSTRIP)
        if it >= 2:
            sy.wait(stws[par])
        for h in range(NHCH):
            sl = slice(par * CW + h * TSTRIP, par * CW + (h + 1) * TSTRIP)
            sy.dma(ct[:, sl], src[h * 128:(h + 1) * 128, tsl], inc=[(lds[par], 16)])
        sy.wait(lds[par])
        for h in range(NHCH):
            sl = slice(par * CW + h * TSTRIP, par * CW + (h + 1) * TSTRIP)
            sy.dma(dst[h * 128:(h + 1) * 128, tsl], ct[:, sl], inc=[(stws[par], 16)])
    sy.wait(stws[0])
    sy.wait(stws[1])


def emit_final_add(P, io, bufs, src_x, acc, out, acc_scale=1.0):
    t, v, s, g, sy = (P.e[n] for n in ENGINES)
    lds = [P.sem("fa_ld0"), P.sem("fa_ld1")]
    stws = [P.sem("fa_stw0"), P.sem("fa_stw1")]
    vx = P.sem("fa_vx")
    CW = NHCH * TSTRIP
    xt = bufs.setdefault("ln_xt", P.sbuf("ln_xt", [128, 2 * CW], F32))
    at = bufs.setdefault("ln_at", P.sbuf("ln_at", [128, 2 * CW], F32))
    for it in range(NSTRIP):
        par = it % 2
        xs = xt[:, par * CW:(par + 1) * CW]
        as_ = at[:, par * CW:(par + 1) * CW]
        tsl = slice(it * TSTRIP, (it + 1) * TSTRIP)
        if it >= 2:
            sy.wait(stws[par])
        for h in range(NHCH):
            sy.dma(xs[:, h * TSTRIP:(h + 1) * TSTRIP],
                   src_x[h * 128:(h + 1) * 128, tsl], inc=[(lds[par], 16)])
            sy.dma(as_[:, h * TSTRIP:(h + 1) * TSTRIP],
                   acc[h * 128:(h + 1) * 128, tsl], inc=[(lds[par], 16)])
        v.wait(lds[par])
        for h in range(NHCH):
            sl = slice(h * TSTRIP, (h + 1) * TSTRIP)
            if acc_scale == 1.0:
                v.op("tensor_tensor", xs[:, sl], xs[:, sl], as_[:, sl], ALU.add)
            else:
                v.op("scalar_tensor_tensor", xs[:, sl], as_[:, sl], acc_scale,
                     xs[:, sl], ALU.mult, ALU.add)
        v.flush()
        v.post_inc(vx)
        sy.wait(vx)
        for h in range(NHCH):
            sl = slice(h * TSTRIP, (h + 1) * TSTRIP)
            sy.dma(out[h * 128:(h + 1) * 128, tsl], xs[:, sl], inc=[(stws[par], 16)])
    sy.wait(stws[0])
    sy.wait(stws[1])


def emit_l0f_router(P, io, cb, xlnf, wr_dram, ce_dram, scale=0.5):
    t, v, s, g, sy = (P.e[n] for n in ENGINES)
    wld = P.sem("r0_wld")
    lds = [P.sem("r0_ld0"), P.sem("r0_ld1")]
    stws = [P.sem("r0_stw0"), P.sem("r0_stw1")]
    tmm = P.sem("r0_tmm")
    vv = P.sem("r0_vv")
    ssx = P.sem("r0_ss")
    ttr = P.sem("r0_ttr")
    vtr = P.sem("r0_vtr")

    wrb = P.sbuf("r0_wr", [128, NHCH * 8], F32)
    xtb = P.sbuf("r0_xt", [128, 2 * NHCH * 128], F32)
    pb = P.sbuf("r0_pb", [128, 6 * 8], F32)
    rowb = P.sbuf("r0_row", [128, 8], F32)
    ceb = P.sbuf("r0_ce", [8, 2 * 128], F32)
    mvals = P.sbuf("r0_mv", [128, 8], F32)
    identb = cb["identf"]
    psL = cb["pool"][2]
    psT = cb["pool"][3]

    for h in range(NHCH):
        g.dma(wrb[:, h * 8:(h + 1) * 8], wr_dram[h * 128:(h + 1) * 128, :],
              inc=[(wld, 16)])
    v.op("memset", mvals[:], -3e38)
    v.flush()
    t.wait(wld)

    NT = TC // 128
    vv_marks = {}
    stw_marks = {}
    for it in range(NT):
        par = it % 2
        xs = xtb[:, par * NHCH * 128:(par + 1) * NHCH * 128]
        ces = ceb[:, par * 128:(par + 1) * 128]
        tsl = slice(it * 128, (it + 1) * 128)
        probs = pb[:, 0:8]
        mr = pb[:, 8:16]
        sel = pb[:, 16:24]
        m1 = rowb[:, 0:1]
        nmx = rowb[:, 1:2]
        z = rowb[:, 2:3]
        zi = rowb[:, 3:4]
        m2 = rowb[:, 4:5]
        den = rowb[:, 5:6]
        rden = rowb[:, 6:7]
        if it >= 2:
            sy.wait(vv, vv_marks[it - 2])
        for h in range(NHCH):
            sy.dma(xs[:, h * 128:(h + 1) * 128], xlnf[h * 128:(h + 1) * 128, tsl],
                   inc=[(lds[par], 16)])
        t.wait(lds[par])
        if it >= 1:
            t.wait(ssx)
        for h in range(NHCH):
            t.mm(psL[:, 0:8], xs[:, h * 128:(h + 1) * 128], wrb[:, h * 8:(h + 1) * 8],
                 start=(h == 0), stop=(h == NHCH - 1))
        t.mark_last(tmm)
        v.wait(tmm)
        v.op("reduce_max", m1, psL[:, 0:8], axis=AX.X)
        v.flush()
        v.op("tensor_scalar", nmx, m1, -1.0, None, ALU.mult, inc=[(vv, 1)])
        s.wait(vv)
        s.op("activation", probs, psL[:, 0:8], AF.Exp, bias=nmx, accum_out=z,
             inc=[(ssx, 1)])
        v.wait(ssx)
        v.op("reciprocal", zi, z)
        v.flush()
        v.op("tensor_scalar", probs, probs, zi, None, ALU.mult)
        v.flush()
        v.op("reduce_max", m1, probs, axis=AX.X)
        v.flush()
        v.op("tensor_copy", mvals[:, 0:1], m1)
        v.flush()
        v.op("match_replace", mr, mvals[:], probs, -1e30)
        v.flush()
        v.op("reduce_max", m2, mr, axis=AX.X)
        v.flush()
        v.op("tensor_tensor", den, m1, m2, ALU.add)
        v.op("tensor_scalar", sel, probs, m2, None, ALU.is_ge)
        v.flush()
        v.op("reciprocal", rden, den)
        v.op("tensor_tensor", sel, sel, probs, ALU.mult)
        v.flush()
        v.op("tensor_scalar", sel, sel, rden, scale, ALU.mult, ALU.mult)
        v.flush()
        v.post_inc(vv)
        t.wait(vv)
        t.wait(vtr)
        t.op("transpose", psT[0:8, 0:128], sel, identb[:], inc=[(ttr, 1)])
        v.wait(ttr)
        if it >= 2:
            v.wait(stws[par])
        v.op("tensor_copy", ces, psT[0:8, 0:128])
        v.flush()
        v.post_inc(vtr)
        vv_marks[it] = vv.mark()
        sy.wait(vtr)
        sy.dma(ce_dram[:, tsl], ces, inc=[(stws[par], 16)])
    sy.wait(stws[0])
    sy.wait(stws[1])


def emit_l0f_experts(P, io, cb, xln, ce_dram, wg_dram, wu_dram, wd_dram, acc_dram,
                     first_writes_acc=True):
    t, v, s, g, sy = (P.e[n] for n in ENGINES)
    wld = P.sem("f0_wld")
    xlds = [P.sem("f0_xld0"), P.sem("f0_xld1")]
    clds = [P.sem("f0_cld0"), P.sem("f0_cld1")]
    adws = [P.sem("f0_adw0"), P.sem("f0_adw1")]
    tgu = P.sem("f0_tgu")
    sgu = P.sem("f0_sgu")
    vhm = P.sem("f0_vhm")
    tdn = P.sem("f0_tdn")
    tcb = P.sem("f0_tcb")
    vcb = P.sem("f0_vcb")
    vdo = P.sem("f0_vdo")

    wgb = P.sbuf("f0_wg", [128, NHCH * I0], BF16)
    wub = P.sbuf("f0_wu", [128, NHCH * I0], BF16)
    wdb = P.sbuf("f0_wd", [128, NI0 * H], BF16)
    xb = P.sbuf("f0_xb", [128, 2 * NHCH * TSTRIP], BF16)
    hm = P.sbuf("f0_hm", [128, NI0 * TSTRIP], BF16)
    gs = P.sbuf("f0_gs", [128, 2 * TSTRIP], BF16)
    cer = P.sbuf("f0_cer", [1, 2 * TSTRIP], F32)
    cbb = P.sbuf("f0_cb", [128, TSTRIP], F32)
    dob = P.sbuf("f0_do", [128, 2 * TSTRIP], F32)
    psGU = [cb["pool"][i] for i in range(4)]
    psDN = [cb["pool"][4], cb["pool"][5]]
    psCB = cb["pool"][0]
    onesf = cb["ones1f"]

    tgu_marks = []
    for e in range(E0):
        if e > 0:
            g.wait(tgu)
            g.wait(tdn)
        for h in range(NHCH):
            g.dma(wgb[:, h * I0:(h + 1) * I0],
                  wg_dram[e * H + h * 128:e * H + (h + 1) * 128, :], inc=[(wld, 16)])
            g.dma(wub[:, h * I0:(h + 1) * I0],
                  wu_dram[e * H + h * 128:e * H + (h + 1) * 128, :], inc=[(wld, 16)])
        for i in range(NI0):
            g.dma(wdb[:, i * H:(i + 1) * H],
                  wd_dram[e * I0 + i * 128:e * I0 + (i + 1) * 128, :], inc=[(wld, 16)])
        t.wait(wld)
        for it in range(NSTRIP):
            par = it % 2
            xs = xb[:, par * NHCH * TSTRIP:(par + 1) * NHCH * TSTRIP]
            cers = cer[0:1, par * TSTRIP:(par + 1) * TSTRIP]
            dos = [dob[:, 0:TSTRIP], dob[:, TSTRIP:2 * TSTRIP]]
            tsl = slice(it * TSTRIP, (it + 1) * TSTRIP)
            le = e * NSTRIP + it
            if le >= 2:
                sy.wait(tgu, tgu_marks[le - 2])
            for h in range(NHCH):
                sy.dma(xs[:, h * TSTRIP:(h + 1) * TSTRIP],
                       xln[h * 128:(h + 1) * 128, tsl], inc=[(xlds[par], 16)])
            sy.dma(cers, ce_dram[e:e + 1, tsl], inc=[(clds[par], 16)])
            t.wait(xlds[par])
            vhm_base = vhm.mark()
            sgu_base = sgu.mark()
            for i in range(NI0):
                pg = psGU[2 * (i % 2)]
                pu = psGU[2 * (i % 2) + 1]
                if i >= 2:
                    t.wait(sgu, sgu_base + i - 1)
                    t.wait(vhm, vhm_base + i - 1)
                for h in range(NHCH):
                    t.mm(pg[:], wgb[:, h * I0 + i * 128:h * I0 + (i + 1) * 128],
                         xs[:, h * TSTRIP:(h + 1) * TSTRIP],
                         start=(h == 0), stop=(h == NHCH - 1))
                t.mark_last(tgu)
                for h in range(NHCH):
                    t.mm(pu[:], wub[:, h * I0 + i * 128:h * I0 + (i + 1) * 128],
                         xs[:, h * TSTRIP:(h + 1) * TSTRIP],
                         start=(h == 0), stop=(h == NHCH - 1))
                t.mark_last(tgu)
                s.wait(tgu, tgu.mark() - 1)
                s.op("activation", gs[:, (i % 2) * TSTRIP:(i % 2 + 1) * TSTRIP],
                     pg[:], AF.Silu, inc=[(sgu, 1)])
                v.wait(tgu)
                v.wait(sgu)
                v.op("tensor_tensor", hm[:, i * TSTRIP:(i + 1) * TSTRIP],
                     gs[:, (i % 2) * TSTRIP:(i % 2 + 1) * TSTRIP], pu[:], ALU.mult,
                     inc=[(vhm, 1)])
            tgu_marks.append(tgu.mark())
            t.wait(clds[par])
            t.wait(sgu)
            t.mm(psCB[:], onesf[0:1, :], cers, start=True, stop=True,
                 inc=[(tcb, 1)])
            v.wait(tcb)
            v.op("tensor_copy", cbb[:], psCB[:], inc=[(vcb, 1)])
            for h in range(NHCH):
                pd = psDN[h % 2]
                if h >= 2:
                    t.wait(vdo, vdo.mark() - 1)
                if h == 0:
                    t.wait(vhm)
                for i in range(NI0):
                    t.mm(pd[:], wdb[:, i * H + h * 128:i * H + (h + 1) * 128],
                         hm[:, i * TSTRIP:(i + 1) * TSTRIP],
                         start=(i == 0), stop=(i == NI0 - 1))
                t.mark_last(tdn)
                v.wait(tdn)
                v.wait(vcb)
                v.wait(adws[h % 2])
                v.op("tensor_tensor", dos[h % 2], pd[:], cbb[:], ALU.mult,
                     inc=[(vdo, 1)])
                g.wait(vdo)
                if e == 0 and first_writes_acc:
                    g.dma(acc_dram[h * 128:(h + 1) * 128, tsl], dos[h % 2],
                          inc=[(adws[h % 2], 16)])
                else:
                    g.dma(acc_dram[h * 128:(h + 1) * 128, tsl], dos[h % 2],
                          accum_op=ALU.add, inc=[(adws[h % 2], 16)])
    g.wait(adws[0])
    g.wait(adws[1])


def emit_l0attn(P, io, cb, xln, wqk_dram, wo_dram, bo_dram, mask_dram, acc_dram):
    t, v, s, g, sy = (P.e[n] for n in ENGINES)
    wl = P.sem("a0_wl")
    xls = [P.sem("a0_xl0"), P.sem("a0_xl1")]
    aws = [P.sem("a0_aw0"), P.sem("a0_aw1")]
    tpj = P.sem("a0_tpj")
    vpj = P.sem("a0_vpj")
    ttr = P.sem("a0_ttr")
    vtr = P.sem("a0_vtr")
    tsc = P.sem("a0_tsc")
    vsc = P.sem("a0_vsc")
    ssx = P.sem("a0_ssx")
    vnp = P.sem("a0_vnp")
    tpt = P.sem("a0_tpt")
    vpt = P.sem("a0_vpt")
    tpv = P.sem("a0_tpv")
    vav = P.sem("a0_vav")
    top = P.sem("a0_top")
    vop = P.sem("a0_vop")

    SB = S
    wqkb = P.sbuf("a0_wqk", [128, NHCH * 2 * H], BF16)
    wob = P.sbuf("a0_wo", [128, NHCH * H], BF16)
    bob = P.sbuf("a0_bo", [128, NHCH], F32)
    maskb = P.sbuf("a0_mask", [128, 2 * WIN], F32)
    xb = P.sbuf("a0_xb", [128, 2 * NHCH * TSTRIP], BF16)
    qf = P.sbuf("a0_qf", [128, NHCH * SB], BF16)
    kf = P.sbuf("a0_kf", [128, NHCH * (SB + WIN)], BF16)
    kt = P.sbuf("a0_kt", [128, (SB // 128 + 1) * H], BF16)
    af = P.sbuf("a0_af", [128, NHCH * TSTRIP], BF16)
    snb = P.sbuf("a0_sn", [128, 2 * WIN], F32)
    peb = P.sbuf("a0_pe", [128, 2 * WIN], F32)
    pnb = P.sbuf("a0_pn", [128, 2 * WIN], BF16)
    ptb = P.sbuf("a0_pt", [128, 2 * WIN], BF16)
    rowz = P.sbuf("a0_rz", [128, 4], F32)
    dob = P.sbuf("a0_do", [128, 2 * TSTRIP], F32)
    psP = [P.psum("a0_psp0", [128, TSTRIP]), P.psum("a0_psp1", [128, TSTRIP])]
    psS = P.psum("a0_pss", [128, 2 * WIN])
    psT1 = P.psum("a0_pst1", [128, 128], BF16)
    psT2 = P.psum("a0_pst2", [128, 128], BF16)
    psV = [P.psum("a0_psv0", [128, 128]), P.psum("a0_psv1", [128, 128])]

    NWIN = SB // WIN
    NTK = SB // 128
    tpj_marks = {}

    for h in range(NHCH):
        g.dma(wqkb[:, h * 2 * H:(h + 1) * 2 * H],
              wqk_dram[h * 128:(h + 1) * 128, :], inc=[(wl, 16)])
        g.dma(wob[:, h * H:(h + 1) * H],
              wo_dram[h * 128:(h + 1) * 128, :], inc=[(wl, 16)])
    g.dma(bob[:], bo_dram[:], inc=[(wl, 16)])
    g.dma(maskb[:], mask_dram[:], inc=[(wl, 16)])
    v.op("memset", kf[:, :], 0.0)
    v.op("memset", kt[:, 0:H], 0.0)
    v.flush()
    v.post_inc(vpj)
    t.wait(wl)

    for b in range(BPC):
        base = b * SB
        nstr = SB // TSTRIP
        for it in range(nstr):
            par = it % 2
            xs = xb[:, par * NHCH * TSTRIP:(par + 1) * NHCH * TSTRIP]
            tsl = slice(base + it * TSTRIP, base + (it + 1) * TSTRIP)
            lsl = slice(it * TSTRIP, (it + 1) * TSTRIP)
            gi = b * nstr + it
            if gi >= 2:
                sy.wait(tpj, tpj_marks[gi - 2])
            for h in range(NHCH):
                sy.dma(xs[:, h * TSTRIP:(h + 1) * TSTRIP],
                       xln[h * 128:(h + 1) * 128, tsl], inc=[(xls[par], 16)])
            t.wait(xls[par])
            for oc in range(NHCH):
                pp = psP[oc % 2]
                if gi > 0 or oc >= 2:
                    t.wait(vpj, vpj.mark() - 1)
                for h in range(NHCH):
                    t.mm(pp[:], wqkb[:, h * 2 * H + oc * 128:h * 2 * H + (oc + 1) * 128],
                         xs[:, h * TSTRIP:(h + 1) * TSTRIP],
                         start=(h == 0), stop=(h == NHCH - 1))
                t.mark_last(tpj)
                v.wait(tpj)
                v.op("tensor_copy", qf[:, oc * SB + it * TSTRIP:oc * SB + (it + 1) * TSTRIP],
                     pp[:], inc=[(vpj, 1)])
            for oc in range(NHCH):
                pp = psP[oc % 2]
                t.wait(vpj, vpj.mark() - 1)
                for h in range(NHCH):
                    t.mm(pp[:], wqkb[:, h * 2 * H + H + oc * 128:h * 2 * H + H + (oc + 1) * 128],
                         xs[:, h * TSTRIP:(h + 1) * TSTRIP],
                         start=(h == 0), stop=(h == NHCH - 1))
                t.mark_last(tpj)
                v.wait(tpj)
                v.op("tensor_copy",
                     kf[:, oc * (SB + WIN) + WIN + it * TSTRIP:oc * (SB + WIN) + WIN + (it + 1) * TSTRIP],
                     pp[:], inc=[(vpj, 1)])
            tpj_marks[gi] = tpj.mark()
        t.wait(vpj)
        for tt in range(NTK):
            for oc in range(NHCH):
                pst = psT1 if (tt * NHCH + oc) % 2 == 0 else psT2
                if tt * NHCH + oc >= 2:
                    t.wait(vtr, vtr.mark() - 1)
                t.op("transpose", pst[:],
                     kf[:, oc * (SB + WIN) + WIN + tt * 128:oc * (SB + WIN) + WIN + (tt + 1) * 128],
                     cb["identb"][:], inc=[(ttr, 1)])
                v.wait(ttr)
                v.op("tensor_copy", kt[:, (tt + 1) * H + oc * 128:(tt + 1) * H + (oc + 1) * 128],
                     pst[:], inc=[(vtr, 1)])
        for w in range(NWIN):
            qsl = slice(w * WIN, (w + 1) * WIN)
            t.wait(vsc)
            for h in range(NHCH):
                t.mm(psS[:], qf[:, h * SB + w * WIN:h * SB + (w + 1) * WIN],
                     kf[:, h * (SB + WIN) + w * WIN:h * (SB + WIN) + (w + 2) * WIN],
                     start=(h == 0), stop=(h == NHCH - 1))
            t.mark_last(tsc)
            v.wait(tsc)
            v.op("scalar_tensor_tensor", snb[:], psS[:], 1.0 / 32.0, maskb[:],
                 ALU.mult, ALU.add)
            v.flush()
            v.op("reduce_max", rowz[:, 0:1], snb[:], axis=AX.X)
            v.flush()
            v.op("tensor_scalar", rowz[:, 1:2], rowz[:, 0:1], -1.0, None, ALU.mult,
                 inc=[(vsc, 1)])
            s.wait(vsc)
            s.op("activation", peb[:], snb[:], AF.Exp, bias=rowz[:, 1:2],
                 accum_out=rowz[:, 2:3], inc=[(ssx, 1)])
            v.wait(ssx)
            v.op("reciprocal", rowz[:, 3:4], rowz[:, 2:3])
            v.flush()
            v.op("tensor_scalar", pnb[:], peb[:], rowz[:, 3:4], None, ALU.mult)
            v.flush()
            v.post_inc(vnp)
            t.wait(vnp)
            for kk in range(2):
                pst = psT1 if kk == 0 else psT2
                t.wait(vpt)
                t.op("transpose", pst[:], pnb[:, kk * 128:(kk + 1) * 128],
                     cb["identb"][:], inc=[(tpt, 1)])
                v.wait(tpt)
                v.op("tensor_copy", ptb[:, kk * 128:(kk + 1) * 128], pst[:],
                     inc=[(vpt, 1)])
            t.wait(vpt)
            for oc in range(NHCH):
                pv = psV[oc % 2]
                if oc >= 2:
                    t.wait(vav, vav.mark() - 1)
                for kk in range(2):
                    t.mm(pv[:], kt[:, (w + kk) * H + oc * 128:(w + kk) * H + (oc + 1) * 128],
                         ptb[:, kk * 128:(kk + 1) * 128],
                         start=(kk == 0), stop=(kk == 1))
                t.mark_last(tpv)
                v.wait(tpv)
                v.op("tensor_copy",
                     af[:, oc * TSTRIP + (w % 4) * WIN:oc * TSTRIP + (w % 4 + 1) * WIN],
                     pv[:], inc=[(vav, 1)])
            if w % 4 == 3:
                it = w // 4
                tsl = slice(base + it * TSTRIP, base + (it + 1) * TSTRIP)
                t.wait(vav)
                for oc in range(NHCH):
                    pp = psP[oc % 2]
                    if True:
                        t.wait(vop, vop.mark() - 1) if (b + it + oc) > 0 else None
                    for h in range(NHCH):
                        t.mm(pp[:], wob[:, h * H + oc * 128:h * H + (oc + 1) * 128],
                             af[:, h * TSTRIP:(h + 1) * TSTRIP],
                             start=(h == 0), stop=(h == NHCH - 1))
                    t.mark_last(top)
                    s.wait(top)
                    s.wait(aws[oc % 2])
                    s.op("activation", dob[:, (oc % 2) * TSTRIP:(oc % 2) * TSTRIP + TSTRIP],
                         pp[:], AF.Identity, bias=bob[:, oc:oc + 1], inc=[(vop, 1)])
                    sy.wait(vop)
                    sy.dma(acc_dram[oc * 128:(oc + 1) * 128, tsl],
                           dob[:, (oc % 2) * TSTRIP:(oc % 2) * TSTRIP + TSTRIP],
                           inc=[(aws[oc % 2], 16)])
    sy.wait(aws[0])
    sy.wait(aws[1])


def _softmax_top2_store(P, cb, psL, pb, rowb, mvals, ceb_slice, scale, vv, ssx,
                        ttr, vtr, psT, identf):
    t, v, s = P.e["tensor"], P.e["vector"], P.e["scalar"]
    probs = pb[:, 0:8]
    mr = pb[:, 8:16]
    sel = pb[:, 16:24]
    m1 = rowb[:, 0:1]
    nmx = rowb[:, 1:2]
    z = rowb[:, 2:3]
    zi = rowb[:, 3:4]
    m2 = rowb[:, 4:5]
    den = rowb[:, 5:6]
    rden = rowb[:, 6:7]
    v.op("reduce_max", m1, psL[:, 0:8], axis=AX.X)
    v.flush()
    v.op("tensor_scalar", nmx, m1, -1.0, None, ALU.mult, inc=[(vv, 1)])
    s.wait(vv)
    s.op("activation", probs, psL[:, 0:8], AF.Exp, bias=nmx, accum_out=z,
         inc=[(ssx, 1)])
    v.wait(ssx)
    v.op("reciprocal", zi, z)
    v.flush()
    v.op("tensor_scalar", probs, probs, zi, None, ALU.mult)
    v.flush()
    v.op("reduce_max", m1, probs, axis=AX.X)
    v.flush()
    v.op("tensor_copy", mvals[:, 0:1], m1)
    v.flush()
    v.op("match_replace", mr, mvals[:], probs, -1e30)
    v.flush()
    v.op("reduce_max", m2, mr, axis=AX.X)
    v.flush()
    v.op("tensor_tensor", den, m1, m2, ALU.add)
    v.op("tensor_scalar", sel, probs, m2, None, ALU.is_ge)
    v.flush()
    v.op("reciprocal", rden, den)
    v.op("tensor_tensor", sel, sel, probs, ALU.mult)
    v.flush()
    v.op("tensor_scalar", sel, sel, rden, scale, ALU.mult, ALU.mult)
    v.flush()
    v.post_inc(vv)
    t.wait(vv)
    t.wait(vtr)
    t.op("transpose", psT[0:8, 0:128], sel, identf[:], inc=[(ttr, 1)])
    v.wait(ttr)
    v.op("tensor_copy", ceb_slice, psT[0:8, 0:128])
    v.flush()
    v.post_inc(vtr)


def emit_l1f_router(P, io, cb, xlnf, rw1_dram, rb1_dram, lng_dram, lnb_dram,
                    rw2_dram, rb2_dram, ge_dram, scale=0.5):
    t, v, s, g, sy = (P.e[n] for n in ENGINES)
    lds = [P.sem("r1_ld0"), P.sem("r1_ld1")]
    stws = [P.sem("r1_stw0"), P.sem("r1_stw1")]
    wld = P.sem("r1_wld")
    thh = P.sem("r1_thh")
    vhn = P.sem("r1_vhn")
    tst = P.sem("r1_tst")
    vrow = P.sem("r1_vrow")
    vst = P.sem("r1_vst")
    sst = P.sem("r1_sst")
    tbc = P.sem("r1_tbc")
    tmm = P.sem("r1_tmm")
    vv = P.sem("r1_vv")
    ssx = P.sem("r1_ss")
    ttr = P.sem("r1_ttr")
    vtr = P.sem("r1_vtr")

    NRC = RH // 128
    w1b = P.sbuf("r1_w1", [128, NHCH * RH], F32)
    w2b = P.sbuf("r1_w2", [128, NRC * 8], F32)
    rb1b = P.sbuf("r1_rb1", [128, NRC], F32)
    lngb = P.sbuf("r1_lng", [128, NRC], F32)
    lnbb = P.sbuf("r1_lnb", [128, NRC], F32)
    rb2b = P.sbuf("r1_rb2", [1, 8], F32)
    xtb = P.sbuf("r1_xt", [128, 2 * NHCH * 128], F32)
    hb = P.sbuf("r1_hb", [128, NRC * 128], F32)
    hnb = P.sbuf("r1_hn", [128, NRC * 128], F32)
    sqb = P.sbuf("r1_sq", [128, 128], F32)
    strow = P.sbuf("r1_srow", [1, 4 * 128], F32)
    abrow = P.sbuf("r1_ab", [128, 2 * 128], F32)
    pb = P.sbuf("r1_pb", [128, 24], F32)
    rowb = P.sbuf("r1_rowb", [128, 8], F32)
    mvals = P.sbuf("r1_mv", [128, 8], F32)
    ceb = P.sbuf("r1_ce", [8, 2 * 128], F32)
    psH = cb["pool"][0]
    psBCA = cb["pool"][1]
    psBCB = cb["pool"][4]
    psL = cb["pool"][2]
    psT = cb["pool"][3]
    psum_r, pssq_r = cb["poolr"][0], cb["poolr"][1]
    onesf = cb["ones1f"]
    onescf = cb["onescf"]
    identf = cb["identf"]

    for h in range(NHCH):
        g.dma(w1b[:, h * RH:(h + 1) * RH], rw1_dram[h * 128:(h + 1) * 128, :],
              inc=[(wld, 16)])
    for c in range(NRC):
        g.dma(w2b[:, c * 8:(c + 1) * 8], rw2_dram[c * 128:(c + 1) * 128, :],
              inc=[(wld, 16)])
    g.dma(rb1b[:], rb1_dram[:], inc=[(wld, 16)])
    g.dma(lngb[:], lng_dram[:], inc=[(wld, 16)])
    g.dma(lnbb[:], lnb_dram[:], inc=[(wld, 16)])
    g.dma(rb2b[:], rb2_dram[:], inc=[(wld, 16)])
    v.op("memset", mvals[:], -3e38)
    v.flush()
    t.wait(wld)
    s.wait(wld)
    v.wait(wld)

    NT = TC // 128
    vv_marks = {}
    stw_marks = {}
    for it in range(NT):
        par = it % 2
        xs = xtb[:, par * NHCH * 128:(par + 1) * NHCH * 128]
        ces = ceb[:, par * 128:(par + 1) * 128]
        tsl = slice(it * 128, (it + 1) * 128)
        r_mu = strow[0:1, 0:128]
        r_v1 = strow[0:1, 128:256]
        r_v2 = strow[0:1, 256:384]
        rA = abrow[:, 0:128]
        rB = abrow[:, 128:256]
        if it >= 2:
            sy.wait(vv, vv_marks[it - 2])
        for h in range(NHCH):
            sy.dma(xs[:, h * 128:(h + 1) * 128], xlnf[h * 128:(h + 1) * 128, tsl],
                   inc=[(lds[par], 16)])
        t.wait(lds[par])
        for c in range(NRC):
            if it >= 1 or c >= 1:
                t.wait(vhn)
            for h in range(NHCH):
                t.mm(psH[:, 0:128], w1b[:, h * RH + c * 128:h * RH + (c + 1) * 128],
                     xs[:, h * 128:(h + 1) * 128],
                     start=(h == 0), stop=(h == NHCH - 1))
            t.mark_last(thh)
            v.wait(thh)
            v.op("tensor_scalar", hb[:, c * 128:(c + 1) * 128], psH[:, 0:128],
                 rb1b[:, c:c + 1], None, ALU.add, inc=[(vhn, 1)])
        if it >= 1:
            t.wait(vrow)
        t.wait(vhn)
        for c in range(NRC):
            t.mm(psum_r[0:1, 0:128], onescf[:, 0:1], hb[:, c * 128:(c + 1) * 128],
                 start=(c == 0), stop=(c == NRC - 1))
        sq_s = P.sem("r1_sqs")
        tsq = P.sem("r1_tsq")
        s.wait(vhn)
        tsq_base = tsq.mark()
        for c in range(NRC):
            if c >= 1:
                s.wait(tsq, tsq_base + c)
            s.op("activation", sqb[:], hb[:, c * 128:(c + 1) * 128], AF.Square,
                 inc=[(sq_s, 1)])
            t.wait(sq_s)
            t.mm(pssq_r[0:1, 0:128], onescf[:, 0:1], sqb[:],
                 start=(c == 0), stop=(c == NRC - 1), inc=[(tsq, 1)])
        v.wait(tsq)
        v.op("tensor_scalar", r_mu, psum_r[0:1, 0:128], 1.0 / RH, None, ALU.mult)
        v.op("tensor_scalar", r_v1, pssq_r[0:1, 0:128], 1.0 / RH, None, ALU.mult)
        v.flush()
        v.post_inc(vrow)
        v.op("tensor_tensor", r_v2, r_mu, r_mu, ALU.mult)
        v.flush()
        v.op("tensor_tensor", r_v1, r_v1, r_v2, ALU.subtract)
        v.flush()
        v.op("tensor_scalar", r_v1, r_v1, 1e-5, None, ALU.add, inc=[(vst, 1)])
        s.wait(vst)
        s.op("activation", r_v2, r_v1, AF.Sqrt, inc=[(sst, 1)])
        v.wait(sst)
        v.op("reciprocal", r_v1, r_v2)
        v.flush()
        v.op("tensor_tensor", r_v2, r_mu, r_v1, ALU.mult)
        v.flush()
        v.op("tensor_scalar", r_v2, r_v2, -1.0, None, ALU.mult)
        v.flush()
        v.post_inc(vst)
        t.wait(vst)
        if it >= 1:
            t.wait(vhn)
        t.mm(psBCA[:, 0:128], onesf[0:1, :], r_v1, start=True, stop=True)
        t.mm(psBCB[:, 0:128], onesf[0:1, :], r_v2, start=True, stop=True)
        t.mark_last(tbc)
        v.wait(tbc)
        for c in range(NRC):
            v.op("tensor_tensor", hnb[:, c * 128:(c + 1) * 128],
                 hb[:, c * 128:(c + 1) * 128], psBCA[:, 0:128], ALU.mult)
            v.flush()
            v.op("tensor_tensor", hnb[:, c * 128:(c + 1) * 128],
                 hnb[:, c * 128:(c + 1) * 128], psBCB[:, 0:128], ALU.add)
            v.flush()
            v.op("tensor_scalar", hnb[:, c * 128:(c + 1) * 128],
                 hnb[:, c * 128:(c + 1) * 128], lngb[:, c:c + 1], lnbb[:, c:c + 1],
                 ALU.mult, ALU.add)
            v.flush()
            v.op("tensor_relu", hnb[:, c * 128:(c + 1) * 128],
                 hnb[:, c * 128:(c + 1) * 128])
            v.flush()
        v.post_inc(vhn)
        t.wait(vhn)
        if it >= 1:
            t.wait(ssx)
        for c in range(NRC):
            t.mm(psL[:, 0:8], hnb[:, c * 128:(c + 1) * 128], w2b[:, c * 8:(c + 1) * 8],
                 start=(c == 0), stop=False)
        t.mm(psL[:, 0:8], onesf[0:1, :], rb2b[:], start=False, stop=True)
        t.mark_last(tmm)
        v.wait(tmm)
        if it >= 2:
            v.wait(stws[par])
        _softmax_top2_store(P, cb, psL, pb, rowb, mvals, ces, scale, vv, ssx,
                            ttr, vtr, psT, identf)
        vv_marks[it] = vv.mark()
        sy.wait(vtr)
        sy.dma(ge_dram[:, tsl], ces, inc=[(stws[par], 16)])
    sy.wait(stws[0])
    sy.wait(stws[1])


def emit_l1f_experts(P, io, cb, xln, ge_dram, wq_dram, wk_dram, wv_dram, wo_dram,
                     bq_dram, bk_dram, bv_dram, bo_dram, acc_dram,
                     first_writes_acc=True, nexp=None):
    t, v, s, g, sy = (P.e[n] for n in ENGINES)
    wls = [P.sem("f1_wl0"), P.sem("f1_wl1")]
    bl = P.sem("f1_bl")
    xl = P.sem("f1_xl")
    gls = [P.sem("f1_gl0"), P.sem("f1_gl1")]
    aws = [P.sem("f1_aw0"), P.sem("f1_aw1")]
    tpj = P.sem("f1_tpj")
    spj = P.sem("f1_spj")
    ttr = P.sem("f1_ttr")
    vtr = P.sem("f1_vtr")
    tsc = P.sem("f1_tsc")
    ssx = P.sem("f1_ssx")
    tpv = P.sem("f1_tpv")
    vuv = P.sem("f1_vuv")
    tzb = P.sem("f1_tzb")
    vzb = P.sem("f1_vzb")
    vaf = P.sem("f1_vaf")
    tgb = P.sem("f1_tgb")
    vgb = P.sem("f1_vgb")
    tpo = P.sem("f1_tpo")
    vpo = P.sem("f1_vpo")

    SB = S
    NTK = SB // 128
    NQB = SB // TSTRIP
    VW = 65

    wbuf2 = P.sbuf("f1_w", [128, 2 * NHCH * H], BF16)
    qf = P.sbuf("f1_qf", [128, NHCH * SB], BF16)
    kf = P.sbuf("f1_kf", [128, NHCH * SB], BF16)
    vt = P.sbuf("f1_vt", [128, NTK * NH1 * VW], BF16)
    scr = P.sbuf("f1_scr", [128, 2 * NTK * TSTRIP], BF16)
    af = P.sbuf("f1_af", [128, NHCH * TSTRIP], BF16)
    xb = P.sbuf("f1_xb", [128, NHCH * TSTRIP], BF16)
    bqb = P.sbuf("f1_bq", [128, 4 * NHCH], F32)
    gbb = P.sbuf("f1_gb", [128, TSTRIP], F32)
    zbb = P.sbuf("f1_zb", [128, 2 * TSTRIP], F32)
    dob = P.sbuf("f1_do", [128, 2 * TSTRIP], F32)
    grow = P.sbuf("f1_grow", [1, 2 * TSTRIP], F32)
    zrow = P.sbuf("f1_zrow", [1, 2 * TSTRIP], F32)
    zrowb = P.sbuf("f1_zrowb", [1, 2 * TSTRIP], BF16)
    sc8 = P.sbuf("f1_sc8", [128, 1], F32)
    nb8 = P.sbuf("f1_nb8", [128, 1], F32)
    sm1 = P.sbuf("f1_sm1", [128, 1], F32)
    dumr = P.sbuf("f1_dumr", [1, TSTRIP], BF16)
    pp = [P.psum("f1_pp0", [128, TSTRIP]), P.psum("f1_pp1", [128, TSTRIP])]
    psSbig = P.psum("f1_psbig", [128, 4 * TSTRIP])
    psT = [P.psum("f1_pt0", [128, TSTRIP]), P.psum("f1_pt1", [128, TSTRIP])]
    psTb = [pT[:].bitcast(BF16) for pT in psT]

    ini = P.sem("f1_ini")
    v.op("memset", sc8[:], 0.125)
    v.op("memset", nb8[:], -8.0)
    v.op("memset", sm1[:], -1.0)
    v.op("memset", dumr[:], 0.0)
    vt3 = vt[:].rearrange("p (t w) -> p t w", w=VW)
    v.op("memset", vt3[:, :, 64:65], 1.0)
    v.flush()
    v.post_inc(ini)
    s_.wait(ini) if False else P.e["scalar"].wait(ini)
    t.wait(ini)

    onesf = cb["ones1f"]
    ones1b = cb["ones1b"]
    identb = cb["identb"]
    NE = nexp if nexp is not None else E1
    PH = globals().get("L1F_PHASE", 5)

    wpar_ctr = [0]
    half_guard = {0: [], 1: []}

    def next_half():
        wp = wpar_ctr[0] % 2
        wpar_ctr[0] += 1
        return wp

    for e in range(NE):
        g.wait(vpo)
        g.dma(bqb[:, 0:NHCH], bq_dram[e * 128:(e + 1) * 128, :], inc=[(bl, 16)])
        g.dma(bqb[:, NHCH:2 * NHCH], bk_dram[e * 128:(e + 1) * 128, :], inc=[(bl, 16)])
        g.dma(bqb[:, 2 * NHCH:3 * NHCH], bv_dram[e * 128:(e + 1) * 128, :], inc=[(bl, 16)])
        g.dma(bqb[:, 3 * NHCH:4 * NHCH], bo_dram[e * 128:(e + 1) * 128, :], inc=[(bl, 16)])
        s.wait(bl)
        for b in range(BPC):
            base = b * SB
            for pj, wd in ((0, wq_dram), (1, wk_dram), (2, wv_dram)):
                wpar = next_half()
                wbuf = wbuf2[:, wpar * NHCH * H:(wpar + 1) * NHCH * H]
                for sem_, val_ in half_guard[wpar]:
                    g.wait(sem_, val_)
                for h in range(NHCH):
                    g.dma(wbuf[:, h * H:(h + 1) * H],
                          wd[e * H + h * 128:e * H + (h + 1) * 128, :],
                          inc=[(wls[wpar], 16)])
                t.wait(wls[wpar])
                for it in range(SB // TSTRIP):
                    tsl = slice(base + it * TSTRIP, base + (it + 1) * TSTRIP)
                    lo = it * TSTRIP
                    sy.wait(tpj)
                    for h in range(NHCH):
                        sy.dma(xb[:, h * TSTRIP:(h + 1) * TSTRIP],
                               xln[h * 128:(h + 1) * 128, tsl], inc=[(xl, 16)])
                    t.wait(xl)
                    for oc in range(NHCH):
                        p_ = pp[oc % 2]
                        if spj.mark() >= 1:
                            t.wait(spj, spj.mark() - 1)
                        for h in range(NHCH):
                            t.mm(p_[:], wbuf[:, h * H + oc * 128:h * H + (oc + 1) * 128],
                                 xb[:, h * TSTRIP:(h + 1) * TSTRIP],
                                 start=(h == 0), stop=(h == NHCH - 1))
                        t.mark_last(tpj)
                        s.wait(tpj)
                        if pj == 2:
                            odst = scr[:, oc * SB + lo:oc * SB + lo + TSTRIP]
                        elif pj == 0:
                            odst = qf[:, oc * SB + lo:oc * SB + lo + TSTRIP]
                        else:
                            odst = kf[:, oc * SB + lo:oc * SB + lo + TSTRIP]
                        s.op("activation", odst, p_[:], AF.Identity,
                             bias=bqb[:, pj * NHCH + oc:pj * NHCH + oc + 1],
                             inc=[(spj, 1)])
                half_guard[wpar] = [(tpj, tpj.mark())]
            if PH < 2:
                continue
            t.wait(spj)
            for tt in range(NTK):
                for c in range(NHCH):
                    j = tt * NHCH + c
                    pst = psTb[j % 2]
                    if j >= 2:
                        t.wait(vtr, vtr.mark() - 2)
                    t.op("transpose", pst[:, 0:128],
                         scr[:, c * SB + tt * 128:c * SB + (tt + 1) * 128],
                         identb[:], inc=[(ttr, 1)])
                    v.wait(ttr)
                    v.op("tensor_copy",
                         vt3[:, tt * NH1 + 2 * c, 0:64], pst[:, 0:64], inc=[(vtr, 1)])
                    v.op("tensor_copy",
                         vt3[:, tt * NH1 + 2 * c + 1, 0:64], pst[:, 64:128],
                         inc=[(vtr, 1)])
            if PH < 3:
                continue
            s.wait(ttr)
            t.wait(vtr)
            for qb in range(NQB):
                qpar = qb % 2
                grs = grow[0:1, qpar * TSTRIP:(qpar + 1) * TSTRIP]
                if tgb.mark() >= 2:
                    sy.wait(tgb, tgb.mark() - 1)
                sy.dma(grs, ge_dram[e:e + 1, base + qb * TSTRIP:
                                    base + (qb + 1) * TSTRIP], inc=[(gls[qpar], 16)])
                t.wait(gls[qpar])
                t.wait(ssx)
                t.wait(vzb)
                t.mm(psSbig[:, 0:TSTRIP], onesf[0:1, :], grs, start=True, stop=True,
                     inc=[(tgb, 1)])
                v.wait(tgb)
                v.wait(vpo)
                v.op("tensor_copy", gbb[:], psSbig[:, 0:TSTRIP], inc=[(vgb, 1)])
                def make_normalize(h0):
                    tpv_b = tpv.mark() - 2

                    def emit():
                        vz0 = P.sem("f1_vz0")
                        vz1 = P.sem("f1_vz1")
                        s.wait(tpv, tpv_b + 1)
                        s.op("activation", zrow[0:1, 0:TSTRIP], psT[0][64:65, :],
                             AF.Ln)
                        s.flush()
                        s.op("activation", zrowb[0:1, 0:TSTRIP],
                             zrow[0:1, 0:TSTRIP], AF.Exp, scale=sm1[0:1, 0:1],
                             inc=[(vz0, 1)])
                        s.wait(tpv, tpv_b + 2)
                        s.op("activation", zrow[0:1, TSTRIP:2 * TSTRIP],
                             psT[1][64:65, :], AF.Ln)
                        s.flush()
                        s.op("activation", zrowb[0:1, TSTRIP:2 * TSTRIP],
                             zrow[0:1, TSTRIP:2 * TSTRIP], AF.Exp,
                             scale=sm1[0:1, 0:1], inc=[(vz1, 1)])
                        t.wait(vz0)
                        t.wait(vzb)
                        t.mm(psT[0][64:128, :], ones1b[0:1, 0:64],
                             zrowb[0:1, 0:TSTRIP], start=True, stop=True,
                             skip_group_check=True)
                        t.wait(vz1)
                        t.mm(psT[1][64:128, :], ones1b[0:1, 0:64],
                             zrowb[0:1, TSTRIP:2 * TSTRIP], start=True, stop=True,
                             skip_group_check=True, inc=[(tzb, 1)])
                        v.wait(tzb)
                        for h2 in range(2):
                            h = h0 + h2
                            zs = zbb[0:64, h2 * TSTRIP:(h2 + 1) * TSTRIP]
                            v.op("tensor_copy", zs, psT[h2][64:128, :])
                            v.flush()
                            v.op("tensor_tensor",
                                 af[(h % 2) * 64:(h % 2) * 64 + 64,
                                    (h // 2) * TSTRIP:(h // 2 + 1) * TSTRIP],
                                 psT[h2][0:64, :], zs, ALU.mult,
                                 inc=[(vuv, 1)])
                        v.post_inc(vzb)
                        v.post_inc(vaf)
                    return emit

                pending = None
                for hp in range(NH1 // 2):
                    h0 = 2 * hp
                    c = hp
                    if hp == 0:
                        t.wait(vgb)
                    ssx_base = ssx.mark()
                    tpv_base = tpv.mark()
                    for kt in range(NTK):
                        bk0 = psSbig[:, (kt % 2) * TSTRIP:(kt % 2 + 1) * TSTRIP]
                        bk1 = psSbig[:, (2 + kt % 2) * TSTRIP:(3 + kt % 2) * TSTRIP]
                        for _d in range(3):
                            t.mm(pp[0][:], ones1b[0:1, :], dumr[0:1, :],
                                 start=True, stop=True, skip_group_check=True)
                        if kt >= 2:
                            t.wait(ssx, ssx_base + 2 * (kt // 2))
                        t.mm(bk0, kf[0:64, c * SB + kt * 128:c * SB + (kt + 1) * 128],
                             qf[0:64, c * SB + qb * TSTRIP:c * SB + (qb + 1) * TSTRIP],
                             start=True, stop=True, tile_position=(0, 0),
                             skip_group_check=True,
                             inc=[(tsc, 1)] if kt % 2 == 1 else None)
                        t.mm(bk1, kf[64:128, c * SB + kt * 128:c * SB + (kt + 1) * 128],
                             qf[64:128, c * SB + qb * TSTRIP:c * SB + (qb + 1) * TSTRIP],
                             start=True, stop=True, tile_position=(64, 0),
                             skip_group_check=True,
                             inc=[(tsc, 1)] if kt % 2 == 1 else None)
                        if kt == 1 and pending is not None:
                            pending()
                            pending = None
                        if kt % 2 == 1:
                            s.wait(tsc, tsc.mark() - 1)
                            s.op("activation",
                                 scr[:, (kt - 1) * TSTRIP:(kt + 1) * TSTRIP],
                                 psSbig[:, 0:2 * TSTRIP], AF.Exp,
                                 bias=nb8[:, 0:1], scale=sc8[:, 0:1], inc=[(ssx, 1)])
                            s.wait(tsc)
                            s.op("activation",
                                 scr[:, (NTK + kt - 1) * TSTRIP:(NTK + kt + 1) * TSTRIP],
                                 psSbig[:, 2 * TSTRIP:4 * TSTRIP], AF.Exp,
                                 bias=nb8[:, 0:1], scale=sc8[:, 0:1], inc=[(ssx, 1)])
                            if PH >= 4 and kt >= 3:
                                if kt == 3:
                                    t.wait(vuv, vuv.mark())
                                jprev = kt // 2 - 1
                                for h2 in range(2):
                                    h = h0 + h2
                                    pts = scr[:, h2 * NTK * TSTRIP:(h2 + 1) * NTK * TSTRIP]
                                    for kk in (2 * jprev, 2 * jprev + 1):
                                        t.mm(psT[h2][0:VW, :],
                                             vt3[:, kk * NH1 + h, :],
                                             pts[:, kk * TSTRIP:(kk + 1) * TSTRIP],
                                             start=(kk == 0), stop=False,
                                             skip_group_check=True)
                    if PH >= 4:
                        t.wait(ssx)
                        for h2 in range(2):
                            h = h0 + h2
                            pts = scr[:, h2 * NTK * TSTRIP:(h2 + 1) * NTK * TSTRIP]
                            for kk in (NTK - 2, NTK - 1):
                                t.mm(psT[h2][0:VW, :], vt3[:, kk * NH1 + h, :],
                                     pts[:, kk * TSTRIP:(kk + 1) * TSTRIP],
                                     start=False, stop=(kk == NTK - 1),
                                     skip_group_check=True)
                            t.mark_last(tpv)
                        pending = make_normalize(h0)
                if pending is not None:
                    pending()
                    pending = None
                if PH < 5:
                    continue
                tsl = slice(base + qb * TSTRIP, base + (qb + 1) * TSTRIP)
                if qb == 0:
                    wopar = next_half()
                    wobuf = wbuf2[:, wopar * NHCH * H:(wopar + 1) * NHCH * H]
                    for sem_, val_ in half_guard[wopar]:
                        g.wait(sem_, val_)
                    for h in range(NHCH):
                        g.dma(wobuf[:, h * H:(h + 1) * H],
                              wo_dram[e * H + h * 128:e * H + (h + 1) * 128],
                              inc=[(wls[wopar], 16)])
                t.wait(vuv)
                t.wait(wls[wopar])
                for oc in range(NHCH):
                    p_ = pp[oc % 2]
                    if vpo.mark() >= 1:
                        t.wait(vpo, vpo.mark() - 1)
                    for h in range(NHCH):
                        t.mm(p_[:], wobuf[:, h * H + oc * 128:h * H + (oc + 1) * 128],
                             af[:, h * TSTRIP:(h + 1) * TSTRIP],
                             start=(h == 0), stop=(h == NHCH - 1))
                    t.mark_last(tpo)
                    v.wait(tpo)
                    v.wait(vgb)
                    v.wait(aws[oc % 2])
                    v.op("scalar_tensor_tensor",
                         dob[:, (oc % 2) * TSTRIP:(oc % 2) * TSTRIP + TSTRIP],
                         p_[:], bqb[:, 3 * NHCH + oc:3 * NHCH + oc + 1], gbb[:],
                         ALU.add, ALU.mult, inc=[(vpo, 1)])
                    g.wait(vpo)
                    if e == 0 and first_writes_acc:
                        g.dma(acc_dram[oc * 128:(oc + 1) * 128, tsl],
                              dob[:, (oc % 2) * TSTRIP:(oc % 2) * TSTRIP + TSTRIP],
                              inc=[(aws[oc % 2], 16)])
                    else:
                        g.dma(acc_dram[oc * 128:(oc + 1) * 128, tsl],
                              dob[:, (oc % 2) * TSTRIP:(oc % 2) * TSTRIP + TSTRIP],
                              accum_op=ALU.add, inc=[(aws[oc % 2], 16)])
            half_guard[wopar] = [(tpo, tpo.mark())]
    g.wait(aws[0])
    g.wait(aws[1])


def emit_l1attn(P, io, cb, xg, wqkv_dram, bqkv_dram, wo_dram, bo_dram, mask_dram,
                out_dram):
    t, v, s, g, sy = (P.e[n] for n in ENGINES)
    wl = P.sem("a1_wl")
    xls = [P.sem("a1_xl0"), P.sem("a1_xl1")]
    tpj = P.sem("a1_tpj")
    spj = P.sem("a1_spj")
    vsb = P.sem("a1_vsb")
    ttr = P.sem("a1_ttr")
    vtr = P.sem("a1_vtr")
    tsc = P.sem("a1_tsc")
    vsc = P.sem("a1_vsc")
    ssx = P.sem("a1_ssx")
    vnp = P.sem("a1_vnp")
    tpt = P.sem("a1_tpt")
    vpt = P.sem("a1_vpt")
    tpv = P.sem("a1_tpv")
    vaf = P.sem("a1_vaf")
    tpo = P.sem("a1_tpo")
    vpo = P.sem("a1_vpo")
    aws = [P.sem("a1_aw0"), P.sem("a1_aw1")]

    NB = B
    CS = 64
    NCHK = 256 // CS
    CT = NB * CS
    NGL = CS // 8

    wbuf = P.sbuf("a1_w", [128, NHCH * 3 * H], BF16)
    wob = P.sbuf("a1_wo", [128, NHCH * H], BF16)
    bqb = P.sbuf("a1_bq", [128, 4 * NHCH], F32)
    bqs = P.sbuf("a1_bqs", [128, NHCH], F32)
    maskb = P.sbuf("a1_mask", [128, 128], F32)
    sc8 = P.sbuf("a1_sc8", [128, 1], F32)
    xb = P.sbuf("a1_xb", [128, 2 * NHCH * CT], BF16)
    qf = P.sbuf("a1_qf", [128, NHCH * CT], BF16)
    kf = P.sbuf("a1_kf", [128, NHCH * CT], BF16)
    scr = P.sbuf("a1_scr", [128, NHCH * CT], BF16)
    vt = P.sbuf("a1_vt", [128, NGL * H], BF16)
    af = P.sbuf("a1_af", [128, NHCH * CT], BF16)
    snb = P.sbuf("a1_sn", [128, 128], F32)
    peb = P.sbuf("a1_pe", [128, 128], F32)
    pnb = P.sbuf("a1_pn", [128, 128], BF16)
    ptb = P.sbuf("a1_pt", [128, 128], BF16)
    rowz = P.sbuf("a1_rz", [128, 4], F32)
    dob = P.sbuf("a1_do", [128, 2 * TSTRIP], F32)
    pp = [P.psum("a1_pp0", [128, TSTRIP]), P.psum("a1_pp1", [128, TSTRIP])]
    psS = [P.psum("a1_ps0", [128, 128]), P.psum("a1_ps1", [128, 128])]
    psT = [P.psum("a1_pt0", [128, 128]), P.psum("a1_pt1", [128, 128])]
    psTb = [x[:].bitcast(BF16) for x in psT]

    ini = P.sem("a1_ini")
    v.op("memset", sc8[:], 0.125)
    v.flush()
    v.post_inc(ini)
    s.wait(ini)

    for h in range(NHCH):
        g.dma(wbuf[:, h * 3 * H:(h + 1) * 3 * H],
              wqkv_dram[h * 128:(h + 1) * 128, :], inc=[(wl, 16)])
        g.dma(wob[:, h * H:(h + 1) * H],
              wo_dram[h * 128:(h + 1) * 128, :], inc=[(wl, 16)])
    g.dma(bqb[:, 0:3 * NHCH], bqkv_dram[:], inc=[(wl, 16)])
    g.dma(bqb[:, 3 * NHCH:4 * NHCH], bo_dram[:], inc=[(wl, 16)])
    g.dma(maskb[:], mask_dram[:], inc=[(wl, 16)])
    v.wait(wl)
    v.op("tensor_scalar", bqs[:], bqb[:, 0:NHCH], 0.125, None, ALU.mult)
    v.flush()
    v.post_inc(vsb)
    t.wait(wl)
    s.wait(vsb)


    for cg in range(NCHK):
        cpar = cg % 2
        xs = xb[:, cpar * NHCH * CT:(cpar + 1) * NHCH * CT]
        if cg >= 2:
            sy.wait(tpj)
        for h in range(NHCH):
            src = xg[h * 128:(h + 1) * 128, :].rearrange("p (b s) -> p b s", b=NB)
            dst = xs[:, h * CT:(h + 1) * CT].rearrange("p (b s) -> p b s", b=NB)
            sy.dma(dst, src[:, :, cg * CS:(cg + 1) * CS], inc=[(xls[cpar], 16)])
        t.wait(xls[cpar])
        for pj in range(3):
            for oc in range(NHCH):
                for st_ in range(CT // TSTRIP):
                    p_ = pp[(oc * 2 + st_) % 2]
                    if spj.mark() >= 1:
                        t.wait(spj, spj.mark() - 1)
                    for h in range(NHCH):
                        t.mm(p_[:], wbuf[:, h * 3 * H + pj * H + oc * 128:
                                         h * 3 * H + pj * H + (oc + 1) * 128],
                             xs[:, h * CT + st_ * TSTRIP:h * CT + (st_ + 1) * TSTRIP],
                             start=(h == 0), stop=(h == NHCH - 1))
                    t.mark_last(tpj)
                    s.wait(tpj)
                    dst = (qf, kf, scr)[pj]
                    od4 = dst[:, oc * CT:(oc + 1) * CT].rearrange(
                        "p (gl b so) -> p b gl so", gl=NGL, b=NB)
                    odst = od4[:, 8 * st_:8 * (st_ + 1), :, :]
                    src4 = p_[:].rearrange("p (b gl so) -> p b gl so", b=8, gl=NGL)
                    if pj == 0:
                        s.op("activation", odst, src4, AF.Identity,
                             bias=bqs[:, oc:oc + 1], scale=sc8[:, 0:1],
                             inc=[(spj, 1)])
                    else:
                        s.op("activation", odst, src4, AF.Identity,
                             bias=bqb[:, pj * NHCH + oc:pj * NHCH + oc + 1],
                             inc=[(spj, 1)])
        t.wait(spj)
        for gl in range(NGL):
            for c in range(NHCH):
                j = gl * NHCH + c
                pst = psTb[j % 2]
                if j >= 2:
                    t.wait(vtr, vtr.mark() - 1)
                t.op("transpose", pst[:, 0:128],
                     scr[:, c * CT + gl * 128:c * CT + (gl + 1) * 128],
                     cb["identb"][:], inc=[(ttr, 1)])
                v.wait(ttr)
                v.op("tensor_copy", vt[:, gl * H + c * 128:gl * H + (c + 1) * 128],
                     pst[:, 0:128], inc=[(vtr, 1)])
        t.wait(vtr)
        for gl in range(NGL):
            for hp in range(NH1 // 2):
                c = hp
                qg = qf[:, c * CT + gl * 128:c * CT + (gl + 1) * 128]
                kg = kf[:, c * CT + gl * 128:c * CT + (gl + 1) * 128]
                t.wait(vsc)
                t.wait(vaf)
                t.mm(psS[0][:], qg[0:64], kg[0:64], start=True, stop=True,
                     tile_position=(0, 0))
                t.mm(psS[1][:], qg[64:128], kg[64:128], start=True, stop=True,
                     tile_position=(64, 0), inc=[(tsc, 1)])
                for h2 in range(2):
                    h = 2 * hp + h2
                    v.wait(tsc)
                    v.op("tensor_tensor", snb[:], psS[h2][:], maskb[:], ALU.add)
                    v.flush()
                    v.op("reduce_max", rowz[:, 0:1], snb[:], axis=AX.X)
                    v.flush()
                    v.op("tensor_scalar", rowz[:, 1:2], rowz[:, 0:1], -1.0, None,
                         ALU.mult, inc=[(vsc, 1)])
                    s.wait(vsc)
                    s.op("activation", peb[:], snb[:], AF.Exp, bias=rowz[:, 1:2],
                         accum_out=rowz[:, 2:3], inc=[(ssx, 1)])
                    v.wait(ssx)
                    v.op("reciprocal", rowz[:, 3:4], rowz[:, 2:3])
                    v.flush()
                    v.op("tensor_scalar", pnb[:], peb[:], rowz[:, 3:4], None,
                         ALU.mult)
                    v.flush()
                    v.post_inc(vnp)
                    t.wait(vnp)
                    t.wait(vpt)
                    t.op("transpose", psTb[h2][:, 0:128], pnb[:], cb["identb"][:],
                         inc=[(tpt, 1)])
                    v.wait(tpt)
                    v.op("tensor_copy", ptb[:], psTb[h2][:, 0:128], inc=[(vpt, 1)])
                    t.wait(vpt)
                    t.mm(psS[h2][0:64, :], vt[:, gl * H + h * 64:gl * H + (h + 1) * 64],
                         ptb[:], start=True, stop=True, inc=[(tpv, 1)])
                    v.wait(tpv)
                    v.op("tensor_copy",
                         af[(h % 2) * 64:(h % 2) * 64 + 64,
                            (h // 2) * CT + gl * 128:(h // 2) * CT + (gl + 1) * 128],
                         psS[h2][0:64, :], inc=[(vaf, 1)])
        t.wait(vaf)
        for st_ in range(CT // TSTRIP):
            for oc in range(NHCH):
                p_ = pp[(st_ * NHCH + oc) % 2]
                if vpo.mark() >= 1:
                    t.wait(vpo, vpo.mark() - 1)
                for h in range(NHCH):
                    t.mm(p_[:], wob[:, h * H + oc * 128:h * H + (oc + 1) * 128],
                         af[:, h * CT + st_ * TSTRIP:h * CT + (st_ + 1) * TSTRIP],
                         start=(h == 0), stop=(h == NHCH - 1))
                t.mark_last(tpo)
                v.wait(tpo)
                v.wait(aws[oc % 2])
                dsl = dob[:, (oc % 2) * TSTRIP:(oc % 2 + 1) * TSTRIP]
                d4 = dsl.rearrange("p (b gl so) -> p gl b so", b=NB, gl=4)
                s4 = p_[:].rearrange("p (gl b so) -> p gl b so", gl=4, b=NB)
                v.op("tensor_scalar", d4, s4,
                     bqb[:, 3 * NHCH + oc:3 * NHCH + oc + 1], None, ALU.add,
                     inc=[(vpo, 1)])
                sy.wait(vpo)
                od = out_dram[oc * 128:(oc + 1) * 128, :].rearrange(
                    "p (b s) -> p b s", b=NB)
                sy.dma(od[:, :, cg * CS + 32 * st_:cg * CS + 32 * (st_ + 1)],
                       dsl.rearrange("p (b s) -> p b s", b=NB),
                       inc=[(aws[oc % 2], 16)])
    sy.wait(aws[0])
    sy.wait(aws[1])



def _din(nc, io, name, shape, dt=F32, kind="ExternalInput"):
    io[name] = nc.dram_tensor(name, shape, dt, kind=kind).ap()


def _load_idents(P, cb, io, which=("identb", "identf")):
    g = P.e["gpsimd"]
    cl = P.sem("cl")
    n = 0
    if "identb" in which:
        cb["identb"] = P.sbuf("c_identb", [128, 128], BF16)
        g.dma(cb["identb"][:], io["ident"][:], inc=[(cl, 16)])
        n += 16
    if "identf" in which:
        cb["identf"] = P.sbuf("c_identf", [128, 128], F32)
        g.dma(cb["identf"][:], io["identf"][:], inc=[(cl, 16)])
        n += 16
    for nm in ENGINES:
        P.e[nm].wait(cl)


def prog_ln(with_acc):
    nc = bass.Bass("TRN2", target_bir_lowering=False)
    es = ExitStack()
    P = Prog(nc, es)
    io = {}
    _din(nc, io, "xin", [H, TC])
    if with_acc:
        _din(nc, io, "accin", [H, TC])
        _din(nc, io, "xout", [H, TC], kind="ExternalOutput")
    _din(nc, io, "lnout", [H, TC], BF16, kind="ExternalOutput")
    _din(nc, io, "lnoutf", [H, TC], kind="ExternalOutput")
    cb = {}
    with es:
        make_consts(P, cb)
        with P.scope():
            make_pool(P, cb)
            emit_ln(P, io, cb, io["xin"], io["lnout"],
                    acc=io.get("accin"), dst_x=io.get("xout"),
                    acc_scale=1.0, out_lnf=io["lnoutf"])
        P.barrier()
        P.replay()
    return nc


def prog_final():
    nc = bass.Bass("TRN2", target_bir_lowering=False)
    es = ExitStack()
    P = Prog(nc, es)
    io = {}
    _din(nc, io, "xin", [H, TC])
    _din(nc, io, "accin", [H, TC])
    _din(nc, io, "xout", [H, TC], kind="ExternalOutput")
    cb = {}
    with es:
        with P.scope():
            emit_final_add(P, io, cb, io["xin"], io["accin"], io["xout"])
        P.barrier()
        P.replay()
    return nc


def prog_l0f():
    nc = bass.Bass("TRN2", target_bir_lowering=False)
    es = ExitStack()
    P = Prog(nc, es)
    io = {}
    _din(nc, io, "lnout", [H, TC], BF16)
    _din(nc, io, "lnoutf", [H, TC])
    _din(nc, io, "l0f_wr", [H, 8])
    _din(nc, io, "l0f_wg", [E0 * H, I0])
    _din(nc, io, "l0f_wu", [E0 * H, I0])
    _din(nc, io, "l0f_wd", [E0 * I0, H])
    _din(nc, io, "identf", [128, 128])
    _din(nc, io, "acc", [H, TC], kind="ExternalOutput")
    ce = nc.dram_tensor("ce", [8, TC], F32, kind="Internal").ap()
    cb = {}
    with es:
        make_consts(P, cb)
        _load_idents(P, cb, io, which=("identf",))
        with P.scope():
            make_pool(P, cb)
            emit_l0f_router(P, io, cb, io["lnoutf"], io["l0f_wr"], ce, scale=0.5)
        P.barrier()
        with P.scope():
            make_pool(P, cb)
            emit_l0f_experts(P, io, cb, io["lnout"], ce, io["l0f_wg"],
                             io["l0f_wu"], io["l0f_wd"], io["acc"])
        P.barrier()
        P.replay()
    return nc


def prog_l0a():
    nc = bass.Bass("TRN2", target_bir_lowering=False)
    es = ExitStack()
    P = Prog(nc, es)
    io = {}
    _din(nc, io, "lnout", [H, TC], BF16)
    _din(nc, io, "l0a_wqk", [H, 2 * H])
    _din(nc, io, "l0a_wo", [H, H])
    _din(nc, io, "l0a_bo", [128, NHCH])
    _din(nc, io, "l0a_mask", [128, 2 * WIN])
    _din(nc, io, "ident", [128, 128], BF16)
    _din(nc, io, "acc", [H, TC], kind="ExternalOutput")
    cb = {}
    with es:
        make_consts(P, cb)
        _load_idents(P, cb, io, which=("identb",))
        with P.scope():
            emit_l0attn(P, io, cb, io["lnout"], io["l0a_wqk"], io["l0a_wo"],
                        io["l0a_bo"], io["l0a_mask"], io["acc"])
        P.barrier()
        P.replay()
    return nc


def prog_l1f():
    nc = bass.Bass("TRN2", target_bir_lowering=False)
    es = ExitStack()
    P = Prog(nc, es)
    io = {}
    _din(nc, io, "lnout", [H, TC], BF16)
    _din(nc, io, "lnoutf", [H, TC])
    _din(nc, io, "l1f_rW1", [H, RH])
    _din(nc, io, "l1f_rb1", [128, RH // 128])
    _din(nc, io, "l1f_lng", [128, RH // 128])
    _din(nc, io, "l1f_lnb", [128, RH // 128])
    _din(nc, io, "l1f_rW2", [RH, 8])
    _din(nc, io, "l1f_rb2", [1, 8])
    for nm in "qkvo":
        _din(nc, io, f"l1f_w{nm}", [E1 * H, H])
        _din(nc, io, f"l1f_b{nm}", [E1 * 128, NHCH])
    _din(nc, io, "identf", [128, 128])
    _din(nc, io, "ident", [128, 128], BF16)
    _din(nc, io, "acc", [H, TC], kind="ExternalOutput")
    ge = nc.dram_tensor("ge", [8, TC], F32, kind="Internal").ap()
    cb = {}
    with es:
        make_consts(P, cb)
        _load_idents(P, cb, io)
        with P.scope():
            make_pool(P, cb)
            emit_l1f_router(P, io, cb, io["lnoutf"], io["l1f_rW1"], io["l1f_rb1"],
                            io["l1f_lng"], io["l1f_lnb"], io["l1f_rW2"],
                            io["l1f_rb2"], ge, scale=0.5)
        P.barrier()
        with P.scope():
            emit_l1f_experts(P, io, cb, io["lnout"], ge,
                             io["l1f_wq"], io["l1f_wk"], io["l1f_wv"], io["l1f_wo"],
                             io["l1f_bq"], io["l1f_bk"], io["l1f_bv"], io["l1f_bo"],
                             io["acc"])
        P.barrier()
        P.replay()
    return nc


def prog_l1a():
    nc = bass.Bass("TRN2", target_bir_lowering=False)
    es = ExitStack()
    P = Prog(nc, es)
    io = {}
    _din(nc, io, "xg", [H, TC], BF16)
    _din(nc, io, "l1a_wqkv", [H, 3 * H])
    _din(nc, io, "l1a_bqkv", [128, 3 * NHCH])
    _din(nc, io, "l1a_wo", [H, H])
    _din(nc, io, "l1a_bo", [128, NHCH])
    _din(nc, io, "l1a_mask", [128, 128])
    _din(nc, io, "ident", [128, 128], BF16)
    _din(nc, io, "out_ss", [H, TC], kind="ExternalOutput")
    cb = {}
    with es:
        make_consts(P, cb)
        _load_idents(P, cb, io, which=("identb",))
        with P.scope():
            emit_l1attn(P, io, cb, io["xg"], io["l1a_wqkv"], io["l1a_bqkv"],
                        io["l1a_wo"], io["l1a_bo"], io["l1a_mask"], io["out_ss"])
        P.barrier()
        P.replay()
    return nc


_PROG_CACHE = {}


def _get_prog(name):
    if name not in _PROG_CACHE:
        _PROG_CACHE[name] = {
            "ln0": lambda: prog_ln(False),
            "ln1": lambda: prog_ln(True),
            "final": prog_final,
            "l0f": prog_l0f,
            "l0a": prog_l0a,
            "l1f": prog_l1f,
            "l1a": prog_l1a,
        }[name]()
    return _PROG_CACHE[name]


def _run(name, in_maps):
    nc = _get_prog(name)
    res = run_bass_kernel_spmd(nc, in_maps, core_ids=list(range(NCORES)))
    return res.results


def kernel_unfused(**inputs):
    w = prep_weights(inputs)
    xcs = prep_x_per_core(inputs["L0_attn_x"])
    ident = w["ident"]
    identf = w["identf"]

    wg2 = w["l0f_wg"].reshape(E0 * H, I0)
    wu2 = w["l0f_wu"].reshape(E0 * H, I0)
    wd2 = w["l0f_wd"].reshape(E0 * I0, H)
    l1w = {nm: w[f"l1f_w{nm}"].reshape(E1 * H, H) for nm in "qkvo"}
    l1b = {nm: w[f"l1f_b{nm}"].reshape(E1 * 128, NHCH) for nm in "qkvo"}

    def ln_first(xs):
        r = _run("ln0", [{"xin": x} for x in xs])
        return [c["lnout"] for c in r], [c["lnoutf"] for c in r]

    def ln_acc(xs, accs):
        r = _run("ln1", [{"xin": x, "accin": a} for x, a in zip(xs, accs)])
        return ([c["xout"] for c in r], [c["lnout"] for c in r],
                [c["lnoutf"] for c in r])

    def l0f(lns, lnfs):
        r = _run("l0f", [{"lnout": ln, "lnoutf": lnf, "l0f_wr": w["l0f_wr"],
                          "l0f_wg": wg2, "l0f_wu": wu2, "l0f_wd": wd2,
                          "identf": identf} for ln, lnf in zip(lns, lnfs)])
        return [c["acc"] for c in r]

    def l0a(lns):
        r = _run("l0a", [{"lnout": ln, "l0a_wqk": w["l0a_wqk"],
                          "l0a_wo": w["l0a_wo"], "l0a_bo": w["l0a_bo"],
                          "l0a_mask": w["l0a_mask"], "ident": ident}
                         for ln in lns])
        return [c["acc"] for c in r]

    def l1f(lns, lnfs):
        r = _run("l1f", [{"lnout": ln, "lnoutf": lnf,
                          "l1f_rW1": w["l1f_rW1"], "l1f_rb1": w["l1f_rb1"],
                          "l1f_lng": w["l1f_lng"], "l1f_lnb": w["l1f_lnb"],
                          "l1f_rW2": w["l1f_rW2"], "l1f_rb2": w["l1f_rb2"],
                          "l1f_wq": l1w["q"], "l1f_wk": l1w["k"],
                          "l1f_wv": l1w["v"], "l1f_wo": l1w["o"],
                          "l1f_bq": l1b["q"], "l1f_bk": l1b["k"],
                          "l1f_bv": l1b["v"], "l1f_bo": l1b["o"],
                          "identf": identf, "ident": ident}
                         for ln, lnf in zip(lns, lnfs)])
        return [c["acc"] for c in r]

    def l1a(lns):
        xgs = []
        for c in range(NCORES):
            parts = []
            for b in range(B):
                r_, bl = b // 2, b % 2
                parts.append(lns[r_][:, bl * S + 256 * c:bl * S + 256 * c + 256])
            xgs.append(np.ascontiguousarray(np.concatenate(parts, axis=1)))
        r = _run("l1a", [{"xg": xg, "l1a_wqkv": w["l1a_wqkv"],
                          "l1a_bqkv": w["l1a_bqkv"], "l1a_wo": w["l1a_wo"],
                          "l1a_bo": w["l1a_bo"], "l1a_mask": w["l1a_mask"],
                          "ident": ident} for xg in xgs])
        outs = [c["out_ss"] for c in r]
        accs = []
        for c in range(NCORES):
            acc = np.empty((H, TC), np.float32)
            for bl in range(BPC):
                for rp in range(NCORES):
                    acc[:, bl * S + 256 * rp:bl * S + 256 * (rp + 1)] = \
                        outs[rp][:, (2 * c + bl) * 256:(2 * c + bl + 1) * 256]
            accs.append(acc)
        return accs

    X = xcs
    lns, lnfs = ln_first(X)
    acc = l0f(lns, lnfs)
    X, lns, lnfs = ln_acc(X, acc)
    acc = l0a(lns)
    X, lns, lnfs = ln_acc(X, acc)
    acc = l0f(lns, lnfs)
    X, lns, lnfs = ln_acc(X, acc)
    acc = l1f(lns, lnfs)
    X, lns, lnfs = ln_acc(X, acc)
    acc = l1a(lns)
    X, lns, lnfs = ln_acc(X, acc)
    acc = l1f(lns, lnfs)
    r = _run("final", [{"xin": x, "accin": a} for x, a in zip(X, acc)])
    Xf = [c["xout"] for c in r]

    out = np.empty((B, S, H), np.float32)
    for c in range(NCORES):
        out[c * BPC:(c + 1) * BPC] = Xf[c].reshape(H, BPC, S).transpose(1, 2, 0)
    return out


def emit_l1a_exchange_in(P, io, cb, lnout, cc1_out, tmp1, xg, regs):
    g = P.e["gpsimd"]
    ccs = P.sem("x1_cc")
    gd = P.sem("x1_gd")
    gx = P.sem("x1_gx")
    rc, rof, r2 = regs
    g.op("collective_compute", "AllGather", ALU.bypass,
         replica_groups=[list(range(NCORES))],
         ins=[lnout[:]], outs=[cc1_out[:]], inc=[(ccs, 1)])
    g.wait(ccs)
    g.op("reg_mul", rof, rc, 256)
    t_cc1 = cc1_out.tensor
    for bl in range(BPC):
        g.op("reg_add", r2, rof, bl * S)
        src = bass.AP(t_cc1, r2, [[TC, NCORES * H], [1, 256]])
        g.dma(tmp1[:, bl * 256:(bl + 1) * 256], src, inc=[(gd, 16)])
    g.wait(gd)
    for r in range(NCORES):
        for bl in range(BPC):
            g.dma(xg[:, (2 * r + bl) * 256:(2 * r + bl + 1) * 256],
                  tmp1[r * H:(r + 1) * H, bl * 256:(bl + 1) * 256],
                  inc=[(gx, 16)])
    g.wait(gx)


def emit_l1a_exchange_out(P, io, cb, out_ss, cc2_out, tmp2, acc, regs):
    g = P.e["gpsimd"]
    ccs = P.sem("x2_cc")
    gd = P.sem("x2_gd")
    gs2 = P.sem("x2_gs")
    rc, rof, r2 = regs
    g.op("collective_compute", "AllGather", ALU.bypass,
         replica_groups=[list(range(NCORES))],
         ins=[out_ss[:]], outs=[cc2_out[:]], inc=[(ccs, 1)])
    g.wait(ccs)
    g.op("reg_mul", rof, rc, 512)
    t_cc2 = cc2_out.tensor
    for bl in range(BPC):
        g.op("reg_add", r2, rof, bl * 256)
        src = bass.AP(t_cc2, r2, [[TC, NCORES * H], [1, 256]])
        g.dma(tmp2[:, bl * 256:(bl + 1) * 256], src, inc=[(gd, 16)])
    g.wait(gd)
    for rp in range(NCORES):
        for bl in range(BPC):
            g.dma(acc[:, bl * S + 256 * rp:bl * S + 256 * (rp + 1)],
                  tmp2[rp * H:(rp + 1) * H, bl * 256:(bl + 1) * 256],
                  inc=[(gs2, 16)])
    g.wait(gs2)


def build_fused(dbg=False):
    from concourse.replica_groups import maybe_share_collective_output_space
    nc = bass.Bass("TRN2", target_bir_lowering=False)
    es = ExitStack()
    P = Prog(nc, es)
    io = {}
    _din(nc, io, "xin", [H, TC])
    _din(nc, io, "coreid", [1, 1], mybir.dt.int32)
    _din(nc, io, "ident", [128, 128], BF16)
    _din(nc, io, "identf", [128, 128])
    _din(nc, io, "l0f_wr", [H, 8])
    _din(nc, io, "l0f_wg", [E0 * H, I0])
    _din(nc, io, "l0f_wu", [E0 * H, I0])
    _din(nc, io, "l0f_wd", [E0 * I0, H])
    _din(nc, io, "l0a_wqk", [H, 2 * H])
    _din(nc, io, "l0a_wo", [H, H])
    _din(nc, io, "l0a_bo", [128, NHCH])
    _din(nc, io, "l0a_mask", [128, 2 * WIN])
    _din(nc, io, "l1f_rW1", [H, RH])
    _din(nc, io, "l1f_rb1", [128, RH // 128])
    _din(nc, io, "l1f_lng", [128, RH // 128])
    _din(nc, io, "l1f_lnb", [128, RH // 128])
    _din(nc, io, "l1f_rW2", [RH, 8])
    _din(nc, io, "l1f_rb2", [1, 8])
    for nm in "qkvo":
        _din(nc, io, f"l1f_w{nm}", [E1 * H, H])
        _din(nc, io, f"l1f_b{nm}", [E1 * 128, NHCH])
    _din(nc, io, "l1a_wqkv", [H, 3 * H])
    _din(nc, io, "l1a_bqkv", [128, 3 * NHCH])
    _din(nc, io, "l1a_wo", [H, H])
    _din(nc, io, "l1a_bo", [128, NHCH])
    _din(nc, io, "l1a_mask", [128, 128])
    _din(nc, io, "xout", [H, TC], kind="ExternalOutput")

    X = nc.dram_tensor("X", [H, TC], F32, kind="Internal").ap()
    ACC = nc.dram_tensor("ACC", [H, TC], F32, kind="Internal").ap()
    LNO = nc.dram_tensor("LNO", [H, TC], BF16, kind="Internal").ap()
    LNF = nc.dram_tensor("LNF", [H, TC], F32, kind="Internal").ap()
    CE = nc.dram_tensor("CEg", [8, TC], F32, kind="Internal").ap()
    GE = nc.dram_tensor("GEg", [8, TC], F32, kind="Internal").ap()
    XG = nc.dram_tensor("XGg", [H, TC], BF16, kind="Internal").ap()
    OSS = nc.dram_tensor("OSS", [H, TC], F32, kind="Internal").ap()
    adsp = maybe_share_collective_output_space("AllGather", [list(range(NCORES))])
    CC1 = nc.dram_tensor("CC1", [NCORES * H, TC], BF16, kind="Internal",
                         addr_space=adsp).ap()
    CC2 = nc.dram_tensor("CC2", [NCORES * H, TC], F32, kind="Internal",
                         addr_space=adsp).ap()
    TMP1 = nc.dram_tensor("TMP1", [NCORES * H, 512], BF16, kind="Internal").ap()
    TMP2 = nc.dram_tensor("TMP2", [NCORES * H, 512], F32, kind="Internal").ap()

    cb = {}
    dbgn = [0]
    with es:
        def snap(src):
            if not dbg:
                return
            nm = f"dbg{dbgn[0]}"
            dbgn[0] += 1
            _din(nc, io, nm, [H, TC], kind="ExternalOutput")
            with P.scope():
                emit_copy_x(P, io, {}, src, io[nm])
            P.barrier()

        rc = es.enter_context(nc.gpsimd.register("rc"))
        rof = es.enter_context(nc.gpsimd.register("rof"))
        r2 = es.enter_context(nc.gpsimd.register("r2"))
        regs = (rc, rof, r2)
        make_consts(P, cb)
        _load_idents(P, cb, io)
        cidb = P.sbuf("c_cid", [1, 1], mybir.dt.int32)
        g = P.e["gpsimd"]
        cl2 = P.sem("cl2")
        g.dma(cidb[:], io["coreid"][:], inc=[(cl2, 16)])
        g.wait(cl2)
        g.op("reg_load", rc, cidb[0:1, 0:1])

        def ln(src, acc, dst, lnf=True):
            with P.scope():
                make_pool(P, cb)
                emit_ln(P, io, cb, src, LNO, acc=acc, dst_x=dst, acc_scale=1.0,
                        out_lnf=LNF if lnf else None)
            P.barrier()

        def l0f():
            with P.scope():
                make_pool(P, cb)
                emit_l0f_router(P, io, cb, LNF, io["l0f_wr"], CE, scale=0.5)
            P.barrier()
            with P.scope():
                make_pool(P, cb)
                emit_l0f_experts(P, io, cb, LNO, CE, io["l0f_wg"], io["l0f_wu"],
                                 io["l0f_wd"], ACC)
            P.barrier()

        def l1f():
            with P.scope():
                make_pool(P, cb)
                emit_l1f_router(P, io, cb, LNF, io["l1f_rW1"], io["l1f_rb1"],
                                io["l1f_lng"], io["l1f_lnb"], io["l1f_rW2"],
                                io["l1f_rb2"], GE, scale=0.5)
            P.barrier()
            with P.scope():
                emit_l1f_experts(P, io, cb, LNO, GE,
                                 io["l1f_wq"], io["l1f_wk"], io["l1f_wv"],
                                 io["l1f_wo"], io["l1f_bq"], io["l1f_bk"],
                                 io["l1f_bv"], io["l1f_bo"], ACC)
            P.barrier()

        ln(io["xin"], None, None)
        l0f()
        snap(ACC)
        ln(io["xin"], ACC, X)
        snap(X)
        with P.scope():
            emit_l0attn(P, io, cb, LNO, io["l0a_wqk"], io["l0a_wo"],
                        io["l0a_bo"], io["l0a_mask"], ACC)
        P.barrier()
        snap(ACC)
        ln(X, ACC, X)
        snap(X)
        l0f()
        snap(ACC)
        ln(X, ACC, X)
        snap(X)
        l1f()
        snap(ACC)
        ln(X, ACC, X)
        snap(X)
        ln_dummy = None
        with P.scope():
            emit_l1a_exchange_in(P, io, cb, LNO, CC1, TMP1, XG, regs)
        P.barrier()
        with P.scope():
            emit_l1attn(P, io, cb, XG, io["l1a_wqkv"], io["l1a_bqkv"],
                        io["l1a_wo"], io["l1a_bo"], io["l1a_mask"], OSS)
        P.barrier()
        with P.scope():
            emit_l1a_exchange_out(P, io, cb, OSS, CC2, TMP2, ACC, regs)
        P.barrier()
        snap(ACC)
        ln(X, ACC, X)
        snap(X)
        l1f()
        snap(ACC)
        with P.scope():
            emit_final_add(P, io, cb, X, ACC, io["xout"])
        P.barrier()
        P.replay()
    return nc


def kernel_fused(**inputs):
    w = prep_weights(inputs)
    xcs = prep_x_per_core(inputs["L0_attn_x"])
    wg2 = w["l0f_wg"].reshape(E0 * H, I0)
    wu2 = w["l0f_wu"].reshape(E0 * H, I0)
    wd2 = w["l0f_wd"].reshape(E0 * I0, H)
    base = {"ident": w["ident"], "identf": w["identf"],
            "l0f_wr": w["l0f_wr"], "l0f_wg": wg2, "l0f_wu": wu2, "l0f_wd": wd2,
            "l0a_wqk": w["l0a_wqk"], "l0a_wo": w["l0a_wo"], "l0a_bo": w["l0a_bo"],
            "l0a_mask": w["l0a_mask"],
            "l1f_rW1": w["l1f_rW1"], "l1f_rb1": w["l1f_rb1"],
            "l1f_lng": w["l1f_lng"], "l1f_lnb": w["l1f_lnb"],
            "l1f_rW2": w["l1f_rW2"], "l1f_rb2": w["l1f_rb2"],
            "l1a_wqkv": w["l1a_wqkv"], "l1a_bqkv": w["l1a_bqkv"],
            "l1a_wo": w["l1a_wo"], "l1a_bo": w["l1a_bo"],
            "l1a_mask": w["l1a_mask"]}
    for nm in "qkvo":
        base[f"l1f_w{nm}"] = w[f"l1f_w{nm}"].reshape(E1 * H, H)
        base[f"l1f_b{nm}"] = w[f"l1f_b{nm}"].reshape(E1 * 128, NHCH)
    if "fused" not in _PROG_CACHE:
        _PROG_CACHE["fused"] = build_fused()
    nc = _PROG_CACHE["fused"]
    in_maps = [{**base, "xin": xcs[c],
                "coreid": np.array([[c]], np.int32)} for c in range(NCORES)]
    res = run_bass_kernel_spmd(nc, in_maps, core_ids=list(range(NCORES)))
    out = np.empty((B, S, H), np.float32)
    for c in range(NCORES):
        xf = res.results[c]["xout"]
        out[c * BPC:(c + 1) * BPC] = xf.reshape(H, BPC, S).transpose(1, 2, 0)
    return out


kernel = kernel_fused


# version 14
# speedup vs baseline: 1.2333x; 1.2333x over previous
import numpy as np
import ml_dtypes
from contextlib import ExitStack
import concourse.bass as bass
import concourse.mybir as mybir
from concourse.bass_utils import run_bass_kernel_spmd

F32 = mybir.dt.float32
BF16 = mybir.dt.bfloat16
AF = mybir.ActivationFunctionType
ALU = mybir.AluOpType
AX = mybir.AxisListType
BF16NP = ml_dtypes.bfloat16

NCORES = 8
B, S, H = 16, 2048, 1024
BPC = B // NCORES
TC = BPC * S
NHCH = H // 128
TSTRIP = 512
NSTRIP = TC // TSTRIP

E0, I0 = 8, 2816
NI0 = I0 // 128
WIN = 128
NH1, DH1 = 16, 64
E1 = 8
RH = 512

ENGINES = ("tensor", "vector", "scalar", "gpsimd", "sync")


class Sem:
    __slots__ = ("name", "h", "n")

    def __init__(self, name, h):
        self.name = name
        self.h = h
        self.n = 0

    def mark(self):
        return self.n


class Eng:

    def __init__(self, prog, name):
        self.P = prog
        self.name = name
        self.ops = []
        self.dsem = None

    def _rec(self, fn, incs=None):
        incs = incs or []
        for sem, k in incs:
            sem.n += k

        def run(e, st):
            inst = fn(e)
            for sem, k in incs:
                inst.then_inc(sem.h, k)
            st["last"][self.name] = inst
        self.ops.append(run)
        return self

    def wait(self, sem, val=None):
        v = sem.n if val is None else val
        self.ops.append(lambda e, st: e.wait_ge(sem.h, v))
        return self

    def flush(self, inc=None):
        if self.dsem is None:
            self.dsem = self.P.sem(f"_d_{self.name}")
        self.dsem.n += 1
        v = self.dsem.n
        sem = self.dsem
        name = self.name

        def run(e, st):
            li = st["last"].get(name)
            assert li is not None, f"flush with no prior inst on {name}"
            li.then_inc(sem.h, 1)
            e.wait_ge(sem.h, v)
        self.ops.append(run)
        return self

    def op(self, method, *args, inc=None, **kwargs):
        import traceback
        stk = traceback.extract_stack()
        site = next((f for f in reversed(stk[:-1])
                     if 'kernel.py' in f.filename and f.lineno > 300), stk[-2])
        def fn(e):
            try:
                return getattr(e, method)(*args, **kwargs)
            except Exception as ex:
                import traceback as tb
                tb.print_exc()
                raise RuntimeError(
                    f"op {method} on {self.name} failed (recorded at "
                    f"{site.filename}:{site.lineno} {site.line}): {ex}") from ex
        return self._rec(fn, incs=inc)

    def mark_last(self, sem, k=1):
        sem.n += k
        name = self.name

        def run(e, st):
            st["last"][name].then_inc(sem.h, k)
        self.ops.append(run)
        return self

    def post_inc(self, sem, k=1):
        sem.n += k

        def run(e, st):
            e.sem_inc(sem.h, k)
        self.ops.append(run)
        return self

    def dma(self, out, in_, inc=None, **kw):
        return self.op("dma_start", out=out, in_=in_, inc=inc, **kw)

    def mm(self, out, lhsT, rhs, start=True, stop=True, inc=None, **kw):
        return self.op("matmul", out, lhsT, rhs, start=start, stop=stop, inc=inc, **kw)


class Prog:
    def __init__(self, nc, es):
        self.nc = nc
        self.es = es
        self._scopes = [es]
        self.e = {n: Eng(self, n) for n in ENGINES}
        self._sems = {}
        self._free_sems = []
        self._scope_sems = [[]]
        self._uid = 0

    def __getattr__(self, name):
        if name in ENGINES:
            return self.e[name]
        raise AttributeError(name)

    def sem(self, name):
        if name not in self._sems:
            glob = name.startswith("_")
            if not glob and self._free_sems and globals().get("SEM_RECYCLE", True):
                so = self._free_sems.pop()
                so.name = name
            else:
                self._uid += 1
                h = self.es.enter_context(self.nc.semaphore(f"{name}_{self._uid}"))
                so = Sem(name, h)
            self._sems[name] = so
            if not glob:
                self._scope_sems[-1].append(name)
        return self._sems[name]

    def sbuf(self, name, shape, dtype):
        self._uid += 1
        return self._scopes[-1].enter_context(
            self.nc.sbuf_tensor(f"{name}_{self._uid}", shape, dtype))

    def psum(self, name, shape, dtype=F32):
        self._uid += 1
        return self._scopes[-1].enter_context(
            self.nc.psum_tensor(f"{name}_{self._uid}", shape, dtype))

    from contextlib import contextmanager as _ctxm

    @_ctxm
    def scope(self):
        es = ExitStack()
        self._scopes.append(es)
        self._scope_sems.append([])
        try:
            with es:
                yield
        finally:
            self._scopes.pop()
            for name in self._scope_sems.pop():
                so = self._sems.pop(name, None)
                if so is not None:
                    self._free_sems.append(so)

    def barrier(self):
        bsem = self.sem("_barrier")
        bsem.n += len(ENGINES)
        v = bsem.n
        for name in ENGINES:
            def run(e, st, h=bsem.h, v=v):
                e.sem_inc(h, 1)
                e.wait_ge(h, v)
            self.e[name].ops.append(run)

    def replay(self):
        st = {"last": {}}
        with self.nc.Block() as block:
            for name in ENGINES:
                def make(ops):
                    def runner(eng):
                        for f in ops:
                            f(eng, st)
                    return runner
                getattr(block, name)(make(self.e[name].ops))



def _kmajor(w):
    return np.ascontiguousarray(np.asarray(w).T).astype(np.float32, copy=False)


def prep_weights(inp):
    w = {}
    w["l0a_wqk"] = np.ascontiguousarray(np.asarray(inp["L0_attn_w_qkv"])[:, 0:2 * H])
    w["l0a_wo"] = np.asarray(inp["L0_attn_w_out"])
    w["l0a_bo"] = np.ascontiguousarray(np.asarray(inp["L0_attn_b_out"]).reshape(NHCH, 128).T)
    w["l0f_wr"] = _kmajor(inp["L0_ffn_w_router"])
    w["l0f_wg"] = np.ascontiguousarray(np.asarray(inp["L0_ffn_w_gate"]).transpose(0, 2, 1))
    w["l0f_wu"] = np.ascontiguousarray(np.asarray(inp["L0_ffn_w_up"]).transpose(0, 2, 1))
    w["l0f_wd"] = np.ascontiguousarray(np.asarray(inp["L0_ffn_w_down"]).transpose(0, 2, 1))
    qkv = np.asarray(inp["L1_attn_qkv_w"])
    w["l1a_wqkv"] = np.ascontiguousarray(np.concatenate(
        [qkv[0:H].T, qkv[H:2 * H].T, qkv[2 * H:3 * H].T], axis=1)).astype(np.float32)
    w["l1a_bqkv"] = np.ascontiguousarray(np.asarray(inp["L1_attn_qkv_b"]).reshape(3 * NHCH, 128).T)
    w["l1a_wo"] = _kmajor(inp["L1_attn_out_w"])
    w["l1a_bo"] = np.ascontiguousarray(np.asarray(inp["L1_attn_out_b"]).reshape(NHCH, 128).T)
    w["l1f_rW1"] = _kmajor(inp["L1_ffn_rW1"])
    w["l1f_rb1"] = np.ascontiguousarray(np.asarray(inp["L1_ffn_rb1"]).reshape(RH // 128, 128).T)
    w["l1f_lng"] = np.ascontiguousarray(np.asarray(inp["L1_ffn_ln_g"]).reshape(RH // 128, 128).T)
    w["l1f_lnb"] = np.ascontiguousarray(np.asarray(inp["L1_ffn_ln_b"]).reshape(RH // 128, 128).T)
    w["l1f_rW2"] = _kmajor(inp["L1_ffn_rW2"])
    w["l1f_rb2"] = np.asarray(inp["L1_ffn_rb2"]).reshape(1, 8)
    for nm, wkey, bkey in (("q", "qW", "qb"), ("k", "kW", "kb"),
                           ("v", "vW", "vb"), ("o", "oW", "ob")):
        w[f"l1f_w{nm}"] = np.ascontiguousarray(
            np.asarray(inp[f"L1_ffn_{wkey}"]).transpose(0, 2, 1))
        w[f"l1f_b{nm}"] = np.ascontiguousarray(np.asarray(inp[f"L1_ffn_{bkey}"]).reshape(E1, NHCH, 128).transpose(0, 2, 1))
    w["ident"] = np.eye(128, dtype=BF16NP)
    w["identf"] = np.eye(128, dtype=np.float32)
    w["ones1"] = np.ones((1, 128), dtype=BF16NP)
    i = np.arange(WIN)[:, None]
    j = np.arange(2 * WIN)[None, :]
    w["l0a_mask"] = np.where(j > i + WIN, -3.0e38, 0.0).astype(np.float32)
    bi = np.arange(128) // 8
    si = np.arange(128) % 8
    valid = (si[:, None] == si[None, :]) & (bi[None, :] <= bi[:, None])
    w["l1a_mask"] = np.where(valid, 0.0, -10000.0).astype(np.float32)
    return w


def prep_x_per_core(x):
    x = np.asarray(x)
    out = []
    for c in range(NCORES):
        xc = np.ascontiguousarray(
            x[c * BPC:(c + 1) * BPC].transpose(2, 0, 1).reshape(H, TC))
        out.append(xc.astype(np.float32, copy=False))
    return out



def emit_load_consts(P, io, bufs):
    g = P.gpsimd
    ld = P.sem("cld")
    bufs["ident"] = P.sbuf("c_ident", [128, 128], BF16)
    bufs["ones1"] = P.sbuf("c_ones1", [1, 128], BF16)
    g.dma(bufs["ident"][:], io["ident"][:], inc=[(ld, 16)])
    g.dma(bufs["ones1"][:], io["ones1"][:], inc=[(ld, 16)])
    for name in ENGINES:
        P.e[name].wait(ld)






def make_consts(P, bufs):
    v = P.e["vector"]
    bufs["ones1f"] = P.sbuf("c_ones1f", [1, 128], F32)
    bufs["onescf"] = P.sbuf("c_onescf", [128, 1], F32)
    bufs["ones1b"] = P.sbuf("c_ones1b", [1, 128], BF16)
    v.op("memset", bufs["ones1f"][:], 1.0)
    v.op("memset", bufs["onescf"][:], 1.0)
    v.op("memset", bufs["ones1b"][:], 1.0)
    v.flush()


def make_pool(P, bufs):
    bufs["pool"] = [P.psum(f"pool{i}", [128, TSTRIP]) for i in range(6)]
    bufs["poolr"] = [P.psum(f"poolr{i}", [1, TSTRIP]) for i in range(2)]


def emit_ln(P, io, bufs, src_x, out_ln, acc=None, dst_x=None, acc_scale=1.0,
            out_lnf=None):
    t, v, s, g, sy = (P.e[n] for n in ENGINES)
    lds = [P.sem("ln_ld0"), P.sem("ln_ld1")]
    stws = [P.sem("ln_stw0"), P.sem("ln_stw1")]
    vx = P.sem("ln_vx")
    sq_s = P.sem("ln_sq")
    tsq = P.sem("ln_tsq")
    tst = P.sem("ln_tst")
    vrow = P.sem("ln_vrow")
    vst = P.sem("ln_vst")
    sst = P.sem("ln_sst")
    tbc = P.sem("ln_tbc")
    vout = P.sem("ln_vout")

    CW = NHCH * TSTRIP
    xt = bufs.setdefault("ln_xt", P.sbuf("ln_xt", [128, 2 * CW], F32))
    at = bufs.setdefault("ln_at", P.sbuf("ln_at", [128, 2 * CW], F32))
    sqb = bufs.setdefault("ln_sqb", P.sbuf("ln_sqb", [128, 2 * TSTRIP], F32))
    tmpb = bufs.setdefault("ln_tmpb", P.sbuf("ln_tmpb", [128, TSTRIP], F32))
    tmp2b = bufs.setdefault("ln_tmp2b", P.sbuf("ln_tmp2b", [128, 2 * NHCH * TSTRIP], F32)) \
        if out_lnf is not None else None
    strow = bufs.setdefault("ln_strow", P.sbuf("ln_strow", [1, 4 * TSTRIP], F32))
    ot = bufs.setdefault("ln_ot", P.sbuf("ln_ot", [128, 2 * CW], BF16))
    psum_r, pssq_r = bufs["poolr"][0], bufs["poolr"][1]
    psA, psB = bufs["pool"][0], bufs["pool"][1]
    onesf = bufs["ones1f"]
    onescf = bufs["onescf"]

    vout_marks = {}
    stw_marks = {}
    for it in range(NSTRIP):
        par = it % 2
        xs = xt[:, par * CW:(par + 1) * CW]
        as_ = at[:, par * CW:(par + 1) * CW]
        os_ = ot[:, par * CW:(par + 1) * CW]
        tsl = slice(it * TSTRIP, (it + 1) * TSTRIP)
        r_mu = strow[0:1, 0 * TSTRIP:1 * TSTRIP]
        r_v1 = strow[0:1, 1 * TSTRIP:2 * TSTRIP]
        r_v2 = strow[0:1, 2 * TSTRIP:3 * TSTRIP]

        if it >= 2:
            sy.wait(vout, vout_marks[it - 2])
            sy.wait(stws[par])
        for h in range(NHCH):
            sy.dma(xs[:, h * TSTRIP:(h + 1) * TSTRIP],
                   src_x[h * 128:(h + 1) * 128, tsl], inc=[(lds[par], 16)])
        if acc is not None:
            for h in range(NHCH):
                sy.dma(as_[:, h * TSTRIP:(h + 1) * TSTRIP],
                       acc[h * 128:(h + 1) * 128, tsl], inc=[(lds[par], 16)])
            v.wait(lds[par])
            for h in range(NHCH):
                sl = slice(h * TSTRIP, (h + 1) * TSTRIP)
                if acc_scale == 1.0:
                    v.op("tensor_tensor", xs[:, sl], xs[:, sl], as_[:, sl], ALU.add)
                else:
                    v.op("scalar_tensor_tensor", xs[:, sl], as_[:, sl], acc_scale,
                         xs[:, sl], ALU.mult, ALU.add)
            v.flush()
            v.post_inc(vx)
            if dst_x is not None:
                sy.wait(vx)
                for h in range(NHCH):
                    sl = slice(h * TSTRIP, (h + 1) * TSTRIP)
                    sy.dma(dst_x[h * 128:(h + 1) * 128, tsl], xs[:, sl],
                           inc=[(stws[par], 16)])
            t.wait(vx)
            s.wait(vx)
        else:
            t.wait(lds[par])
            s.wait(lds[par])
        if it >= 1:
            t.wait(vrow)
        for h in range(NHCH):
            sl = slice(h * TSTRIP, (h + 1) * TSTRIP)
            t.mm(psum_r[:], onescf[:, 0:1], xs[:, sl],
                 start=(h == 0), stop=(h == NHCH - 1))
        tsq_base = tsq.mark()
        for h in range(NHCH):
            sl = slice(h * TSTRIP, (h + 1) * TSTRIP)
            sqs = sqb[:, (h % 2) * TSTRIP:(h % 2 + 1) * TSTRIP]
            if h >= 2:
                s.wait(tsq, tsq_base + h - 1)
            s.op("activation", sqs, xs[:, sl], AF.Square, inc=[(sq_s, 1)])
            t.wait(sq_s)
            t.mm(pssq_r[:], onescf[:, 0:1], sqs,
                 start=(h == 0), stop=(h == NHCH - 1), inc=[(tsq, 1)])
        v.wait(tsq)
        v.op("tensor_scalar", r_mu, psum_r[:], 1.0 / H, None, ALU.mult)
        v.op("tensor_scalar", r_v1, pssq_r[:], 1.0 / H, None, ALU.mult)
        v.flush()
        v.post_inc(vrow)
        v.op("tensor_tensor", r_v2, r_mu, r_mu, ALU.mult)
        v.flush()
        v.op("tensor_tensor", r_v1, r_v1, r_v2, ALU.subtract)
        v.flush()
        v.op("tensor_scalar", r_v1, r_v1, 1e-5, None, ALU.add, inc=[(vst, 1)])
        s.wait(vst)
        s.op("activation", r_v2, r_v1, AF.Sqrt, inc=[(sst, 1)])
        v.wait(sst)
        v.op("reciprocal", r_v1, r_v2)
        v.flush()
        v.op("tensor_tensor", r_v2, r_mu, r_v1, ALU.mult)
        v.flush()
        v.op("tensor_scalar", r_v2, r_v2, -1.0, None, ALU.mult)
        v.flush()
        v.post_inc(vst)
        t.wait(vst)
        if it >= 2:
            t.wait(vout, vout_marks[it - 2])
        t.mm(psA[:], onesf[0:1, :], r_v1, start=True, stop=True)
        t.mm(psB[:], onesf[0:1, :], r_v2, start=True, stop=True)
        t.mark_last(tbc)
        v.wait(tbc)
        for h in range(NHCH):
            sl = slice(h * TSTRIP, (h + 1) * TSTRIP)
            v.op("tensor_tensor", tmpb[:], xs[:, sl], psA[:], ALU.mult)
            v.flush()
            if out_lnf is None:
                v.op("tensor_tensor", os_[:, sl], tmpb[:], psB[:], ALU.add)
                v.flush()
            else:
                f_ = tmp2b[:, par * NHCH * TSTRIP:(par + 1) * NHCH * TSTRIP]
                v.op("tensor_tensor", f_[:, sl], tmpb[:], psB[:], ALU.add)
                v.flush()
                v.op("tensor_copy", os_[:, sl], f_[:, sl])
                v.flush()
        v.post_inc(vout)
        vout_marks[it] = vout.mark()
        sy.wait(vout)
        for h in range(NHCH):
            sl = slice(h * TSTRIP, (h + 1) * TSTRIP)
            sy.dma(out_ln[h * 128:(h + 1) * 128, tsl], os_[:, sl], inc=[(stws[par], 16)])
            if out_lnf is not None:
                f_ = tmp2b[:, par * NHCH * TSTRIP:(par + 1) * NHCH * TSTRIP]
                sy.dma(out_lnf[h * 128:(h + 1) * 128, tsl], f_[:, sl],
                       inc=[(stws[par], 16)])
    sy.wait(stws[0])
    sy.wait(stws[1])


def emit_copy_x(P, io, bufs, src, dst):
    sy = P.e["sync"]
    lds = [P.sem("cp_ld0"), P.sem("cp_ld1")]
    stws = [P.sem("cp_stw0"), P.sem("cp_stw1")]
    ct = bufs.setdefault("cp_t", P.sbuf("cp_t", [128, 2 * NHCH * TSTRIP], F32))
    CW = NHCH * TSTRIP
    for it in range(NSTRIP):
        par = it % 2
        tsl = slice(it * TSTRIP, (it + 1) * TSTRIP)
        if it >= 2:
            sy.wait(stws[par])
        for h in range(NHCH):
            sl = slice(par * CW + h * TSTRIP, par * CW + (h + 1) * TSTRIP)
            sy.dma(ct[:, sl], src[h * 128:(h + 1) * 128, tsl], inc=[(lds[par], 16)])
        sy.wait(lds[par])
        for h in range(NHCH):
            sl = slice(par * CW + h * TSTRIP, par * CW + (h + 1) * TSTRIP)
            sy.dma(dst[h * 128:(h + 1) * 128, tsl], ct[:, sl], inc=[(stws[par], 16)])
    sy.wait(stws[0])
    sy.wait(stws[1])


def emit_final_add(P, io, bufs, src_x, acc, out, acc_scale=1.0):
    t, v, s, g, sy = (P.e[n] for n in ENGINES)
    lds = [P.sem("fa_ld0"), P.sem("fa_ld1")]
    stws = [P.sem("fa_stw0"), P.sem("fa_stw1")]
    vx = P.sem("fa_vx")
    CW = NHCH * TSTRIP
    xt = bufs.setdefault("ln_xt", P.sbuf("ln_xt", [128, 2 * CW], F32))
    at = bufs.setdefault("ln_at", P.sbuf("ln_at", [128, 2 * CW], F32))
    for it in range(NSTRIP):
        par = it % 2
        xs = xt[:, par * CW:(par + 1) * CW]
        as_ = at[:, par * CW:(par + 1) * CW]
        tsl = slice(it * TSTRIP, (it + 1) * TSTRIP)
        if it >= 2:
            sy.wait(stws[par])
        for h in range(NHCH):
            sy.dma(xs[:, h * TSTRIP:(h + 1) * TSTRIP],
                   src_x[h * 128:(h + 1) * 128, tsl], inc=[(lds[par], 16)])
            sy.dma(as_[:, h * TSTRIP:(h + 1) * TSTRIP],
                   acc[h * 128:(h + 1) * 128, tsl], inc=[(lds[par], 16)])
        v.wait(lds[par])
        for h in range(NHCH):
            sl = slice(h * TSTRIP, (h + 1) * TSTRIP)
            if acc_scale == 1.0:
                v.op("tensor_tensor", xs[:, sl], xs[:, sl], as_[:, sl], ALU.add)
            else:
                v.op("scalar_tensor_tensor", xs[:, sl], as_[:, sl], acc_scale,
                     xs[:, sl], ALU.mult, ALU.add)
        v.flush()
        v.post_inc(vx)
        sy.wait(vx)
        for h in range(NHCH):
            sl = slice(h * TSTRIP, (h + 1) * TSTRIP)
            sy.dma(out[h * 128:(h + 1) * 128, tsl], xs[:, sl], inc=[(stws[par], 16)])
    sy.wait(stws[0])
    sy.wait(stws[1])


def emit_l0f_router(P, io, cb, xlnf, wr_dram, ce_dram, scale=0.5):
    t, v, s, g, sy = (P.e[n] for n in ENGINES)
    wld = P.sem("r0_wld")
    lds = [P.sem("r0_ld0"), P.sem("r0_ld1")]
    stws = [P.sem("r0_stw0"), P.sem("r0_stw1")]
    tmm = P.sem("r0_tmm")
    vv = P.sem("r0_vv")
    ssx = P.sem("r0_ss")
    ttr = P.sem("r0_ttr")
    vtr = P.sem("r0_vtr")

    wrb = P.sbuf("r0_wr", [128, NHCH * 8], F32)
    xtb = P.sbuf("r0_xt", [128, 2 * NHCH * 128], F32)
    pb = P.sbuf("r0_pb", [128, 6 * 8], F32)
    rowb = P.sbuf("r0_row", [128, 8], F32)
    ceb = P.sbuf("r0_ce", [8, 2 * 128], F32)
    mvals = P.sbuf("r0_mv", [128, 8], F32)
    identb = cb["identf"]
    psL = cb["pool"][2]
    psT = cb["pool"][3]

    for h in range(NHCH):
        g.dma(wrb[:, h * 8:(h + 1) * 8], wr_dram[h * 128:(h + 1) * 128, :],
              inc=[(wld, 16)])
    v.op("memset", mvals[:], -3e38)
    v.flush()
    t.wait(wld)

    NT = TC // 128
    vv_marks = {}
    stw_marks = {}
    for it in range(NT):
        par = it % 2
        xs = xtb[:, par * NHCH * 128:(par + 1) * NHCH * 128]
        ces = ceb[:, par * 128:(par + 1) * 128]
        tsl = slice(it * 128, (it + 1) * 128)
        probs = pb[:, 0:8]
        mr = pb[:, 8:16]
        sel = pb[:, 16:24]
        m1 = rowb[:, 0:1]
        nmx = rowb[:, 1:2]
        z = rowb[:, 2:3]
        zi = rowb[:, 3:4]
        m2 = rowb[:, 4:5]
        den = rowb[:, 5:6]
        rden = rowb[:, 6:7]
        if it >= 2:
            sy.wait(vv, vv_marks[it - 2])
        for h in range(NHCH):
            sy.dma(xs[:, h * 128:(h + 1) * 128], xlnf[h * 128:(h + 1) * 128, tsl],
                   inc=[(lds[par], 16)])
        t.wait(lds[par])
        if it >= 1:
            t.wait(ssx)
        for h in range(NHCH):
            t.mm(psL[:, 0:8], xs[:, h * 128:(h + 1) * 128], wrb[:, h * 8:(h + 1) * 8],
                 start=(h == 0), stop=(h == NHCH - 1))
        t.mark_last(tmm)
        v.wait(tmm)
        v.op("reduce_max", m1, psL[:, 0:8], axis=AX.X)
        v.flush()
        v.op("tensor_scalar", nmx, m1, -1.0, None, ALU.mult, inc=[(vv, 1)])
        s.wait(vv)
        s.op("activation", probs, psL[:, 0:8], AF.Exp, bias=nmx, accum_out=z,
             inc=[(ssx, 1)])
        v.wait(ssx)
        v.op("reciprocal", zi, z)
        v.flush()
        v.op("tensor_scalar", probs, probs, zi, None, ALU.mult)
        v.flush()
        v.op("reduce_max", m1, probs, axis=AX.X)
        v.flush()
        v.op("tensor_copy", mvals[:, 0:1], m1)
        v.flush()
        v.op("match_replace", mr, mvals[:], probs, -1e30)
        v.flush()
        v.op("reduce_max", m2, mr, axis=AX.X)
        v.flush()
        v.op("tensor_tensor", den, m1, m2, ALU.add)
        v.op("tensor_scalar", sel, probs, m2, None, ALU.is_ge)
        v.flush()
        v.op("reciprocal", rden, den)
        v.op("tensor_tensor", sel, sel, probs, ALU.mult)
        v.flush()
        v.op("tensor_scalar", sel, sel, rden, scale, ALU.mult, ALU.mult)
        v.flush()
        v.post_inc(vv)
        t.wait(vv)
        t.wait(vtr)
        t.op("transpose", psT[0:8, 0:128], sel, identb[:], inc=[(ttr, 1)])
        v.wait(ttr)
        if it >= 2:
            v.wait(stws[par])
        v.op("tensor_copy", ces, psT[0:8, 0:128])
        v.flush()
        v.post_inc(vtr)
        vv_marks[it] = vv.mark()
        sy.wait(vtr)
        sy.dma(ce_dram[:, tsl], ces, inc=[(stws[par], 16)])
    sy.wait(stws[0])
    sy.wait(stws[1])


def emit_l0f_experts(P, io, cb, xln, ce_dram, wg_dram, wu_dram, wd_dram, acc_dram,
                     first_writes_acc=True):
    t, v, s, g, sy = (P.e[n] for n in ENGINES)
    wld = P.sem("f0_wld")
    xlds = [P.sem("f0_xld0"), P.sem("f0_xld1")]
    clds = [P.sem("f0_cld0"), P.sem("f0_cld1")]
    adws = [P.sem("f0_adw0"), P.sem("f0_adw1")]
    tgu = P.sem("f0_tgu")
    sgu = P.sem("f0_sgu")
    vhm = P.sem("f0_vhm")
    tdn = P.sem("f0_tdn")
    tcb = P.sem("f0_tcb")
    vcb = P.sem("f0_vcb")
    vdo = P.sem("f0_vdo")

    wgb = P.sbuf("f0_wg", [128, NHCH * I0], BF16)
    wub = P.sbuf("f0_wu", [128, NHCH * I0], BF16)
    wdb = P.sbuf("f0_wd", [128, NI0 * H], BF16)
    xb = P.sbuf("f0_xb", [128, 2 * NHCH * TSTRIP], BF16)
    hm = P.sbuf("f0_hm", [128, NI0 * TSTRIP], BF16)
    gs = P.sbuf("f0_gs", [128, 2 * TSTRIP], BF16)
    cer = P.sbuf("f0_cer", [1, 2 * TSTRIP], F32)
    cbb = P.sbuf("f0_cb", [128, TSTRIP], F32)
    dob = P.sbuf("f0_do", [128, 2 * TSTRIP], F32)
    psGU = [cb["pool"][i] for i in range(4)]
    psDN = [cb["pool"][4], cb["pool"][5]]
    psCB = cb["pool"][0]
    onesf = cb["ones1f"]

    tgu_marks = []
    for e in range(E0):
        if e > 0:
            g.wait(tgu)
            g.wait(tdn)
        for h in range(NHCH):
            g.dma(wgb[:, h * I0:(h + 1) * I0],
                  wg_dram[e * H + h * 128:e * H + (h + 1) * 128, :], inc=[(wld, 16)])
            g.dma(wub[:, h * I0:(h + 1) * I0],
                  wu_dram[e * H + h * 128:e * H + (h + 1) * 128, :], inc=[(wld, 16)])
        for i in range(NI0):
            g.dma(wdb[:, i * H:(i + 1) * H],
                  wd_dram[e * I0 + i * 128:e * I0 + (i + 1) * 128, :], inc=[(wld, 16)])
        t.wait(wld)
        for it in range(NSTRIP):
            par = it % 2
            xs = xb[:, par * NHCH * TSTRIP:(par + 1) * NHCH * TSTRIP]
            cers = cer[0:1, par * TSTRIP:(par + 1) * TSTRIP]
            dos = [dob[:, 0:TSTRIP], dob[:, TSTRIP:2 * TSTRIP]]
            tsl = slice(it * TSTRIP, (it + 1) * TSTRIP)
            le = e * NSTRIP + it
            if le >= 2:
                sy.wait(tgu, tgu_marks[le - 2])
            for h in range(NHCH):
                sy.dma(xs[:, h * TSTRIP:(h + 1) * TSTRIP],
                       xln[h * 128:(h + 1) * 128, tsl], inc=[(xlds[par], 16)])
            sy.dma(cers, ce_dram[e:e + 1, tsl], inc=[(clds[par], 16)])
            t.wait(xlds[par])
            vhm_base = vhm.mark()
            sgu_base = sgu.mark()
            for i in range(NI0):
                pg = psGU[2 * (i % 2)]
                pu = psGU[2 * (i % 2) + 1]
                if i >= 2:
                    t.wait(sgu, sgu_base + i - 1)
                    t.wait(vhm, vhm_base + i - 1)
                for h in range(NHCH):
                    t.mm(pg[:], wgb[:, h * I0 + i * 128:h * I0 + (i + 1) * 128],
                         xs[:, h * TSTRIP:(h + 1) * TSTRIP],
                         start=(h == 0), stop=(h == NHCH - 1))
                t.mark_last(tgu)
                for h in range(NHCH):
                    t.mm(pu[:], wub[:, h * I0 + i * 128:h * I0 + (i + 1) * 128],
                         xs[:, h * TSTRIP:(h + 1) * TSTRIP],
                         start=(h == 0), stop=(h == NHCH - 1))
                t.mark_last(tgu)
                s.wait(tgu, tgu.mark() - 1)
                s.op("activation", gs[:, (i % 2) * TSTRIP:(i % 2 + 1) * TSTRIP],
                     pg[:], AF.Silu, inc=[(sgu, 1)])
                v.wait(tgu)
                v.wait(sgu)
                v.op("tensor_tensor", hm[:, i * TSTRIP:(i + 1) * TSTRIP],
                     gs[:, (i % 2) * TSTRIP:(i % 2 + 1) * TSTRIP], pu[:], ALU.mult,
                     inc=[(vhm, 1)])
            tgu_marks.append(tgu.mark())
            t.wait(clds[par])
            t.wait(sgu)
            t.mm(psCB[:], onesf[0:1, :], cers, start=True, stop=True,
                 inc=[(tcb, 1)])
            v.wait(tcb)
            v.op("tensor_copy", cbb[:], psCB[:], inc=[(vcb, 1)])
            for h in range(NHCH):
                pd = psDN[h % 2]
                if h >= 2:
                    t.wait(vdo, vdo.mark() - 1)
                if h == 0:
                    t.wait(vhm)
                for i in range(NI0):
                    t.mm(pd[:], wdb[:, i * H + h * 128:i * H + (h + 1) * 128],
                         hm[:, i * TSTRIP:(i + 1) * TSTRIP],
                         start=(i == 0), stop=(i == NI0 - 1))
                t.mark_last(tdn)
                v.wait(tdn)
                v.wait(vcb)
                v.wait(adws[h % 2])
                v.op("tensor_tensor", dos[h % 2], pd[:], cbb[:], ALU.mult,
                     inc=[(vdo, 1)])
                g.wait(vdo)
                if e == 0 and first_writes_acc:
                    g.dma(acc_dram[h * 128:(h + 1) * 128, tsl], dos[h % 2],
                          inc=[(adws[h % 2], 16)])
                else:
                    g.dma(acc_dram[h * 128:(h + 1) * 128, tsl], dos[h % 2],
                          accum_op=ALU.add, inc=[(adws[h % 2], 16)])
    g.wait(adws[0])
    g.wait(adws[1])


def emit_l0attn(P, io, cb, xln, wqk_dram, wo_dram, bo_dram, mask_dram, acc_dram):
    t, v, s, g, sy = (P.e[n] for n in ENGINES)
    wl = P.sem("a0_wl")
    xls = [P.sem("a0_xl0"), P.sem("a0_xl1")]
    aws = [P.sem("a0_aw0"), P.sem("a0_aw1")]
    tpj = P.sem("a0_tpj")
    vpj = P.sem("a0_vpj")
    ttr = P.sem("a0_ttr")
    vtr = P.sem("a0_vtr")
    tsc = P.sem("a0_tsc")
    vsc = P.sem("a0_vsc")
    ssx = P.sem("a0_ssx")
    vnp = P.sem("a0_vnp")
    tpt = P.sem("a0_tpt")
    vpt = P.sem("a0_vpt")
    tpv = P.sem("a0_tpv")
    vav = P.sem("a0_vav")
    top = P.sem("a0_top")
    vop = P.sem("a0_vop")

    SB = S
    wqkb = P.sbuf("a0_wqk", [128, NHCH * 2 * H], BF16)
    wob = P.sbuf("a0_wo", [128, NHCH * H], BF16)
    bob = P.sbuf("a0_bo", [128, NHCH], F32)
    maskb = P.sbuf("a0_mask", [128, 2 * WIN], F32)
    xb = P.sbuf("a0_xb", [128, 2 * NHCH * TSTRIP], BF16)
    qf = P.sbuf("a0_qf", [128, NHCH * SB], BF16)
    kf = P.sbuf("a0_kf", [128, NHCH * (SB + WIN)], BF16)
    kt = P.sbuf("a0_kt", [128, (SB // 128 + 1) * H], BF16)
    af = P.sbuf("a0_af", [128, NHCH * TSTRIP], BF16)
    snb = P.sbuf("a0_sn", [128, 2 * WIN], F32)
    peb = P.sbuf("a0_pe", [128, 2 * WIN], F32)
    pnb = P.sbuf("a0_pn", [128, 2 * WIN], BF16)
    ptb = P.sbuf("a0_pt", [128, 2 * WIN], BF16)
    rowz = P.sbuf("a0_rz", [128, 4], F32)
    dob = P.sbuf("a0_do", [128, 2 * TSTRIP], F32)
    psP = [P.psum("a0_psp0", [128, TSTRIP]), P.psum("a0_psp1", [128, TSTRIP])]
    psS = P.psum("a0_pss", [128, 2 * WIN])
    psT1 = P.psum("a0_pst1", [128, 128], BF16)
    psT2 = P.psum("a0_pst2", [128, 128], BF16)
    psV = [P.psum("a0_psv0", [128, 128]), P.psum("a0_psv1", [128, 128])]

    NWIN = SB // WIN
    NTK = SB // 128
    tpj_marks = {}

    for h in range(NHCH):
        g.dma(wqkb[:, h * 2 * H:(h + 1) * 2 * H],
              wqk_dram[h * 128:(h + 1) * 128, :], inc=[(wl, 16)])
        g.dma(wob[:, h * H:(h + 1) * H],
              wo_dram[h * 128:(h + 1) * 128, :], inc=[(wl, 16)])
    g.dma(bob[:], bo_dram[:], inc=[(wl, 16)])
    g.dma(maskb[:], mask_dram[:], inc=[(wl, 16)])
    v.op("memset", kf[:, :], 0.0)
    v.op("memset", kt[:, 0:H], 0.0)
    v.flush()
    v.post_inc(vpj)
    t.wait(wl)

    for b in range(BPC):
        base = b * SB
        nstr = SB // TSTRIP
        for it in range(nstr):
            par = it % 2
            xs = xb[:, par * NHCH * TSTRIP:(par + 1) * NHCH * TSTRIP]
            tsl = slice(base + it * TSTRIP, base + (it + 1) * TSTRIP)
            lsl = slice(it * TSTRIP, (it + 1) * TSTRIP)
            gi = b * nstr + it
            if gi >= 2:
                sy.wait(tpj, tpj_marks[gi - 2])
            for h in range(NHCH):
                sy.dma(xs[:, h * TSTRIP:(h + 1) * TSTRIP],
                       xln[h * 128:(h + 1) * 128, tsl], inc=[(xls[par], 16)])
            t.wait(xls[par])
            for oc in range(NHCH):
                pp = psP[oc % 2]
                if gi > 0 or oc >= 2:
                    t.wait(vpj, vpj.mark() - 1)
                for h in range(NHCH):
                    t.mm(pp[:], wqkb[:, h * 2 * H + oc * 128:h * 2 * H + (oc + 1) * 128],
                         xs[:, h * TSTRIP:(h + 1) * TSTRIP],
                         start=(h == 0), stop=(h == NHCH - 1))
                t.mark_last(tpj)
                v.wait(tpj)
                v.op("tensor_copy", qf[:, oc * SB + it * TSTRIP:oc * SB + (it + 1) * TSTRIP],
                     pp[:], inc=[(vpj, 1)])
            for oc in range(NHCH):
                pp = psP[oc % 2]
                t.wait(vpj, vpj.mark() - 1)
                for h in range(NHCH):
                    t.mm(pp[:], wqkb[:, h * 2 * H + H + oc * 128:h * 2 * H + H + (oc + 1) * 128],
                         xs[:, h * TSTRIP:(h + 1) * TSTRIP],
                         start=(h == 0), stop=(h == NHCH - 1))
                t.mark_last(tpj)
                v.wait(tpj)
                v.op("tensor_copy",
                     kf[:, oc * (SB + WIN) + WIN + it * TSTRIP:oc * (SB + WIN) + WIN + (it + 1) * TSTRIP],
                     pp[:], inc=[(vpj, 1)])
            tpj_marks[gi] = tpj.mark()
        t.wait(vpj)
        for tt in range(NTK):
            for oc in range(NHCH):
                pst = psT1 if (tt * NHCH + oc) % 2 == 0 else psT2
                if tt * NHCH + oc >= 2:
                    t.wait(vtr, vtr.mark() - 1)
                t.op("transpose", pst[:],
                     kf[:, oc * (SB + WIN) + WIN + tt * 128:oc * (SB + WIN) + WIN + (tt + 1) * 128],
                     cb["identb"][:], inc=[(ttr, 1)])
                v.wait(ttr)
                v.op("tensor_copy", kt[:, (tt + 1) * H + oc * 128:(tt + 1) * H + (oc + 1) * 128],
                     pst[:], inc=[(vtr, 1)])
        for w in range(NWIN):
            qsl = slice(w * WIN, (w + 1) * WIN)
            t.wait(vsc)
            for h in range(NHCH):
                t.mm(psS[:], qf[:, h * SB + w * WIN:h * SB + (w + 1) * WIN],
                     kf[:, h * (SB + WIN) + w * WIN:h * (SB + WIN) + (w + 2) * WIN],
                     start=(h == 0), stop=(h == NHCH - 1))
            t.mark_last(tsc)
            v.wait(tsc)
            v.op("scalar_tensor_tensor", snb[:], psS[:], 1.0 / 32.0, maskb[:],
                 ALU.mult, ALU.add)
            v.flush()
            v.op("reduce_max", rowz[:, 0:1], snb[:], axis=AX.X)
            v.flush()
            v.op("tensor_scalar", rowz[:, 1:2], rowz[:, 0:1], -1.0, None, ALU.mult,
                 inc=[(vsc, 1)])
            s.wait(vsc)
            s.op("activation", peb[:], snb[:], AF.Exp, bias=rowz[:, 1:2],
                 accum_out=rowz[:, 2:3], inc=[(ssx, 1)])
            v.wait(ssx)
            v.op("reciprocal", rowz[:, 3:4], rowz[:, 2:3])
            v.flush()
            v.op("tensor_scalar", pnb[:], peb[:], rowz[:, 3:4], None, ALU.mult)
            v.flush()
            v.post_inc(vnp)
            t.wait(vnp)
            for kk in range(2):
                pst = psT1 if kk == 0 else psT2
                t.wait(vpt)
                t.op("transpose", pst[:], pnb[:, kk * 128:(kk + 1) * 128],
                     cb["identb"][:], inc=[(tpt, 1)])
                v.wait(tpt)
                v.op("tensor_copy", ptb[:, kk * 128:(kk + 1) * 128], pst[:],
                     inc=[(vpt, 1)])
            t.wait(vpt)
            for oc in range(NHCH):
                pv = psV[oc % 2]
                if oc >= 2:
                    t.wait(vav, vav.mark() - 1)
                for kk in range(2):
                    t.mm(pv[:], kt[:, (w + kk) * H + oc * 128:(w + kk) * H + (oc + 1) * 128],
                         ptb[:, kk * 128:(kk + 1) * 128],
                         start=(kk == 0), stop=(kk == 1))
                t.mark_last(tpv)
                v.wait(tpv)
                v.op("tensor_copy",
                     af[:, oc * TSTRIP + (w % 4) * WIN:oc * TSTRIP + (w % 4 + 1) * WIN],
                     pv[:], inc=[(vav, 1)])
            if w % 4 == 3:
                it = w // 4
                tsl = slice(base + it * TSTRIP, base + (it + 1) * TSTRIP)
                t.wait(vav)
                for oc in range(NHCH):
                    pp = psP[oc % 2]
                    if True:
                        t.wait(vop, vop.mark() - 1) if (b + it + oc) > 0 else None
                    for h in range(NHCH):
                        t.mm(pp[:], wob[:, h * H + oc * 128:h * H + (oc + 1) * 128],
                             af[:, h * TSTRIP:(h + 1) * TSTRIP],
                             start=(h == 0), stop=(h == NHCH - 1))
                    t.mark_last(top)
                    s.wait(top)
                    s.wait(aws[oc % 2])
                    s.op("activation", dob[:, (oc % 2) * TSTRIP:(oc % 2) * TSTRIP + TSTRIP],
                         pp[:], AF.Identity, bias=bob[:, oc:oc + 1], inc=[(vop, 1)])
                    sy.wait(vop)
                    sy.dma(acc_dram[oc * 128:(oc + 1) * 128, tsl],
                           dob[:, (oc % 2) * TSTRIP:(oc % 2) * TSTRIP + TSTRIP],
                           inc=[(aws[oc % 2], 16)])
    sy.wait(aws[0])
    sy.wait(aws[1])


def _softmax_top2_store(P, cb, psL, pb, rowb, mvals, ceb_slice, scale, vv, ssx,
                        ttr, vtr, psT, identf):
    t, v, s = P.e["tensor"], P.e["vector"], P.e["scalar"]
    probs = pb[:, 0:8]
    mr = pb[:, 8:16]
    sel = pb[:, 16:24]
    m1 = rowb[:, 0:1]
    nmx = rowb[:, 1:2]
    z = rowb[:, 2:3]
    zi = rowb[:, 3:4]
    m2 = rowb[:, 4:5]
    den = rowb[:, 5:6]
    rden = rowb[:, 6:7]
    v.op("reduce_max", m1, psL[:, 0:8], axis=AX.X)
    v.flush()
    v.op("tensor_scalar", nmx, m1, -1.0, None, ALU.mult, inc=[(vv, 1)])
    s.wait(vv)
    s.op("activation", probs, psL[:, 0:8], AF.Exp, bias=nmx, accum_out=z,
         inc=[(ssx, 1)])
    v.wait(ssx)
    v.op("reciprocal", zi, z)
    v.flush()
    v.op("tensor_scalar", probs, probs, zi, None, ALU.mult)
    v.flush()
    v.op("reduce_max", m1, probs, axis=AX.X)
    v.flush()
    v.op("tensor_copy", mvals[:, 0:1], m1)
    v.flush()
    v.op("match_replace", mr, mvals[:], probs, -1e30)
    v.flush()
    v.op("reduce_max", m2, mr, axis=AX.X)
    v.flush()
    v.op("tensor_tensor", den, m1, m2, ALU.add)
    v.op("tensor_scalar", sel, probs, m2, None, ALU.is_ge)
    v.flush()
    v.op("reciprocal", rden, den)
    v.op("tensor_tensor", sel, sel, probs, ALU.mult)
    v.flush()
    v.op("tensor_scalar", sel, sel, rden, scale, ALU.mult, ALU.mult)
    v.flush()
    v.post_inc(vv)
    t.wait(vv)
    t.wait(vtr)
    t.op("transpose", psT[0:8, 0:128], sel, identf[:], inc=[(ttr, 1)])
    v.wait(ttr)
    v.op("tensor_copy", ceb_slice, psT[0:8, 0:128])
    v.flush()
    v.post_inc(vtr)


def emit_l1f_router(P, io, cb, xlnf, rw1_dram, rb1_dram, lng_dram, lnb_dram,
                    rw2_dram, rb2_dram, ge_dram, scale=0.5):
    t, v, s, g, sy = (P.e[n] for n in ENGINES)
    lds = [P.sem("r1_ld0"), P.sem("r1_ld1")]
    stws = [P.sem("r1_stw0"), P.sem("r1_stw1")]
    wld = P.sem("r1_wld")
    thh = P.sem("r1_thh")
    vhn = P.sem("r1_vhn")
    tst = P.sem("r1_tst")
    vrow = P.sem("r1_vrow")
    vst = P.sem("r1_vst")
    sst = P.sem("r1_sst")
    tbc = P.sem("r1_tbc")
    tmm = P.sem("r1_tmm")
    vv = P.sem("r1_vv")
    ssx = P.sem("r1_ss")
    ttr = P.sem("r1_ttr")
    vtr = P.sem("r1_vtr")

    NRC = RH // 128
    w1b = P.sbuf("r1_w1", [128, NHCH * RH], F32)
    w2b = P.sbuf("r1_w2", [128, NRC * 8], F32)
    rb1b = P.sbuf("r1_rb1", [128, NRC], F32)
    lngb = P.sbuf("r1_lng", [128, NRC], F32)
    lnbb = P.sbuf("r1_lnb", [128, NRC], F32)
    rb2b = P.sbuf("r1_rb2", [1, 8], F32)
    xtb = P.sbuf("r1_xt", [128, 2 * NHCH * 128], F32)
    hb = P.sbuf("r1_hb", [128, NRC * 128], F32)
    hnb = P.sbuf("r1_hn", [128, NRC * 128], F32)
    sqb = P.sbuf("r1_sq", [128, 128], F32)
    strow = P.sbuf("r1_srow", [1, 4 * 128], F32)
    abrow = P.sbuf("r1_ab", [128, 2 * 128], F32)
    pb = P.sbuf("r1_pb", [128, 24], F32)
    rowb = P.sbuf("r1_rowb", [128, 8], F32)
    mvals = P.sbuf("r1_mv", [128, 8], F32)
    ceb = P.sbuf("r1_ce", [8, 2 * 128], F32)
    psH = cb["pool"][0]
    psBCA = cb["pool"][1]
    psBCB = cb["pool"][4]
    psL = cb["pool"][2]
    psT = cb["pool"][3]
    psum_r, pssq_r = cb["poolr"][0], cb["poolr"][1]
    onesf = cb["ones1f"]
    onescf = cb["onescf"]
    identf = cb["identf"]

    for h in range(NHCH):
        g.dma(w1b[:, h * RH:(h + 1) * RH], rw1_dram[h * 128:(h + 1) * 128, :],
              inc=[(wld, 16)])
    for c in range(NRC):
        g.dma(w2b[:, c * 8:(c + 1) * 8], rw2_dram[c * 128:(c + 1) * 128, :],
              inc=[(wld, 16)])
    g.dma(rb1b[:], rb1_dram[:], inc=[(wld, 16)])
    g.dma(lngb[:], lng_dram[:], inc=[(wld, 16)])
    g.dma(lnbb[:], lnb_dram[:], inc=[(wld, 16)])
    g.dma(rb2b[:], rb2_dram[:], inc=[(wld, 16)])
    v.op("memset", mvals[:], -3e38)
    v.flush()
    t.wait(wld)
    s.wait(wld)
    v.wait(wld)

    NT = TC // 128
    vv_marks = {}
    stw_marks = {}
    for it in range(NT):
        par = it % 2
        xs = xtb[:, par * NHCH * 128:(par + 1) * NHCH * 128]
        ces = ceb[:, par * 128:(par + 1) * 128]
        tsl = slice(it * 128, (it + 1) * 128)
        r_mu = strow[0:1, 0:128]
        r_v1 = strow[0:1, 128:256]
        r_v2 = strow[0:1, 256:384]
        rA = abrow[:, 0:128]
        rB = abrow[:, 128:256]
        if it >= 2:
            sy.wait(vv, vv_marks[it - 2])
        for h in range(NHCH):
            sy.dma(xs[:, h * 128:(h + 1) * 128], xlnf[h * 128:(h + 1) * 128, tsl],
                   inc=[(lds[par], 16)])
        t.wait(lds[par])
        for c in range(NRC):
            if it >= 1 or c >= 1:
                t.wait(vhn)
            for h in range(NHCH):
                t.mm(psH[:, 0:128], w1b[:, h * RH + c * 128:h * RH + (c + 1) * 128],
                     xs[:, h * 128:(h + 1) * 128],
                     start=(h == 0), stop=(h == NHCH - 1))
            t.mark_last(thh)
            v.wait(thh)
            v.op("tensor_scalar", hb[:, c * 128:(c + 1) * 128], psH[:, 0:128],
                 rb1b[:, c:c + 1], None, ALU.add, inc=[(vhn, 1)])
        if it >= 1:
            t.wait(vrow)
        t.wait(vhn)
        for c in range(NRC):
            t.mm(psum_r[0:1, 0:128], onescf[:, 0:1], hb[:, c * 128:(c + 1) * 128],
                 start=(c == 0), stop=(c == NRC - 1))
        sq_s = P.sem("r1_sqs")
        tsq = P.sem("r1_tsq")
        s.wait(vhn)
        tsq_base = tsq.mark()
        for c in range(NRC):
            if c >= 1:
                s.wait(tsq, tsq_base + c)
            s.op("activation", sqb[:], hb[:, c * 128:(c + 1) * 128], AF.Square,
                 inc=[(sq_s, 1)])
            t.wait(sq_s)
            t.mm(pssq_r[0:1, 0:128], onescf[:, 0:1], sqb[:],
                 start=(c == 0), stop=(c == NRC - 1), inc=[(tsq, 1)])
        v.wait(tsq)
        v.op("tensor_scalar", r_mu, psum_r[0:1, 0:128], 1.0 / RH, None, ALU.mult)
        v.op("tensor_scalar", r_v1, pssq_r[0:1, 0:128], 1.0 / RH, None, ALU.mult)
        v.flush()
        v.post_inc(vrow)
        v.op("tensor_tensor", r_v2, r_mu, r_mu, ALU.mult)
        v.flush()
        v.op("tensor_tensor", r_v1, r_v1, r_v2, ALU.subtract)
        v.flush()
        v.op("tensor_scalar", r_v1, r_v1, 1e-5, None, ALU.add, inc=[(vst, 1)])
        s.wait(vst)
        s.op("activation", r_v2, r_v1, AF.Sqrt, inc=[(sst, 1)])
        v.wait(sst)
        v.op("reciprocal", r_v1, r_v2)
        v.flush()
        v.op("tensor_tensor", r_v2, r_mu, r_v1, ALU.mult)
        v.flush()
        v.op("tensor_scalar", r_v2, r_v2, -1.0, None, ALU.mult)
        v.flush()
        v.post_inc(vst)
        t.wait(vst)
        if it >= 1:
            t.wait(vhn)
        t.mm(psBCA[:, 0:128], onesf[0:1, :], r_v1, start=True, stop=True)
        t.mm(psBCB[:, 0:128], onesf[0:1, :], r_v2, start=True, stop=True)
        t.mark_last(tbc)
        v.wait(tbc)
        for c in range(NRC):
            v.op("tensor_tensor", hnb[:, c * 128:(c + 1) * 128],
                 hb[:, c * 128:(c + 1) * 128], psBCA[:, 0:128], ALU.mult)
            v.flush()
            v.op("tensor_tensor", hnb[:, c * 128:(c + 1) * 128],
                 hnb[:, c * 128:(c + 1) * 128], psBCB[:, 0:128], ALU.add)
            v.flush()
            v.op("tensor_scalar", hnb[:, c * 128:(c + 1) * 128],
                 hnb[:, c * 128:(c + 1) * 128], lngb[:, c:c + 1], lnbb[:, c:c + 1],
                 ALU.mult, ALU.add)
            v.flush()
            v.op("tensor_relu", hnb[:, c * 128:(c + 1) * 128],
                 hnb[:, c * 128:(c + 1) * 128])
            v.flush()
        v.post_inc(vhn)
        t.wait(vhn)
        if it >= 1:
            t.wait(ssx)
        for c in range(NRC):
            t.mm(psL[:, 0:8], hnb[:, c * 128:(c + 1) * 128], w2b[:, c * 8:(c + 1) * 8],
                 start=(c == 0), stop=False)
        t.mm(psL[:, 0:8], onesf[0:1, :], rb2b[:], start=False, stop=True)
        t.mark_last(tmm)
        v.wait(tmm)
        if it >= 2:
            v.wait(stws[par])
        _softmax_top2_store(P, cb, psL, pb, rowb, mvals, ces, scale, vv, ssx,
                            ttr, vtr, psT, identf)
        vv_marks[it] = vv.mark()
        sy.wait(vtr)
        sy.dma(ge_dram[:, tsl], ces, inc=[(stws[par], 16)])
    sy.wait(stws[0])
    sy.wait(stws[1])


def emit_l1f_experts(P, io, cb, xln, ge_dram, wq_dram, wk_dram, wv_dram, wo_dram,
                     bq_dram, bk_dram, bv_dram, bo_dram, acc_dram,
                     first_writes_acc=True, nexp=None):
    t, v, s, g, sy = (P.e[n] for n in ENGINES)
    wls = [P.sem("f1_wl0"), P.sem("f1_wl1")]
    bl = P.sem("f1_bl")
    xl = P.sem("f1_xl")
    gls = [P.sem("f1_gl0"), P.sem("f1_gl1")]
    aws = [P.sem("f1_aw0"), P.sem("f1_aw1")]
    tpj = P.sem("f1_tpj")
    spj = P.sem("f1_spj")
    ttr = P.sem("f1_ttr")
    vtr = P.sem("f1_vtr")
    tsc = P.sem("f1_tsc")
    ssx = P.sem("f1_ssx")
    tpv = P.sem("f1_tpv")
    vuv = P.sem("f1_vuv")
    tzb = P.sem("f1_tzb")
    vzb = P.sem("f1_vzb")
    vaf = P.sem("f1_vaf")
    tgb = P.sem("f1_tgb")
    vgb = P.sem("f1_vgb")
    tpo = P.sem("f1_tpo")
    vpo = P.sem("f1_vpo")

    SB = S
    NTK = SB // 128
    NQB = SB // TSTRIP
    VW = 65

    wbuf2 = P.sbuf("f1_w", [128, 2 * NHCH * H], BF16)
    qf = P.sbuf("f1_qf", [128, NHCH * SB], BF16)
    kf = P.sbuf("f1_kf", [128, NHCH * SB], BF16)
    vt = P.sbuf("f1_vt", [128, NTK * NH1 * VW], BF16)
    scr = P.sbuf("f1_scr", [128, 2 * NTK * TSTRIP], BF16)
    af = P.sbuf("f1_af", [128, NHCH * TSTRIP], BF16)
    xb = P.sbuf("f1_xb", [128, NHCH * TSTRIP], BF16)
    bqb = P.sbuf("f1_bq", [128, 4 * NHCH], F32)
    gbb = P.sbuf("f1_gb", [128, TSTRIP], F32)
    zbb = P.sbuf("f1_zb", [128, 2 * TSTRIP], F32)
    dob = P.sbuf("f1_do", [128, 2 * TSTRIP], F32)
    grow = P.sbuf("f1_grow", [1, 2 * TSTRIP], F32)
    zrow = P.sbuf("f1_zrow", [1, 2 * TSTRIP], F32)
    zrowb = P.sbuf("f1_zrowb", [1, 2 * TSTRIP], BF16)
    sc8 = P.sbuf("f1_sc8", [128, 1], F32)
    nb8 = P.sbuf("f1_nb8", [128, 1], F32)
    sm1 = P.sbuf("f1_sm1", [128, 1], F32)
    dumr = P.sbuf("f1_dumr", [1, TSTRIP], BF16)
    pp = [P.psum("f1_pp0", [128, TSTRIP]), P.psum("f1_pp1", [128, TSTRIP])]
    psSbig = P.psum("f1_psbig", [128, 4 * TSTRIP])
    psT = [P.psum("f1_pt0", [128, TSTRIP]), P.psum("f1_pt1", [128, TSTRIP])]
    psTb = [pT[:].bitcast(BF16) for pT in psT]

    ini = P.sem("f1_ini")
    v.op("memset", sc8[:], 0.125)
    v.op("memset", nb8[:], -8.0)
    v.op("memset", sm1[:], -1.0)
    v.op("memset", dumr[:], 0.0)
    vt3 = vt[:].rearrange("p (t w) -> p t w", w=VW)
    v.op("memset", vt3[:, :, 64:65], 1.0)
    v.flush()
    v.post_inc(ini)
    s_.wait(ini) if False else P.e["scalar"].wait(ini)
    t.wait(ini)

    onesf = cb["ones1f"]
    ones1b = cb["ones1b"]
    identb = cb["identb"]
    NE = nexp if nexp is not None else E1
    PH = globals().get("L1F_PHASE", 5)

    wpar_ctr = [0]
    half_guard = {0: [], 1: []}

    def next_half():
        wp = wpar_ctr[0] % 2
        wpar_ctr[0] += 1
        return wp

    for e in range(NE):
        g.wait(vpo)
        g.dma(bqb[:, 0:NHCH], bq_dram[e * 128:(e + 1) * 128, :], inc=[(bl, 16)])
        g.dma(bqb[:, NHCH:2 * NHCH], bk_dram[e * 128:(e + 1) * 128, :], inc=[(bl, 16)])
        g.dma(bqb[:, 2 * NHCH:3 * NHCH], bv_dram[e * 128:(e + 1) * 128, :], inc=[(bl, 16)])
        g.dma(bqb[:, 3 * NHCH:4 * NHCH], bo_dram[e * 128:(e + 1) * 128, :], inc=[(bl, 16)])
        s.wait(bl)
        for b in range(BPC):
            base = b * SB
            for pj, wd in ((0, wq_dram), (1, wk_dram), (2, wv_dram)):
                wpar = next_half()
                wbuf = wbuf2[:, wpar * NHCH * H:(wpar + 1) * NHCH * H]
                for sem_, val_ in half_guard[wpar]:
                    g.wait(sem_, val_)
                for h in range(NHCH):
                    g.dma(wbuf[:, h * H:(h + 1) * H],
                          wd[e * H + h * 128:e * H + (h + 1) * 128, :],
                          inc=[(wls[wpar], 16)])
                t.wait(wls[wpar])
                for it in range(SB // TSTRIP):
                    tsl = slice(base + it * TSTRIP, base + (it + 1) * TSTRIP)
                    lo = it * TSTRIP
                    sy.wait(tpj)
                    for h in range(NHCH):
                        sy.dma(xb[:, h * TSTRIP:(h + 1) * TSTRIP],
                               xln[h * 128:(h + 1) * 128, tsl], inc=[(xl, 16)])
                    t.wait(xl)
                    for oc in range(NHCH):
                        p_ = pp[oc % 2]
                        if spj.mark() >= 1:
                            t.wait(spj, spj.mark() - 1)
                        for h in range(NHCH):
                            t.mm(p_[:], wbuf[:, h * H + oc * 128:h * H + (oc + 1) * 128],
                                 xb[:, h * TSTRIP:(h + 1) * TSTRIP],
                                 start=(h == 0), stop=(h == NHCH - 1))
                        t.mark_last(tpj)
                        s.wait(tpj)
                        if pj == 2:
                            odst = scr[:, oc * SB + lo:oc * SB + lo + TSTRIP]
                        elif pj == 0:
                            odst = qf[:, oc * SB + lo:oc * SB + lo + TSTRIP]
                        else:
                            odst = kf[:, oc * SB + lo:oc * SB + lo + TSTRIP]
                        s.op("activation", odst, p_[:], AF.Identity,
                             bias=bqb[:, pj * NHCH + oc:pj * NHCH + oc + 1],
                             inc=[(spj, 1)])
                half_guard[wpar] = [(tpj, tpj.mark())]
            if PH < 2:
                continue
            t.wait(spj)
            for tt in range(NTK):
                for c in range(NHCH):
                    j = tt * NHCH + c
                    pst = psTb[j % 2]
                    if j >= 2:
                        t.wait(vtr, vtr.mark() - 2)
                    t.op("transpose", pst[:, 0:128],
                         scr[:, c * SB + tt * 128:c * SB + (tt + 1) * 128],
                         identb[:], inc=[(ttr, 1)])
                    v.wait(ttr)
                    v.op("tensor_copy",
                         vt3[:, tt * NH1 + 2 * c, 0:64], pst[:, 0:64], inc=[(vtr, 1)])
                    v.op("tensor_copy",
                         vt3[:, tt * NH1 + 2 * c + 1, 0:64], pst[:, 64:128],
                         inc=[(vtr, 1)])
            if PH < 3:
                continue
            s.wait(ttr)
            t.wait(vtr)
            for qb in range(NQB):
                qpar = qb % 2
                grs = grow[0:1, qpar * TSTRIP:(qpar + 1) * TSTRIP]
                if tgb.mark() >= 2:
                    sy.wait(tgb, tgb.mark() - 1)
                sy.dma(grs, ge_dram[e:e + 1, base + qb * TSTRIP:
                                    base + (qb + 1) * TSTRIP], inc=[(gls[qpar], 16)])
                t.wait(gls[qpar])
                t.wait(ssx)
                t.wait(vzb)
                t.mm(psSbig[:, 0:TSTRIP], onesf[0:1, :], grs, start=True, stop=True,
                     inc=[(tgb, 1)])
                v.wait(tgb)
                v.wait(vpo)
                v.op("tensor_copy", gbb[:], psSbig[:, 0:TSTRIP], inc=[(vgb, 1)])
                def make_normalize(h0):
                    tpv_b = tpv.mark() - 2

                    def emit():
                        vz0 = P.sem("f1_vz0")
                        vz1 = P.sem("f1_vz1")
                        s.wait(tpv, tpv_b + 1)
                        s.op("activation", zrow[0:1, 0:TSTRIP], psT[0][64:65, :],
                             AF.Ln)
                        s.flush()
                        s.op("activation", zrowb[0:1, 0:TSTRIP],
                             zrow[0:1, 0:TSTRIP], AF.Exp, scale=sm1[0:1, 0:1],
                             inc=[(vz0, 1)])
                        s.wait(tpv, tpv_b + 2)
                        s.op("activation", zrow[0:1, TSTRIP:2 * TSTRIP],
                             psT[1][64:65, :], AF.Ln)
                        s.flush()
                        s.op("activation", zrowb[0:1, TSTRIP:2 * TSTRIP],
                             zrow[0:1, TSTRIP:2 * TSTRIP], AF.Exp,
                             scale=sm1[0:1, 0:1], inc=[(vz1, 1)])
                        t.wait(vz0)
                        t.wait(vzb)
                        t.mm(psT[0][64:128, :], ones1b[0:1, 0:64],
                             zrowb[0:1, 0:TSTRIP], start=True, stop=True,
                             skip_group_check=True)
                        t.wait(vz1)
                        t.mm(psT[1][64:128, :], ones1b[0:1, 0:64],
                             zrowb[0:1, TSTRIP:2 * TSTRIP], start=True, stop=True,
                             skip_group_check=True, inc=[(tzb, 1)])
                        v.wait(tzb)
                        for h2 in range(2):
                            h = h0 + h2
                            zs = zbb[0:64, h2 * TSTRIP:(h2 + 1) * TSTRIP]
                            v.op("tensor_copy", zs, psT[h2][64:128, :])
                            v.flush()
                            v.op("tensor_tensor",
                                 af[(h % 2) * 64:(h % 2) * 64 + 64,
                                    (h // 2) * TSTRIP:(h // 2 + 1) * TSTRIP],
                                 psT[h2][0:64, :], zs, ALU.mult,
                                 inc=[(vuv, 1)])
                        v.post_inc(vzb)
                        v.post_inc(vaf)
                    return emit

                pending = None
                for hp in range(NH1 // 2):
                    h0 = 2 * hp
                    c = hp
                    if hp == 0:
                        t.wait(vgb)
                    ssx_base = ssx.mark()
                    tpv_base = tpv.mark()
                    for kt in range(NTK):
                        bk0 = psSbig[:, (kt % 2) * TSTRIP:(kt % 2 + 1) * TSTRIP]
                        bk1 = psSbig[:, (2 + kt % 2) * TSTRIP:(3 + kt % 2) * TSTRIP]
                        if kt >= 2:
                            t.wait(ssx, ssx_base + 2 * (kt // 2))
                        t.mm(bk0, kf[0:64, c * SB + kt * 128:c * SB + (kt + 1) * 128],
                             qf[0:64, c * SB + qb * TSTRIP:c * SB + (qb + 1) * TSTRIP],
                             start=True, stop=True, tile_position=(0, 0),
                             skip_group_check=True,
                             inc=[(tsc, 1)] if kt % 2 == 1 else None)
                        t.mm(bk1, kf[64:128, c * SB + kt * 128:c * SB + (kt + 1) * 128],
                             qf[64:128, c * SB + qb * TSTRIP:c * SB + (qb + 1) * TSTRIP],
                             start=True, stop=True, tile_position=(64, 0),
                             skip_group_check=True,
                             inc=[(tsc, 1)] if kt % 2 == 1 else None)
                        if kt == 1 and pending is not None:
                            pending()
                            pending = None
                        if kt % 2 == 1:
                            s.wait(tsc, tsc.mark() - 1)
                            s.op("activation",
                                 scr[:, (kt - 1) * TSTRIP:(kt + 1) * TSTRIP],
                                 psSbig[:, 0:2 * TSTRIP], AF.Exp,
                                 bias=nb8[:, 0:1], scale=sc8[:, 0:1], inc=[(ssx, 1)])
                            s.wait(tsc)
                            s.op("activation",
                                 scr[:, (NTK + kt - 1) * TSTRIP:(NTK + kt + 1) * TSTRIP],
                                 psSbig[:, 2 * TSTRIP:4 * TSTRIP], AF.Exp,
                                 bias=nb8[:, 0:1], scale=sc8[:, 0:1], inc=[(ssx, 1)])
                            if PH >= 4 and kt >= 3:
                                if kt == 3:
                                    t.wait(vuv, vuv.mark())
                                jprev = kt // 2 - 1
                                for h2 in range(2):
                                    h = h0 + h2
                                    pts = scr[:, h2 * NTK * TSTRIP:(h2 + 1) * NTK * TSTRIP]
                                    for kk in (2 * jprev, 2 * jprev + 1):
                                        t.mm(psT[h2][0:VW, :],
                                             vt3[:, kk * NH1 + h, :],
                                             pts[:, kk * TSTRIP:(kk + 1) * TSTRIP],
                                             start=(kk == 0), stop=False,
                                             skip_group_check=True)
                    if PH >= 4:
                        t.wait(ssx)
                        for h2 in range(2):
                            h = h0 + h2
                            pts = scr[:, h2 * NTK * TSTRIP:(h2 + 1) * NTK * TSTRIP]
                            for kk in (NTK - 2, NTK - 1):
                                t.mm(psT[h2][0:VW, :], vt3[:, kk * NH1 + h, :],
                                     pts[:, kk * TSTRIP:(kk + 1) * TSTRIP],
                                     start=False, stop=(kk == NTK - 1),
                                     skip_group_check=True)
                            t.mark_last(tpv)
                        pending = make_normalize(h0)
                if pending is not None:
                    pending()
                    pending = None
                if PH < 5:
                    continue
                tsl = slice(base + qb * TSTRIP, base + (qb + 1) * TSTRIP)
                if qb == 0:
                    wopar = next_half()
                    wobuf = wbuf2[:, wopar * NHCH * H:(wopar + 1) * NHCH * H]
                    for sem_, val_ in half_guard[wopar]:
                        g.wait(sem_, val_)
                    for h in range(NHCH):
                        g.dma(wobuf[:, h * H:(h + 1) * H],
                              wo_dram[e * H + h * 128:e * H + (h + 1) * 128],
                              inc=[(wls[wopar], 16)])
                t.wait(vuv)
                t.wait(wls[wopar])
                for oc in range(NHCH):
                    p_ = pp[oc % 2]
                    if vpo.mark() >= 1:
                        t.wait(vpo, vpo.mark() - 1)
                    for h in range(NHCH):
                        t.mm(p_[:], wobuf[:, h * H + oc * 128:h * H + (oc + 1) * 128],
                             af[:, h * TSTRIP:(h + 1) * TSTRIP],
                             start=(h == 0), stop=(h == NHCH - 1))
                    t.mark_last(tpo)
                    v.wait(tpo)
                    v.wait(vgb)
                    v.wait(aws[oc % 2])
                    v.op("scalar_tensor_tensor",
                         dob[:, (oc % 2) * TSTRIP:(oc % 2) * TSTRIP + TSTRIP],
                         p_[:], bqb[:, 3 * NHCH + oc:3 * NHCH + oc + 1], gbb[:],
                         ALU.add, ALU.mult, inc=[(vpo, 1)])
                    g.wait(vpo)
                    if e == 0 and first_writes_acc:
                        g.dma(acc_dram[oc * 128:(oc + 1) * 128, tsl],
                              dob[:, (oc % 2) * TSTRIP:(oc % 2) * TSTRIP + TSTRIP],
                              inc=[(aws[oc % 2], 16)])
                    else:
                        g.dma(acc_dram[oc * 128:(oc + 1) * 128, tsl],
                              dob[:, (oc % 2) * TSTRIP:(oc % 2) * TSTRIP + TSTRIP],
                              accum_op=ALU.add, inc=[(aws[oc % 2], 16)])
            half_guard[wopar] = [(tpo, tpo.mark())]
    g.wait(aws[0])
    g.wait(aws[1])


def emit_l1attn(P, io, cb, xg, wqkv_dram, bqkv_dram, wo_dram, bo_dram, mask_dram,
                out_dram):
    t, v, s, g, sy = (P.e[n] for n in ENGINES)
    wl = P.sem("a1_wl")
    xls = [P.sem("a1_xl0"), P.sem("a1_xl1")]
    tpj = P.sem("a1_tpj")
    spj = P.sem("a1_spj")
    vsb = P.sem("a1_vsb")
    ttr = P.sem("a1_ttr")
    vtr = P.sem("a1_vtr")
    tsc = P.sem("a1_tsc")
    vsc = P.sem("a1_vsc")
    ssx = P.sem("a1_ssx")
    vnp = P.sem("a1_vnp")
    tpt = P.sem("a1_tpt")
    vpt = P.sem("a1_vpt")
    tpv = P.sem("a1_tpv")
    vaf = P.sem("a1_vaf")
    tpo = P.sem("a1_tpo")
    vpo = P.sem("a1_vpo")
    aws = [P.sem("a1_aw0"), P.sem("a1_aw1")]

    NB = B
    CS = 64
    NCHK = 256 // CS
    CT = NB * CS
    NGL = CS // 8

    wbuf = P.sbuf("a1_w", [128, NHCH * 3 * H], BF16)
    wob = P.sbuf("a1_wo", [128, NHCH * H], BF16)
    bqb = P.sbuf("a1_bq", [128, 4 * NHCH], F32)
    bqs = P.sbuf("a1_bqs", [128, NHCH], F32)
    maskb = P.sbuf("a1_mask", [128, 128], F32)
    sc8 = P.sbuf("a1_sc8", [128, 1], F32)
    xb = P.sbuf("a1_xb", [128, 2 * NHCH * CT], BF16)
    qf = P.sbuf("a1_qf", [128, NHCH * CT], BF16)
    kf = P.sbuf("a1_kf", [128, NHCH * CT], BF16)
    scr = P.sbuf("a1_scr", [128, NHCH * CT], BF16)
    vt = P.sbuf("a1_vt", [128, NGL * H], BF16)
    af = P.sbuf("a1_af", [128, NHCH * CT], BF16)
    snb = P.sbuf("a1_sn", [128, 128], F32)
    peb = P.sbuf("a1_pe", [128, 128], F32)
    pnb = P.sbuf("a1_pn", [128, 128], BF16)
    ptb = P.sbuf("a1_pt", [128, 128], BF16)
    rowz = P.sbuf("a1_rz", [128, 4], F32)
    dob = P.sbuf("a1_do", [128, 2 * TSTRIP], F32)
    pp = [P.psum("a1_pp0", [128, TSTRIP]), P.psum("a1_pp1", [128, TSTRIP])]
    psS = [P.psum("a1_ps0", [128, 128]), P.psum("a1_ps1", [128, 128])]
    psT = [P.psum("a1_pt0", [128, 128]), P.psum("a1_pt1", [128, 128])]
    psTb = [x[:].bitcast(BF16) for x in psT]

    ini = P.sem("a1_ini")
    v.op("memset", sc8[:], 0.125)
    v.flush()
    v.post_inc(ini)
    s.wait(ini)

    for h in range(NHCH):
        g.dma(wbuf[:, h * 3 * H:(h + 1) * 3 * H],
              wqkv_dram[h * 128:(h + 1) * 128, :], inc=[(wl, 16)])
        g.dma(wob[:, h * H:(h + 1) * H],
              wo_dram[h * 128:(h + 1) * 128, :], inc=[(wl, 16)])
    g.dma(bqb[:, 0:3 * NHCH], bqkv_dram[:], inc=[(wl, 16)])
    g.dma(bqb[:, 3 * NHCH:4 * NHCH], bo_dram[:], inc=[(wl, 16)])
    g.dma(maskb[:], mask_dram[:], inc=[(wl, 16)])
    v.wait(wl)
    v.op("tensor_scalar", bqs[:], bqb[:, 0:NHCH], 0.125, None, ALU.mult)
    v.flush()
    v.post_inc(vsb)
    t.wait(wl)
    s.wait(vsb)


    for cg in range(NCHK):
        cpar = cg % 2
        xs = xb[:, cpar * NHCH * CT:(cpar + 1) * NHCH * CT]
        if cg >= 2:
            sy.wait(tpj)
        for h in range(NHCH):
            src = xg[h * 128:(h + 1) * 128, :].rearrange("p (b s) -> p b s", b=NB)
            dst = xs[:, h * CT:(h + 1) * CT].rearrange("p (b s) -> p b s", b=NB)
            sy.dma(dst, src[:, :, cg * CS:(cg + 1) * CS], inc=[(xls[cpar], 16)])
        t.wait(xls[cpar])
        for pj in range(3):
            for oc in range(NHCH):
                for st_ in range(CT // TSTRIP):
                    p_ = pp[(oc * 2 + st_) % 2]
                    if spj.mark() >= 1:
                        t.wait(spj, spj.mark() - 1)
                    for h in range(NHCH):
                        t.mm(p_[:], wbuf[:, h * 3 * H + pj * H + oc * 128:
                                         h * 3 * H + pj * H + (oc + 1) * 128],
                             xs[:, h * CT + st_ * TSTRIP:h * CT + (st_ + 1) * TSTRIP],
                             start=(h == 0), stop=(h == NHCH - 1))
                    t.mark_last(tpj)
                    s.wait(tpj)
                    dst = (qf, kf, scr)[pj]
                    od4 = dst[:, oc * CT:(oc + 1) * CT].rearrange(
                        "p (gl b so) -> p b gl so", gl=NGL, b=NB)
                    odst = od4[:, 8 * st_:8 * (st_ + 1), :, :]
                    src4 = p_[:].rearrange("p (b gl so) -> p b gl so", b=8, gl=NGL)
                    if pj == 0:
                        s.op("activation", odst, src4, AF.Identity,
                             bias=bqs[:, oc:oc + 1], scale=sc8[:, 0:1],
                             inc=[(spj, 1)])
                    else:
                        s.op("activation", odst, src4, AF.Identity,
                             bias=bqb[:, pj * NHCH + oc:pj * NHCH + oc + 1],
                             inc=[(spj, 1)])
        t.wait(spj)
        for gl in range(NGL):
            for c in range(NHCH):
                j = gl * NHCH + c
                pst = psTb[j % 2]
                if j >= 2:
                    t.wait(vtr, vtr.mark() - 1)
                t.op("transpose", pst[:, 0:128],
                     scr[:, c * CT + gl * 128:c * CT + (gl + 1) * 128],
                     cb["identb"][:], inc=[(ttr, 1)])
                v.wait(ttr)
                v.op("tensor_copy", vt[:, gl * H + c * 128:gl * H + (c + 1) * 128],
                     pst[:, 0:128], inc=[(vtr, 1)])
        t.wait(vtr)
        for gl in range(NGL):
            for hp in range(NH1 // 2):
                c = hp
                qg = qf[:, c * CT + gl * 128:c * CT + (gl + 1) * 128]
                kg = kf[:, c * CT + gl * 128:c * CT + (gl + 1) * 128]
                t.wait(vsc)
                t.wait(vaf)
                t.mm(psS[0][:], qg[0:64], kg[0:64], start=True, stop=True,
                     tile_position=(0, 0))
                t.mm(psS[1][:], qg[64:128], kg[64:128], start=True, stop=True,
                     tile_position=(64, 0), inc=[(tsc, 1)])
                for h2 in range(2):
                    h = 2 * hp + h2
                    v.wait(tsc)
                    v.op("tensor_tensor", snb[:], psS[h2][:], maskb[:], ALU.add)
                    v.flush()
                    v.op("reduce_max", rowz[:, 0:1], snb[:], axis=AX.X)
                    v.flush()
                    v.op("tensor_scalar", rowz[:, 1:2], rowz[:, 0:1], -1.0, None,
                         ALU.mult, inc=[(vsc, 1)])
                    s.wait(vsc)
                    s.op("activation", peb[:], snb[:], AF.Exp, bias=rowz[:, 1:2],
                         accum_out=rowz[:, 2:3], inc=[(ssx, 1)])
                    v.wait(ssx)
                    v.op("reciprocal", rowz[:, 3:4], rowz[:, 2:3])
                    v.flush()
                    v.op("tensor_scalar", pnb[:], peb[:], rowz[:, 3:4], None,
                         ALU.mult)
                    v.flush()
                    v.post_inc(vnp)
                    t.wait(vnp)
                    t.wait(vpt)
                    t.op("transpose", psTb[h2][:, 0:128], pnb[:], cb["identb"][:],
                         inc=[(tpt, 1)])
                    v.wait(tpt)
                    v.op("tensor_copy", ptb[:], psTb[h2][:, 0:128], inc=[(vpt, 1)])
                    t.wait(vpt)
                    t.mm(psS[h2][0:64, :], vt[:, gl * H + h * 64:gl * H + (h + 1) * 64],
                         ptb[:], start=True, stop=True, inc=[(tpv, 1)])
                    v.wait(tpv)
                    v.op("tensor_copy",
                         af[(h % 2) * 64:(h % 2) * 64 + 64,
                            (h // 2) * CT + gl * 128:(h // 2) * CT + (gl + 1) * 128],
                         psS[h2][0:64, :], inc=[(vaf, 1)])
        t.wait(vaf)
        for st_ in range(CT // TSTRIP):
            for oc in range(NHCH):
                p_ = pp[(st_ * NHCH + oc) % 2]
                if vpo.mark() >= 1:
                    t.wait(vpo, vpo.mark() - 1)
                for h in range(NHCH):
                    t.mm(p_[:], wob[:, h * H + oc * 128:h * H + (oc + 1) * 128],
                         af[:, h * CT + st_ * TSTRIP:h * CT + (st_ + 1) * TSTRIP],
                         start=(h == 0), stop=(h == NHCH - 1))
                t.mark_last(tpo)
                v.wait(tpo)
                v.wait(aws[oc % 2])
                dsl = dob[:, (oc % 2) * TSTRIP:(oc % 2 + 1) * TSTRIP]
                d4 = dsl.rearrange("p (b gl so) -> p gl b so", b=NB, gl=4)
                s4 = p_[:].rearrange("p (gl b so) -> p gl b so", gl=4, b=NB)
                v.op("tensor_scalar", d4, s4,
                     bqb[:, 3 * NHCH + oc:3 * NHCH + oc + 1], None, ALU.add,
                     inc=[(vpo, 1)])
                sy.wait(vpo)
                od = out_dram[oc * 128:(oc + 1) * 128, :].rearrange(
                    "p (b s) -> p b s", b=NB)
                sy.dma(od[:, :, cg * CS + 32 * st_:cg * CS + 32 * (st_ + 1)],
                       dsl.rearrange("p (b s) -> p b s", b=NB),
                       inc=[(aws[oc % 2], 16)])
    sy.wait(aws[0])
    sy.wait(aws[1])



def _din(nc, io, name, shape, dt=F32, kind="ExternalInput"):
    io[name] = nc.dram_tensor(name, shape, dt, kind=kind).ap()


def _load_idents(P, cb, io, which=("identb", "identf")):
    g = P.e["gpsimd"]
    cl = P.sem("cl")
    n = 0
    if "identb" in which:
        cb["identb"] = P.sbuf("c_identb", [128, 128], BF16)
        g.dma(cb["identb"][:], io["ident"][:], inc=[(cl, 16)])
        n += 16
    if "identf" in which:
        cb["identf"] = P.sbuf("c_identf", [128, 128], F32)
        g.dma(cb["identf"][:], io["identf"][:], inc=[(cl, 16)])
        n += 16
    for nm in ENGINES:
        P.e[nm].wait(cl)


def prog_ln(with_acc):
    nc = bass.Bass("TRN2", target_bir_lowering=False)
    es = ExitStack()
    P = Prog(nc, es)
    io = {}
    _din(nc, io, "xin", [H, TC])
    if with_acc:
        _din(nc, io, "accin", [H, TC])
        _din(nc, io, "xout", [H, TC], kind="ExternalOutput")
    _din(nc, io, "lnout", [H, TC], BF16, kind="ExternalOutput")
    _din(nc, io, "lnoutf", [H, TC], kind="ExternalOutput")
    cb = {}
    with es:
        make_consts(P, cb)
        with P.scope():
            make_pool(P, cb)
            emit_ln(P, io, cb, io["xin"], io["lnout"],
                    acc=io.get("accin"), dst_x=io.get("xout"),
                    acc_scale=1.0, out_lnf=io["lnoutf"])
        P.barrier()
        P.replay()
    return nc


def prog_final():
    nc = bass.Bass("TRN2", target_bir_lowering=False)
    es = ExitStack()
    P = Prog(nc, es)
    io = {}
    _din(nc, io, "xin", [H, TC])
    _din(nc, io, "accin", [H, TC])
    _din(nc, io, "xout", [H, TC], kind="ExternalOutput")
    cb = {}
    with es:
        with P.scope():
            emit_final_add(P, io, cb, io["xin"], io["accin"], io["xout"])
        P.barrier()
        P.replay()
    return nc


def prog_l0f():
    nc = bass.Bass("TRN2", target_bir_lowering=False)
    es = ExitStack()
    P = Prog(nc, es)
    io = {}
    _din(nc, io, "lnout", [H, TC], BF16)
    _din(nc, io, "lnoutf", [H, TC])
    _din(nc, io, "l0f_wr", [H, 8])
    _din(nc, io, "l0f_wg", [E0 * H, I0])
    _din(nc, io, "l0f_wu", [E0 * H, I0])
    _din(nc, io, "l0f_wd", [E0 * I0, H])
    _din(nc, io, "identf", [128, 128])
    _din(nc, io, "acc", [H, TC], kind="ExternalOutput")
    ce = nc.dram_tensor("ce", [8, TC], F32, kind="Internal").ap()
    cb = {}
    with es:
        make_consts(P, cb)
        _load_idents(P, cb, io, which=("identf",))
        with P.scope():
            make_pool(P, cb)
            emit_l0f_router(P, io, cb, io["lnoutf"], io["l0f_wr"], ce, scale=0.5)
        P.barrier()
        with P.scope():
            make_pool(P, cb)
            emit_l0f_experts(P, io, cb, io["lnout"], ce, io["l0f_wg"],
                             io["l0f_wu"], io["l0f_wd"], io["acc"])
        P.barrier()
        P.replay()
    return nc


def prog_l0a():
    nc = bass.Bass("TRN2", target_bir_lowering=False)
    es = ExitStack()
    P = Prog(nc, es)
    io = {}
    _din(nc, io, "lnout", [H, TC], BF16)
    _din(nc, io, "l0a_wqk", [H, 2 * H])
    _din(nc, io, "l0a_wo", [H, H])
    _din(nc, io, "l0a_bo", [128, NHCH])
    _din(nc, io, "l0a_mask", [128, 2 * WIN])
    _din(nc, io, "ident", [128, 128], BF16)
    _din(nc, io, "acc", [H, TC], kind="ExternalOutput")
    cb = {}
    with es:
        make_consts(P, cb)
        _load_idents(P, cb, io, which=("identb",))
        with P.scope():
            emit_l0attn(P, io, cb, io["lnout"], io["l0a_wqk"], io["l0a_wo"],
                        io["l0a_bo"], io["l0a_mask"], io["acc"])
        P.barrier()
        P.replay()
    return nc


def prog_l1f():
    nc = bass.Bass("TRN2", target_bir_lowering=False)
    es = ExitStack()
    P = Prog(nc, es)
    io = {}
    _din(nc, io, "lnout", [H, TC], BF16)
    _din(nc, io, "lnoutf", [H, TC])
    _din(nc, io, "l1f_rW1", [H, RH])
    _din(nc, io, "l1f_rb1", [128, RH // 128])
    _din(nc, io, "l1f_lng", [128, RH // 128])
    _din(nc, io, "l1f_lnb", [128, RH // 128])
    _din(nc, io, "l1f_rW2", [RH, 8])
    _din(nc, io, "l1f_rb2", [1, 8])
    for nm in "qkvo":
        _din(nc, io, f"l1f_w{nm}", [E1 * H, H])
        _din(nc, io, f"l1f_b{nm}", [E1 * 128, NHCH])
    _din(nc, io, "identf", [128, 128])
    _din(nc, io, "ident", [128, 128], BF16)
    _din(nc, io, "acc", [H, TC], kind="ExternalOutput")
    ge = nc.dram_tensor("ge", [8, TC], F32, kind="Internal").ap()
    cb = {}
    with es:
        make_consts(P, cb)
        _load_idents(P, cb, io)
        with P.scope():
            make_pool(P, cb)
            emit_l1f_router(P, io, cb, io["lnoutf"], io["l1f_rW1"], io["l1f_rb1"],
                            io["l1f_lng"], io["l1f_lnb"], io["l1f_rW2"],
                            io["l1f_rb2"], ge, scale=0.5)
        P.barrier()
        with P.scope():
            emit_l1f_experts(P, io, cb, io["lnout"], ge,
                             io["l1f_wq"], io["l1f_wk"], io["l1f_wv"], io["l1f_wo"],
                             io["l1f_bq"], io["l1f_bk"], io["l1f_bv"], io["l1f_bo"],
                             io["acc"])
        P.barrier()
        P.replay()
    return nc


def prog_l1a():
    nc = bass.Bass("TRN2", target_bir_lowering=False)
    es = ExitStack()
    P = Prog(nc, es)
    io = {}
    _din(nc, io, "xg", [H, TC], BF16)
    _din(nc, io, "l1a_wqkv", [H, 3 * H])
    _din(nc, io, "l1a_bqkv", [128, 3 * NHCH])
    _din(nc, io, "l1a_wo", [H, H])
    _din(nc, io, "l1a_bo", [128, NHCH])
    _din(nc, io, "l1a_mask", [128, 128])
    _din(nc, io, "ident", [128, 128], BF16)
    _din(nc, io, "out_ss", [H, TC], kind="ExternalOutput")
    cb = {}
    with es:
        make_consts(P, cb)
        _load_idents(P, cb, io, which=("identb",))
        with P.scope():
            emit_l1attn(P, io, cb, io["xg"], io["l1a_wqkv"], io["l1a_bqkv"],
                        io["l1a_wo"], io["l1a_bo"], io["l1a_mask"], io["out_ss"])
        P.barrier()
        P.replay()
    return nc


_PROG_CACHE = {}


def _get_prog(name):
    if name not in _PROG_CACHE:
        _PROG_CACHE[name] = {
            "ln0": lambda: prog_ln(False),
            "ln1": lambda: prog_ln(True),
            "final": prog_final,
            "l0f": prog_l0f,
            "l0a": prog_l0a,
            "l1f": prog_l1f,
            "l1a": prog_l1a,
        }[name]()
    return _PROG_CACHE[name]


def _run(name, in_maps):
    nc = _get_prog(name)
    res = run_bass_kernel_spmd(nc, in_maps, core_ids=list(range(NCORES)))
    return res.results


def kernel_unfused(**inputs):
    w = prep_weights(inputs)
    xcs = prep_x_per_core(inputs["L0_attn_x"])
    ident = w["ident"]
    identf = w["identf"]

    wg2 = w["l0f_wg"].reshape(E0 * H, I0)
    wu2 = w["l0f_wu"].reshape(E0 * H, I0)
    wd2 = w["l0f_wd"].reshape(E0 * I0, H)
    l1w = {nm: w[f"l1f_w{nm}"].reshape(E1 * H, H) for nm in "qkvo"}
    l1b = {nm: w[f"l1f_b{nm}"].reshape(E1 * 128, NHCH) for nm in "qkvo"}

    def ln_first(xs):
        r = _run("ln0", [{"xin": x} for x in xs])
        return [c["lnout"] for c in r], [c["lnoutf"] for c in r]

    def ln_acc(xs, accs):
        r = _run("ln1", [{"xin": x, "accin": a} for x, a in zip(xs, accs)])
        return ([c["xout"] for c in r], [c["lnout"] for c in r],
                [c["lnoutf"] for c in r])

    def l0f(lns, lnfs):
        r = _run("l0f", [{"lnout": ln, "lnoutf": lnf, "l0f_wr": w["l0f_wr"],
                          "l0f_wg": wg2, "l0f_wu": wu2, "l0f_wd": wd2,
                          "identf": identf} for ln, lnf in zip(lns, lnfs)])
        return [c["acc"] for c in r]

    def l0a(lns):
        r = _run("l0a", [{"lnout": ln, "l0a_wqk": w["l0a_wqk"],
                          "l0a_wo": w["l0a_wo"], "l0a_bo": w["l0a_bo"],
                          "l0a_mask": w["l0a_mask"], "ident": ident}
                         for ln in lns])
        return [c["acc"] for c in r]

    def l1f(lns, lnfs):
        r = _run("l1f", [{"lnout": ln, "lnoutf": lnf,
                          "l1f_rW1": w["l1f_rW1"], "l1f_rb1": w["l1f_rb1"],
                          "l1f_lng": w["l1f_lng"], "l1f_lnb": w["l1f_lnb"],
                          "l1f_rW2": w["l1f_rW2"], "l1f_rb2": w["l1f_rb2"],
                          "l1f_wq": l1w["q"], "l1f_wk": l1w["k"],
                          "l1f_wv": l1w["v"], "l1f_wo": l1w["o"],
                          "l1f_bq": l1b["q"], "l1f_bk": l1b["k"],
                          "l1f_bv": l1b["v"], "l1f_bo": l1b["o"],
                          "identf": identf, "ident": ident}
                         for ln, lnf in zip(lns, lnfs)])
        return [c["acc"] for c in r]

    def l1a(lns):
        xgs = []
        for c in range(NCORES):
            parts = []
            for b in range(B):
                r_, bl = b // 2, b % 2
                parts.append(lns[r_][:, bl * S + 256 * c:bl * S + 256 * c + 256])
            xgs.append(np.ascontiguousarray(np.concatenate(parts, axis=1)))
        r = _run("l1a", [{"xg": xg, "l1a_wqkv": w["l1a_wqkv"],
                          "l1a_bqkv": w["l1a_bqkv"], "l1a_wo": w["l1a_wo"],
                          "l1a_bo": w["l1a_bo"], "l1a_mask": w["l1a_mask"],
                          "ident": ident} for xg in xgs])
        outs = [c["out_ss"] for c in r]
        accs = []
        for c in range(NCORES):
            acc = np.empty((H, TC), np.float32)
            for bl in range(BPC):
                for rp in range(NCORES):
                    acc[:, bl * S + 256 * rp:bl * S + 256 * (rp + 1)] = \
                        outs[rp][:, (2 * c + bl) * 256:(2 * c + bl + 1) * 256]
            accs.append(acc)
        return accs

    X = xcs
    lns, lnfs = ln_first(X)
    acc = l0f(lns, lnfs)
    X, lns, lnfs = ln_acc(X, acc)
    acc = l0a(lns)
    X, lns, lnfs = ln_acc(X, acc)
    acc = l0f(lns, lnfs)
    X, lns, lnfs = ln_acc(X, acc)
    acc = l1f(lns, lnfs)
    X, lns, lnfs = ln_acc(X, acc)
    acc = l1a(lns)
    X, lns, lnfs = ln_acc(X, acc)
    acc = l1f(lns, lnfs)
    r = _run("final", [{"xin": x, "accin": a} for x, a in zip(X, acc)])
    Xf = [c["xout"] for c in r]

    out = np.empty((B, S, H), np.float32)
    for c in range(NCORES):
        out[c * BPC:(c + 1) * BPC] = Xf[c].reshape(H, BPC, S).transpose(1, 2, 0)
    return out


def emit_l1a_exchange_in(P, io, cb, lnout, cc1_out, tmp1, xg, regs):
    g = P.e["gpsimd"]
    ccs = P.sem("x1_cc")
    gd = P.sem("x1_gd")
    gx = P.sem("x1_gx")
    rc, rof, r2 = regs
    g.op("collective_compute", "AllGather", ALU.bypass,
         replica_groups=[list(range(NCORES))],
         ins=[lnout[:]], outs=[cc1_out[:]], inc=[(ccs, 1)])
    g.wait(ccs)
    g.op("reg_mul", rof, rc, 256)
    t_cc1 = cc1_out.tensor
    for bl in range(BPC):
        g.op("reg_add", r2, rof, bl * S)
        src = bass.AP(t_cc1, r2, [[TC, NCORES * H], [1, 256]])
        g.dma(tmp1[:, bl * 256:(bl + 1) * 256], src, inc=[(gd, 16)])
    g.wait(gd)
    for r in range(NCORES):
        for bl in range(BPC):
            g.dma(xg[:, (2 * r + bl) * 256:(2 * r + bl + 1) * 256],
                  tmp1[r * H:(r + 1) * H, bl * 256:(bl + 1) * 256],
                  inc=[(gx, 16)])
    g.wait(gx)


def emit_l1a_exchange_out(P, io, cb, out_ss, cc2_out, tmp2, acc, regs):
    g = P.e["gpsimd"]
    ccs = P.sem("x2_cc")
    gd = P.sem("x2_gd")
    gs2 = P.sem("x2_gs")
    rc, rof, r2 = regs
    g.op("collective_compute", "AllGather", ALU.bypass,
         replica_groups=[list(range(NCORES))],
         ins=[out_ss[:]], outs=[cc2_out[:]], inc=[(ccs, 1)])
    g.wait(ccs)
    g.op("reg_mul", rof, rc, 512)
    t_cc2 = cc2_out.tensor
    for bl in range(BPC):
        g.op("reg_add", r2, rof, bl * 256)
        src = bass.AP(t_cc2, r2, [[TC, NCORES * H], [1, 256]])
        g.dma(tmp2[:, bl * 256:(bl + 1) * 256], src, inc=[(gd, 16)])
    g.wait(gd)
    for rp in range(NCORES):
        for bl in range(BPC):
            g.dma(acc[:, bl * S + 256 * rp:bl * S + 256 * (rp + 1)],
                  tmp2[rp * H:(rp + 1) * H, bl * 256:(bl + 1) * 256],
                  inc=[(gs2, 16)])
    g.wait(gs2)


def build_fused(dbg=False):
    from concourse.replica_groups import maybe_share_collective_output_space
    nc = bass.Bass("TRN2", target_bir_lowering=False)
    es = ExitStack()
    P = Prog(nc, es)
    io = {}
    _din(nc, io, "xin", [H, TC])
    _din(nc, io, "coreid", [1, 1], mybir.dt.int32)
    _din(nc, io, "ident", [128, 128], BF16)
    _din(nc, io, "identf", [128, 128])
    _din(nc, io, "l0f_wr", [H, 8])
    _din(nc, io, "l0f_wg", [E0 * H, I0])
    _din(nc, io, "l0f_wu", [E0 * H, I0])
    _din(nc, io, "l0f_wd", [E0 * I0, H])
    _din(nc, io, "l0a_wqk", [H, 2 * H])
    _din(nc, io, "l0a_wo", [H, H])
    _din(nc, io, "l0a_bo", [128, NHCH])
    _din(nc, io, "l0a_mask", [128, 2 * WIN])
    _din(nc, io, "l1f_rW1", [H, RH])
    _din(nc, io, "l1f_rb1", [128, RH // 128])
    _din(nc, io, "l1f_lng", [128, RH // 128])
    _din(nc, io, "l1f_lnb", [128, RH // 128])
    _din(nc, io, "l1f_rW2", [RH, 8])
    _din(nc, io, "l1f_rb2", [1, 8])
    for nm in "qkvo":
        _din(nc, io, f"l1f_w{nm}", [E1 * H, H])
        _din(nc, io, f"l1f_b{nm}", [E1 * 128, NHCH])
    _din(nc, io, "l1a_wqkv", [H, 3 * H])
    _din(nc, io, "l1a_bqkv", [128, 3 * NHCH])
    _din(nc, io, "l1a_wo", [H, H])
    _din(nc, io, "l1a_bo", [128, NHCH])
    _din(nc, io, "l1a_mask", [128, 128])
    _din(nc, io, "xout", [H, TC], kind="ExternalOutput")

    X = nc.dram_tensor("X", [H, TC], F32, kind="Internal").ap()
    ACC = nc.dram_tensor("ACC", [H, TC], F32, kind="Internal").ap()
    LNO = nc.dram_tensor("LNO", [H, TC], BF16, kind="Internal").ap()
    LNF = nc.dram_tensor("LNF", [H, TC], F32, kind="Internal").ap()
    CE = nc.dram_tensor("CEg", [8, TC], F32, kind="Internal").ap()
    GE = nc.dram_tensor("GEg", [8, TC], F32, kind="Internal").ap()
    XG = nc.dram_tensor("XGg", [H, TC], BF16, kind="Internal").ap()
    OSS = nc.dram_tensor("OSS", [H, TC], F32, kind="Internal").ap()
    adsp = maybe_share_collective_output_space("AllGather", [list(range(NCORES))])
    CC1 = nc.dram_tensor("CC1", [NCORES * H, TC], BF16, kind="Internal",
                         addr_space=adsp).ap()
    CC2 = nc.dram_tensor("CC2", [NCORES * H, TC], F32, kind="Internal",
                         addr_space=adsp).ap()
    TMP1 = nc.dram_tensor("TMP1", [NCORES * H, 512], BF16, kind="Internal").ap()
    TMP2 = nc.dram_tensor("TMP2", [NCORES * H, 512], F32, kind="Internal").ap()

    cb = {}
    dbgn = [0]
    with es:
        def snap(src):
            if not dbg:
                return
            nm = f"dbg{dbgn[0]}"
            dbgn[0] += 1
            _din(nc, io, nm, [H, TC], kind="ExternalOutput")
            with P.scope():
                emit_copy_x(P, io, {}, src, io[nm])
            P.barrier()

        rc = es.enter_context(nc.gpsimd.register("rc"))
        rof = es.enter_context(nc.gpsimd.register("rof"))
        r2 = es.enter_context(nc.gpsimd.register("r2"))
        regs = (rc, rof, r2)
        make_consts(P, cb)
        _load_idents(P, cb, io)
        cidb = P.sbuf("c_cid", [1, 1], mybir.dt.int32)
        g = P.e["gpsimd"]
        cl2 = P.sem("cl2")
        g.dma(cidb[:], io["coreid"][:], inc=[(cl2, 16)])
        g.wait(cl2)
        g.op("reg_load", rc, cidb[0:1, 0:1])

        def ln(src, acc, dst, lnf=True):
            with P.scope():
                make_pool(P, cb)
                emit_ln(P, io, cb, src, LNO, acc=acc, dst_x=dst, acc_scale=1.0,
                        out_lnf=LNF if lnf else None)
            P.barrier()

        def l0f():
            with P.scope():
                make_pool(P, cb)
                emit_l0f_router(P, io, cb, LNF, io["l0f_wr"], CE, scale=0.5)
            P.barrier()
            with P.scope():
                make_pool(P, cb)
                emit_l0f_experts(P, io, cb, LNO, CE, io["l0f_wg"], io["l0f_wu"],
                                 io["l0f_wd"], ACC)
            P.barrier()

        def l1f():
            with P.scope():
                make_pool(P, cb)
                emit_l1f_router(P, io, cb, LNF, io["l1f_rW1"], io["l1f_rb1"],
                                io["l1f_lng"], io["l1f_lnb"], io["l1f_rW2"],
                                io["l1f_rb2"], GE, scale=0.5)
            P.barrier()
            with P.scope():
                emit_l1f_experts(P, io, cb, LNO, GE,
                                 io["l1f_wq"], io["l1f_wk"], io["l1f_wv"],
                                 io["l1f_wo"], io["l1f_bq"], io["l1f_bk"],
                                 io["l1f_bv"], io["l1f_bo"], ACC)
            P.barrier()

        ln(io["xin"], None, None)
        l0f()
        snap(ACC)
        ln(io["xin"], ACC, X)
        snap(X)
        with P.scope():
            emit_l0attn(P, io, cb, LNO, io["l0a_wqk"], io["l0a_wo"],
                        io["l0a_bo"], io["l0a_mask"], ACC)
        P.barrier()
        snap(ACC)
        ln(X, ACC, X)
        snap(X)
        l0f()
        snap(ACC)
        ln(X, ACC, X)
        snap(X)
        l1f()
        snap(ACC)
        ln(X, ACC, X)
        snap(X)
        ln_dummy = None
        with P.scope():
            emit_l1a_exchange_in(P, io, cb, LNO, CC1, TMP1, XG, regs)
        P.barrier()
        with P.scope():
            emit_l1attn(P, io, cb, XG, io["l1a_wqkv"], io["l1a_bqkv"],
                        io["l1a_wo"], io["l1a_bo"], io["l1a_mask"], OSS)
        P.barrier()
        with P.scope():
            emit_l1a_exchange_out(P, io, cb, OSS, CC2, TMP2, ACC, regs)
        P.barrier()
        snap(ACC)
        ln(X, ACC, X)
        snap(X)
        l1f()
        snap(ACC)
        with P.scope():
            emit_final_add(P, io, cb, X, ACC, io["xout"])
        P.barrier()
        P.replay()
    return nc


def kernel_fused(**inputs):
    w = prep_weights(inputs)
    xcs = prep_x_per_core(inputs["L0_attn_x"])
    wg2 = w["l0f_wg"].reshape(E0 * H, I0)
    wu2 = w["l0f_wu"].reshape(E0 * H, I0)
    wd2 = w["l0f_wd"].reshape(E0 * I0, H)
    base = {"ident": w["ident"], "identf": w["identf"],
            "l0f_wr": w["l0f_wr"], "l0f_wg": wg2, "l0f_wu": wu2, "l0f_wd": wd2,
            "l0a_wqk": w["l0a_wqk"], "l0a_wo": w["l0a_wo"], "l0a_bo": w["l0a_bo"],
            "l0a_mask": w["l0a_mask"],
            "l1f_rW1": w["l1f_rW1"], "l1f_rb1": w["l1f_rb1"],
            "l1f_lng": w["l1f_lng"], "l1f_lnb": w["l1f_lnb"],
            "l1f_rW2": w["l1f_rW2"], "l1f_rb2": w["l1f_rb2"],
            "l1a_wqkv": w["l1a_wqkv"], "l1a_bqkv": w["l1a_bqkv"],
            "l1a_wo": w["l1a_wo"], "l1a_bo": w["l1a_bo"],
            "l1a_mask": w["l1a_mask"]}
    for nm in "qkvo":
        base[f"l1f_w{nm}"] = w[f"l1f_w{nm}"].reshape(E1 * H, H)
        base[f"l1f_b{nm}"] = w[f"l1f_b{nm}"].reshape(E1 * 128, NHCH)
    if "fused" not in _PROG_CACHE:
        _PROG_CACHE["fused"] = build_fused()
    nc = _PROG_CACHE["fused"]
    in_maps = [{**base, "xin": xcs[c],
                "coreid": np.array([[c]], np.int32)} for c in range(NCORES)]
    res = run_bass_kernel_spmd(nc, in_maps, core_ids=list(range(NCORES)))
    out = np.empty((B, S, H), np.float32)
    for c in range(NCORES):
        xf = res.results[c]["xout"]
        out[c * BPC:(c + 1) * BPC] = xf.reshape(H, BPC, S).transpose(1, 2, 0)
    return out


kernel = kernel_fused
